# version 05
# baseline (speedup 1.0000x reference)
import contextlib
import math
import numpy as np
import ml_dtypes
import concourse.bass as bass
import concourse.mybir as mybir
from concourse.bass_utils import run_bass_kernel_spmd

F32 = mybir.dt.float32
BF16 = mybir.dt.bfloat16
AF = mybir.ActivationFunctionType
ALU = mybir.AluOpType
AX = mybir.AxisListType

T = 4096
D = 1024
NCORE = 8
NTILE = T // 128

ENGINES = ("sync", "scalar", "vector", "tensor", "gpsimd")


class Op:
    __slots__ = ("incs",)

    def __init__(self):
        self.incs = []

    def then_inc(self, sem, n):
        self.incs.append((sem, n))
        return self


class Rec:
    def __init__(self):
        self.ops = []

    def __getattr__(self, m):
        def f(*a, **k):
            op = Op()
            self.ops.append((m, a, k, op))
            return op

        return f

    def replay(self, eng):
        for m, a, k, op in self.ops:
            ins = getattr(eng, m)(*a, **k)
            for sem, n in op.incs:
                ins.then_inc(sem, n)


class EM:

    def __init__(self, nc, es):
        self.nc = nc
        self.es = es
        self.rec = {e: Rec() for e in ENGINES}
        self.semh = {}
        self.cnt = {}
        self.qsem = {"sync": "dS", "scalar": "dC", "gpsimd": "dG"}
        self.dma_sems = set()
        for s in ("dS", "dC", "dG", "pe", "ve", "sc", "gp", "sp"):
            self._mk(s)
        for s in ("dS", "dC", "dG"):
            self.dma_sems.add(s)
        self.own = {"sync": "sp", "scalar": "sc", "vector": "ve",
                    "tensor": "pe", "gpsimd": "gp"}
        self.pools = []

    def _mk(self, name):
        self.semh[name] = self.es.enter_context(self.nc.semaphore(name))
        self.cnt[name] = 0

    def on(self, engine):
        return self.rec[engine]

    def salloc(self, *a, **k):
        return self.cur_es.enter_context(self.nc.sbuf_tensor(*a, **k))

    def inc(self, op, sem, k=1):
        op.then_inc(self.semh[sem], k)
        self.cnt[sem] += k
        return self.cnt[sem]

    def dma(self, queue, out, in_, sem=None, **kw):
        if sem is None:
            sem = self.qsem[queue]
        if sem not in self.semh:
            self._mk(sem)
        self.dma_sems.add(sem)
        op = self.rec[queue].dma_start(out=out, in_=in_, **kw)
        return self.inc(op, sem, 16)

    def wait(self, engine, sem, count):
        if count > 0:
            self.rec[engine].wait_ge(self.semh[sem], count)

    def barrier(self):
        for s in sorted(self.dma_sems):
            self.wait("sync", s, self.cnt[s])
        lvl = {}
        for e in ENGINES:
            if e == "sync":
                op = self.rec[e].nop()
            else:
                op = self.rec[e].drain()
            lvl[e] = self.inc(op, self.own[e])
        for e in ENGINES:
            for o in ENGINES:
                if o != e:
                    self.wait(e, self.own[o], lvl[o])
        for p in self.pools:
            p.reset()

    def finish(self):
        with self.nc.Block() as blk:
            blk.sync(lambda e: self.rec["sync"].replay(e))
            blk.scalar(lambda e: self.rec["scalar"].replay(e))
            blk.vector(lambda e: self.rec["vector"].replay(e))
            blk.tensor(lambda e: self.rec["tensor"].replay(e))
            blk.gpsimd(lambda e: self.rec["gpsimd"].replay(e))


class PsumPool:

    def __init__(self, em, tiles):
        self.em = em
        self.tiles = list(tiles)
        self.last_free = [0] * len(self.tiles)
        self.i = 0
        em.pools.append(self)

    def reset(self):
        self.last_free = [0] * len(self.tiles)
        self.i = 0

    def acquire(self):
        idx = self.i % len(self.tiles)
        self.i += 1
        free_at = self.last_free[idx]
        if free_at:
            sem, cnt = free_at
            self.em.wait("tensor", sem, cnt)
        return idx, self.tiles[idx]

    def release(self, idx, evac_op, evac_engine):
        sem = self.em.own[evac_engine]
        cnt = self.em.inc(evac_op, sem)
        self.last_free[idx] = (sem, cnt)
        return cnt


def ceil_div(a, b):
    return (a + b - 1) // b


def emit_ln_sweep(em, bufs, x_dram, lnxT, ncols, col0, stride, router=None,
                  with_tokmajor=None, double_ln=False, xt_gate=None):
    nc = em.nc
    ntile = ncols // 128
    xt = bufs["xt"]
    scr = bufs["scr"]
    stats = bufs["stats"]
    eye = bufs["eye"]
    eye32 = bufs["eye32"]
    tp = bufs["tp_pool"]
    poolA = bufs["poolA"]
    poolO = bufs["poolO"]
    lnx_tile = bufs["lnx_tile"]
    norm32 = bufs["norm32"]
    f32T = bufs["f32T"]

    SC = em.on("scalar")
    V = em.on("vector")
    TE = em.on("tensor")

    ld = []
    for i in range(ntile):
        t0 = col0 + i * 128
        b = i % 2
        if i >= 2:
            em.wait("sync", "sc", ld[i - 2][1])
        elif xt_gate is not None:
            em.wait("sync", "ve", xt_gate[b])
        c = em.dma("sync", xt[b][:, :], x_dram[t0:t0 + 128, :], sem=f"xa{b}")
        em.wait("scalar", f"xa{b}", c)
        SC.activation(out=scr[:, :], in_=xt[b][:, :], func=AF.Copy,
                      accum_out=stats[:, 4 * i:4 * i + 1])
        op = SC.activation(out=bufs["scr2"][:, :], in_=xt[b][:, :],
                           func=AF.Square,
                           accum_out=stats[:, 4 * i + 1:4 * i + 2])
        ld.append((c, em.inc(op, "sc")))
    sum_ap = stats.ap()[:, 0:4 * ntile:4]
    ssq_ap = stats.ap()[:, 1:4 * ntile:4]
    rstd_ap = stats.ap()[:, 2:4 * ntile:4]
    negmr_ap = stats.ap()[:, 3:4 * ntile:4]
    mean = bufs["mean"]
    em.wait("vector", "sc", ld[-1][1])
    V.tensor_scalar(out=mean[:, :ntile], in0=sum_ap, scalar1=1.0 / D,
                    scalar2=None, op0=ALU.mult)
    fence(em, "vector")
    V.tensor_tensor(out=scr[:, :ntile], in0=mean[:, :ntile],
                    in1=mean[:, :ntile], op=ALU.mult)
    V.tensor_scalar(out=rstd_ap, in0=ssq_ap, scalar1=1.0 / D, scalar2=None,
                    op0=ALU.mult)
    fence(em, "vector")
    op = V.tensor_tensor(out=rstd_ap, in0=rstd_ap, in1=scr[:, :ntile],
                         op=ALU.subtract)
    vready = em.inc(op, "ve")
    if double_ln:
        V.tensor_copy(out=bufs["mean"].ap()[:, NTILE:NTILE + ntile],
                      in_=rstd_ap)
        fence(em, "vector")
    em.wait("scalar", "ve", em.cnt["ve"])
    op = SC.activation(out=rstd_ap, in_=rstd_ap, func=AF.Sqrt,
                       bias=bufs["konst"][:, 0:1])
    sready = em.inc(op, "sc")
    em.wait("vector", "sc", sready)
    V.reciprocal(rstd_ap, rstd_ap)
    fence(em, "vector")
    if double_ln:
        var_ap = bufs["mean"].ap()[:, NTILE:NTILE + ntile]
        tmp_ap = bufs["mean"].ap()[:, NTILE + ntile:NTILE + 2 * ntile]
        V.tensor_scalar(out=tmp_ap, in0=var_ap, scalar1=1e-5, scalar2=None,
                        op0=ALU.add)
        fence(em, "vector")
        V.reciprocal(tmp_ap, tmp_ap)
        fence(em, "vector")
        op = V.tensor_tensor(out=tmp_ap, in0=var_ap, in1=tmp_ap, op=ALU.mult)
        c0 = em.inc(op, "ve")
        em.wait("scalar", "ve", c0)
        op = SC.activation(out=tmp_ap, in_=tmp_ap, func=AF.Sqrt,
                           bias=bufs["konst"][:, 0:1])
        c0 = em.inc(op, "sc")
        em.wait("vector", "sc", c0)
        V.reciprocal(tmp_ap, tmp_ap)
        fence(em, "vector")
        V.tensor_tensor(out=rstd_ap, in0=rstd_ap, in1=tmp_ap, op=ALU.mult)
        fence(em, "vector")
    V.tensor_tensor(out=scr[:, :ntile], in0=mean[:, :ntile],
                    in1=rstd_ap, op=ALU.mult)
    fence(em, "vector")
    op = V.tensor_scalar(out=negmr_ap, in0=scr[:, :ntile], scalar1=-1.0,
                         scalar2=None, op0=ALU.mult)
    nready = em.inc(op, "ve")
    em.wait("scalar", "ve", nready)

    norm_free = [0, 0]
    f32T_busy = [0, 0]
    sweep1_last = ld[-1][1]
    norm_done_hist = [0, 0]
    for i in range(ntile):
        t0 = col0 + i * 128
        b = i % 2
        if i < 2:
            em.wait("sync", "sc", sweep1_last)
        else:
            em.wait("sync", "sc", norm_done_hist[b])
        c = em.dma("sync", xt[b][:, :], x_dram[t0:t0 + 128, :], sem=f"xa{b}")
        em.wait("scalar", f"xa{b}", c)
        em.wait("scalar", "pe", norm_free[b])
        op = SC.activation(out=norm32[b][:, :], in_=xt[b][:, :],
                           func=AF.Identity,
                           scale=stats.ap()[:, 4 * i + 2:4 * i + 3],
                           bias=stats.ap()[:, 4 * i + 3:4 * i + 4])
        norm_done = em.inc(op, "sc")
        norm_done_hist[b] = norm_done
        if with_tokmajor is not None:
            em.wait("vector", "sc", norm_done)
            op = V.tensor_copy(out=with_tokmajor[:, i * D:(i + 1) * D],
                               in_=norm32[b][:, :])
            em.inc(op, "ve")
        em.wait("tensor", "sc", norm_done)
        last_pe = 0
        evac_ves = 0
        for c8 in range(8):
            idx, pt = poolA.acquire()
            op = TE.transpose(pt[:, 0:128], norm32[b][:, c8 * 128:(c8 + 1) * 128],
                              eye32[:, :])
            pe_c = em.inc(op, "pe")
            last_pe = pe_c
            em.wait("vector", "pe", pe_c)
            if router is not None and c8 == 0:
                em.wait("vector", "pe", f32T_busy[i % 2])
            V.tensor_copy(
                out=lnxT[:, c8 * stride + i * 128: c8 * stride + i * 128 + 128],
                in_=pt[:, 0:128])
            if router is not None:
                op = V.tensor_copy(out=f32T[i % 2][:, c8 * 128:(c8 + 1) * 128],
                                   in_=pt[:, 0:128])
                evac_ves = poolA.release(idx, op, "vector")
            else:
                m_, a_, k_, op2 = em.rec["vector"].ops[-1]
                evac_ves = poolA.release(idx, op2, "vector")
        norm_free[b] = last_pe
        if router is None or router.get("skip_mm"):
            continue
        E = router["E"]
        idxo, pso = poolO.acquire()
        em.wait("tensor", "ve", evac_ves)
        if globals().get("DEBUG_ON") and i == globals().get("DEBUG_F32T_TILE", -1):
            o = em.nc.dram_tensor("DBG_f32T", [128, 1024], F32,
                                  kind="ExternalOutput").ap()
            em.wait("sync", "ve", evac_ves)
            cdma = em.dma("sync", o[:, :], f32T[i % 2][:, :], sem="dbgs")
            em.wait("vector", "dbgs", cdma)
        for c8 in range(8):
            op = TE.matmul(pso[:, 0:E], f32T[i % 2][:, c8 * 128:(c8 + 1) * 128],
                           router["gate32"][:, c8 * E:(c8 + 1) * E],
                           start=(c8 == 0),
                           stop=(router["gbias32"] is None and c8 == 7))
        if router["gbias32"] is not None:
            op = TE.matmul(pso[:, 0:E], bufs["ones32"][0:1, :],
                           router["gbias32"][0:1, :], start=False, stop=True)
        pe_c = em.inc(op, "pe")
        f32T_busy[i % 2] = pe_c
        extras = router.get("extras") or {}
        if globals().get("DEBUG_ON") and "route_dbg" in bufs:
            extras = dict(extras)
            extras["dbg"] = bufs["route_dbg"]
            extras["dbg_i"] = i
        route_tile(em, bufs, router["kind"], pso, pe_c,
                   router["cw"][:, i * E:(i + 1) * E], E, poolO, idxo,
                   extras=extras or None)
    return norm_done_hist


def alloc_common(em, lnxT_stride=0):
    nc = em.nc
    bufs = {}
    bufs["xt"] = [nc.alloc_sbuf_tensor(f"xt{i}", [128, D], F32) for i in range(2)]
    bufs["scr"] = nc.alloc_sbuf_tensor("scr", [128, D], F32)
    bufs["scr2"] = bufs["scr"]
    bufs["stats"] = nc.alloc_sbuf_tensor("stats", [128, 4 * NTILE], F32)
    bufs["mean"] = nc.alloc_sbuf_tensor("meanb", [128, 3 * NTILE], F32)
    bufs["eye"] = nc.alloc_sbuf_tensor("eye", [128, 128], BF16)
    bufs["eye32"] = nc.alloc_sbuf_tensor("eye32", [128, 128], F32)
    bufs["lnx_tile"] = [nc.alloc_sbuf_tensor(f"lnxt{i}", [128, D], BF16)
                        for i in range(2)]
    bufs["norm32"] = [nc.alloc_sbuf_tensor(f"norm32_{i}", [128, D], F32)
                      for i in range(2)]
    bufs["f32T"] = [nc.alloc_sbuf_tensor(f"f32T{i}", [128, 8 * 128], F32)
                    for i in range(2)]
    bufs["ones_row"] = nc.alloc_sbuf_tensor("ones_row", [1, 128], BF16)
    bufs["konst"] = nc.alloc_sbuf_tensor("konst", [128, 8], F32)
    bufs["iotabig"] = nc.alloc_sbuf_tensor("iotabig", [128, 16], F32)
    bufs["ones32"] = nc.alloc_sbuf_tensor("ones32", [1, 128], F32)
    bufs["ones_col"] = nc.alloc_sbuf_tensor("ones_col", [128, 1], F32)
    pb = [nc.alloc_psum_tensor(f"PB{i}", [128, 512], F32) for i in range(6)]
    tb = [nc.alloc_psum_tensor(f"PT{i}", [128, 1024], BF16) for i in range(2)]
    em.pbanks = pb
    em.tbanks = tb
    bufs["poolA"] = PsumPool(em, pb[0:4])
    bufs["poolO"] = PsumPool(em, pb[4:6])
    bufs["tp_pool"] = PsumPool(em, tb)
    return bufs


def emit_load_const(em, bufs, eye_dram, konst_dram, konst_dram2):
    cg = em.dma("gpsimd", bufs["eye"][:, :], eye_dram[:, :], sem="cstg")
    em.dma("sync", bufs["eye32"][:, :], eye_dram[:, :], sem="cst")
    em.dma("sync", bufs["konst"][:, :], konst_dram[:, :], sem="cst")
    c2 = em.dma("sync", bufs["iotabig"][:, :], konst_dram2[:, :], sem="cst")
    em.wait("vector", "cst", c2)
    em.wait("vector", "cstg", cg)
    em.wait("tensor", "cst", c2)
    em.wait("tensor", "cstg", cg)
    em.wait("scalar", "cst", c2)
    op = em.on("vector").memset(bufs["ones_row"][0:1, :], 1.0)
    op = em.on("vector").memset(bufs["ones32"][0:1, :], 1.0)
    op = em.on("vector").memset(bufs["ones_col"][:, :], 1.0)
    em.inc(op, "ve")
    return em.cnt["ve"]


def fence(em, engine):
    rec = em.rec[engine]
    m, a, k, op = rec.ops[-1]
    own = em.own[engine]
    for sem, n in op.incs:
        if sem is em.semh[own]:
            em.wait(engine, own, em.cnt[own])
            return
    c = em.inc(op, own)
    em.wait(engine, own, c)


def first_mask(em, bufs, eq_ap, out_ap, tmp_ap, mcol, width):
    V = em.on("vector")
    BIG = 1.0e30
    op = V.scalar_tensor_tensor(out=tmp_ap, in0=eq_ap, scalar=-BIG,
                                in1=bufs["iotabig"][:, 0:width],
                                op0=ALU.mult, op1=ALU.add)
    c = em.inc(op, "ve"); em.wait("vector", "ve", c)
    op = V.tensor_reduce(mcol, tmp_ap, axis=AX.X, op=ALU.min)
    c = em.inc(op, "ve"); em.wait("vector", "ve", c)
    op = V.tensor_scalar(out=out_ap, in0=tmp_ap, scalar1=mcol, scalar2=None,
                         op0=ALU.is_equal)
    c = em.inc(op, "ve"); em.wait("vector", "ve", c)


def pick_max(em, bufs, val_ap, mask_out, tmp_ap, mcols, width):
    V = em.on("vector")
    op = V.reduce_max(mcols[:, 0:1], val_ap, axis=AX.X)
    c = em.inc(op, "ve"); em.wait("vector", "ve", c)
    op = V.tensor_scalar(out=tmp_ap, in0=val_ap, scalar1=mcols[:, 0:1],
                         scalar2=None, op0=ALU.is_equal)
    c = em.inc(op, "ve"); em.wait("vector", "ve", c)
    first_mask(em, bufs, tmp_ap, mask_out, tmp_ap, mcols[:, 1:2], width)


def route_tile(em, bufs, kind, ps, pe_c, cw, E, pool, idx, extras=None):
    V = em.on("vector")
    SC = em.on("scalar")
    lg = bufs["route_lg"]
    t0c = bufs["route_t0"]
    t1c = bufs["route_t1"]
    t2c = bufs["route_t2"]
    colA = bufs["route_cA"]
    BIG = 1.0e30
    em.wait("vector", "pe", pe_c)

    if kind == "argmax1":
        op = V.tensor_copy(out=lg[:, 0:E], in_=ps[:, 0:E])
        V.reduce_max(colA[:, 0:1], ps[:, 0:E], axis=AX.X)
        op = V.reduce_max(colA[:, 1:2], ps[:, 0:E], axis=AX.X, negate=True)
        c = em.inc(op, "ve"); em.wait("vector", "ve", c)
        pool.release(idx, em.rec["vector"].ops[-1][3], "vector")
        op = V.tensor_scalar(out=t0c[:, 0:E], in0=lg[:, 0:E],
                             scalar1=colA[:, 0:1], scalar2=None,
                             op0=ALU.is_equal)
        c = em.inc(op, "ve"); em.wait("vector", "ve", c)
        first_mask(em, bufs, t0c[:, 0:E], t0c[:, 0:E], t1c[:, 0:E],
                   colA[:, 4:5], E)
        em.wait("scalar", "ve", em.cnt["ve"])
        op = SC.activation(out=t2c[:, 0:E], in_=lg[:, 0:E], func=AF.Exp,
                           bias=colA[:, 1:2])
        sc_c = em.inc(op, "sc")
        em.wait("vector", "sc", sc_c)
        op = V.reduce_sum(colA[:, 2:3], t2c[:, 0:E], axis=AX.X)
        c = em.inc(op, "ve"); em.wait("vector", "ve", c)
        op = V.reciprocal(colA[:, 3:4], colA[:, 2:3])
        c = em.inc(op, "ve"); em.wait("vector", "ve", c)
        op = V.tensor_tensor(out=t2c[:, 0:E], in0=t2c[:, 0:E],
                             in1=t0c[:, 0:E], op=ALU.mult)
        c = em.inc(op, "ve"); em.wait("vector", "ve", c)
        op = V.tensor_scalar(out=cw, in0=t2c[:, 0:E], scalar1=colA[:, 3:4],
                             scalar2=None, op0=ALU.mult)
        em.inc(op, "ve")
        if extras and "dbg" in extras:
            d, ti = extras["dbg"], extras["dbg_i"]
            V.tensor_copy(out=d[:, ti * 32 + 0:ti * 32 + 8], in_=lg[:, 0:E])
            V.tensor_copy(out=d[:, ti * 32 + 8:ti * 32 + 16], in_=t0c[:, 0:E])
            V.tensor_copy(out=d[:, ti * 32 + 16:ti * 32 + 24], in_=t2c[:, 0:E])
            op = V.tensor_copy(out=d[:, ti * 32 + 24:ti * 32 + 28], in_=colA[:, 0:4])
            em.inc(op, "ve")

    elif kind == "top2softmax":
        op = V.tensor_copy(out=lg[:, 0:E], in_=ps[:, 0:E])
        c = em.inc(op, "ve"); em.wait("vector", "ve", c)
        pool.release(idx, em.rec["vector"].ops[-1][3], "vector")
        pick_max(em, bufs, lg[:, 0:E], t0c[:, 0:E], t1c[:, 0:E], colA[:, 0:2], E)
        op = V.reduce_max(colA[:, 0:1], lg[:, 0:E], axis=AX.X)
        c = em.inc(op, "ve"); em.wait("vector", "ve", c)
        op = V.scalar_tensor_tensor(out=t1c[:, 0:E], in0=t0c[:, 0:E],
                                    scalar=-BIG, in1=lg[:, 0:E],
                                    op0=ALU.mult, op1=ALU.add)
        c = em.inc(op, "ve"); em.wait("vector", "ve", c)
        op = V.reduce_max(colA[:, 1:2], t1c[:, 0:E], axis=AX.X)
        c = em.inc(op, "ve"); em.wait("vector", "ve", c)
        pick_max(em, bufs, t1c[:, 0:E], t2c[:, 0:E], t1c[:, 16:16 + E],
                 colA[:, 2:4], E)
        op = V.tensor_tensor(out=colA[:, 4:5], in0=colA[:, 0:1],
                             in1=colA[:, 1:2], op=ALU.subtract)
        c = em.inc(op, "ve")
        em.wait("scalar", "ve", c)
        op = SC.activation(out=colA[:, 5:6], in_=colA[:, 4:5], func=AF.Sigmoid)
        sc_c = em.inc(op, "sc")
        em.wait("vector", "sc", sc_c)
        op = V.tensor_scalar(out=colA[:, 6:7], in0=colA[:, 5:6],
                             scalar1=-1.0, scalar2=1.0, op0=ALU.mult,
                             op1=ALU.add)
        c = em.inc(op, "ve"); em.wait("vector", "ve", c)
        op = V.tensor_scalar(out=t0c[:, 0:E], in0=t0c[:, 0:E],
                             scalar1=colA[:, 5:6], scalar2=None, op0=ALU.mult)
        c = em.inc(op, "ve"); em.wait("vector", "ve", c)
        op = V.scalar_tensor_tensor(out=cw, in0=t2c[:, 0:E],
                                    scalar=colA[:, 6:7], in1=t0c[:, 0:E],
                                    op0=ALU.mult, op1=ALU.add)
        em.inc(op, "ve")

    elif kind == "l1group":
        NG, GS, TOPK = 4, 4, 4
        selm = bufs["route_sel"]
        em.wait("scalar", "pe", pe_c)
        op = SC.activation(out=lg[:, 0:E], in_=ps[:, 0:E], func=AF.Sigmoid)
        sc_c = em.inc(op, "sc")
        em.wait("vector", "sc", sc_c)
        pool.release(idx, em.rec["vector"].ops[-1][3], "vector")
        op = V.tensor_tensor(out=t0c[:, 0:E], in0=lg[:, 0:E],
                             in1=extras["bias_rep"][:, 0:E], op=ALU.add)
        c = em.inc(op, "ve"); em.wait("vector", "ve", c)
        for g in range(NG):
            sl = t0c[:, g * GS:(g + 1) * GS]
            op = V.reduce_max(colA[:, 0:1], sl, axis=AX.X)
            c = em.inc(op, "ve"); em.wait("vector", "ve", c)
            pick_max(em, bufs, sl, t1c[:, 0:GS], t1c[:, 8:8 + GS],
                     colA[:, 6:8], GS)
            op = V.scalar_tensor_tensor(out=t1c[:, 0:GS], in0=t1c[:, 0:GS],
                                        scalar=-BIG, in1=sl,
                                        op0=ALU.mult, op1=ALU.add)
            c = em.inc(op, "ve"); em.wait("vector", "ve", c)
            op = V.reduce_max(colA[:, 1:2], t1c[:, 0:GS], axis=AX.X)
            c = em.inc(op, "ve"); em.wait("vector", "ve", c)
            op = V.tensor_tensor(out=colA[:, 2 + g:3 + g], in0=colA[:, 0:1],
                                 in1=colA[:, 1:2], op=ALU.add)
            c = em.inc(op, "ve"); em.wait("vector", "ve", c)
        gs_ap = colA[:, 2:6]
        pick_max(em, bufs, gs_ap, t2c[:, 0:4], t1c[:, 0:4], colA[:, 6:8], 4)
        op = V.scalar_tensor_tensor(out=t1c[:, 0:4], in0=t2c[:, 0:4],
                                    scalar=-BIG, in1=gs_ap,
                                    op0=ALU.mult, op1=ALU.add)
        c = em.inc(op, "ve"); em.wait("vector", "ve", c)
        pick_max(em, bufs, t1c[:, 0:4], t1c[:, 4:8], t1c[:, 8:12],
                 colA[:, 6:8], 4)
        op = V.tensor_tensor(out=t2c[:, 0:4], in0=t2c[:, 0:4],
                             in1=t1c[:, 4:8], op=ALU.max)
        c = em.inc(op, "ve"); em.wait("vector", "ve", c)
        for g in range(NG):
            V.tensor_scalar(out=t1c[:, g * GS:(g + 1) * GS],
                            in0=t0c[:, g * GS:(g + 1) * GS],
                            scalar1=t2c[:, g:g + 1], scalar2=None,
                            op0=ALU.mult)
        V.memset(selm[:, 0:E], 0.0)
        fence(em, "vector")
        for kk in range(TOPK):
            pick_max(em, bufs, t1c[:, 0:E], t2c[:, 0:E], t2c[:, 16:16 + E],
                     colA[:, 6:8], E)
            V.tensor_tensor(out=selm[:, 0:E], in0=selm[:, 0:E],
                            in1=t2c[:, 0:E], op=ALU.max)
            op = V.scalar_tensor_tensor(out=t1c[:, 0:E], in0=t2c[:, 0:E],
                                        scalar=-BIG, in1=t1c[:, 0:E],
                                        op0=ALU.mult, op1=ALU.add)
            c = em.inc(op, "ve"); em.wait("vector", "ve", c)
        op = V.tensor_tensor(out=t0c[:, 0:E], in0=lg[:, 0:E],
                             in1=selm[:, 0:E], op=ALU.mult)
        c = em.inc(op, "ve"); em.wait("vector", "ve", c)
        op = V.reduce_sum(colA[:, 0:1], t0c[:, 0:E], axis=AX.X)
        c = em.inc(op, "ve"); em.wait("vector", "ve", c)
        V.tensor_scalar(out=colA[:, 0:1], in0=colA[:, 0:1], scalar1=1e-20,
                        scalar2=None, op0=ALU.add)
        fence(em, "vector")
        op = V.reciprocal(colA[:, 1:2], colA[:, 0:1])
        c = em.inc(op, "ve"); em.wait("vector", "ve", c)
        op = V.tensor_scalar(out=cw, in0=t0c[:, 0:E], scalar1=colA[:, 1:2],
                             scalar2=2.5, op0=ALU.mult, op1=ALU.mult)
        em.inc(op, "ve")
    else:
        raise ValueError(kind)



def alloc_route_bufs(em):
    nc = em.nc
    b = {}
    b["route_lg"] = nc.alloc_sbuf_tensor("route_lg", [128, 16], F32)
    b["route_t0"] = nc.alloc_sbuf_tensor("route_t0", [128, 32], F32)
    b["route_t1"] = nc.alloc_sbuf_tensor("route_t1", [128, 32], F32)
    b["route_t2"] = nc.alloc_sbuf_tensor("route_t2", [128, 32], F32)
    b["route_cA"] = nc.alloc_sbuf_tensor("route_cA", [128, 8], F32)
    b["route_zr"] = nc.alloc_sbuf_tensor("route_zr", [128, 16], F32)
    b["route_sel"] = nc.alloc_sbuf_tensor("route_sel", [128, 16], F32)
    op = em.on("vector").memset(b["route_zr"][:, :], 0.0)
    em.inc(op, "ve")
    return b


def emit_ffn_block(em, bufs, xin, xout, cfg):
    nc = em.nc
    M = cfg["macro"]
    E = cfg["E"]
    nmac = T // M
    ntile_m = M // 128
    nts = M // 512
    lnxT = cfg["lnxT"]
    acc = cfg["acc"]
    cwb = cfg["cw"]
    hT = cfg["hT"]
    w1b, w2b = cfg["w1b"], cfg["w2b"]
    w3b = cfg.get("w3b")
    scrA = cfg["scrA"]
    poolA, poolO = cfg["poolA"], cfg["poolO"]
    ones = bufs["ones_row"]
    TE = em.on("tensor")
    V = em.on("vector")
    SC = em.on("scalar")

    HU = cfg["HU"]
    w_last_pe = {"w1": [0, 0], "w2": [0, 0], "w3": [0, 0]}
    hT_last_pe = [0, 0]
    uctr = 0
    scrA_last_ve = [0, 0]
    xo = cfg["xo"]
    xo_store = [0, 0]

    for mi in range(nmac):
        em.barrier()
        ln_hist = emit_ln_sweep(em, bufs, xin, lnxT, M, mi * M, M,
                                router=cfg["router"])
        fence(em, "vector")
        for ui, u in enumerate(units_list := cfg["units"]):
            hid = u["hid"]
            nhc = hid // 128
            sl = uctr % 2
            uctr += 1
            w1o = sl * 8 * HU
            w2o = sl * (HU // 128) * D
            hto = sl * (HU // 128) * 512
            em.wait("gpsimd", "pe", w_last_pe["w1"][sl])
            c1 = em.dma("gpsimd",
                        w1b.ap()[:, w1o:w1o + 8 * hid].rearrange(
                            "p (c h) -> p c h", c=8),
                        u["w1"].rearrange("(c p) h -> p c h", p=128),
                        sem=f"w1s{sl}")
            em.wait("gpsimd", "pe", w_last_pe["w2"][sl])
            c2 = em.dma("gpsimd",
                        w2b.ap()[:, w2o:w2o + nhc * D].rearrange(
                            "p (c d) -> p c d", c=nhc),
                        u["w2"].rearrange("(c p) d -> p c d", p=128),
                        sem=f"w2s{sl}")
            if u.get("w3") is not None:
                em.wait("gpsimd", "pe", w_last_pe["w3"][sl])
                c3 = em.dma("gpsimd",
                            w3b.ap()[:, w1o:w1o + 8 * hid].rearrange(
                                "p (c h) -> p c h", c=8),
                            u["w3"].rearrange("(c p) h -> p c h", p=128),
                            sem=f"w3s{sl}")
                em.wait("tensor", f"w3s{sl}", c3)
            em.wait("tensor", f"w1s{sl}", c1)
            em.wait("tensor", f"w2s{sl}", c2)
            for ts in range(nts):
                for hc in range(nhc):
                    idx, ps = poolA.acquire()
                    for kc in range(8):
                        op = TE.matmul(
                            ps[:, :],
                            w1b[:, w1o + kc * hid + hc * 128:
                                w1o + kc * hid + hc * 128 + 128],
                            lnxT[:, kc * M + ts * 512: kc * M + ts * 512 + 512],
                            start=(kc == 0), stop=(kc == 7))
                    pe_c = em.inc(op, "pe")
                    w_last_pe["w1"][sl] = pe_c
                    hcol = u["hid0"] + hc * 128
                    if u["act"] == "gelu":
                        em.wait("scalar", "pe", pe_c)
                        em.wait("scalar", "pe", hT_last_pe[sl])
                        kw = {}
                        if u.get("b1") is not None:
                            bt, be, bE = u["b1"]
                            bc = (hcol // 128) * bE + be
                            kw["bias"] = bt[:, bc:bc + 1]
                        op = SC.activation(
                            out=hT[:, hto + hc * 512:hto + (hc + 1) * 512],
                            in_=ps[:, :], func=AF.Gelu, **kw)
                        poolA.release(idx, op, "scalar")
                    elif u["act"] == "relu2":
                        sb = (ts * nhc + hc) % 2
                        em.wait("scalar", "ve", scrA_last_ve[sb])
                        em.wait("scalar", "pe", pe_c)
                        op = SC.activation(out=scrA[sb][:, :], in_=ps[:, :],
                                           func=AF.Relu)
                        sc_c = poolA.release(idx, op, "scalar")
                        em.wait("vector", "sc", sc_c)
                        em.wait("vector", "pe", hT_last_pe[sl])
                        op = V.tensor_tensor(
                            out=hT[:, hto + hc * 512:hto + (hc + 1) * 512],
                            in0=scrA[sb][:, :],
                            in1=scrA[sb][:, :], op=ALU.mult)
                        scrA_last_ve[sb] = em.inc(op, "ve")
                    elif u["act"] == "silu_gate":
                        idx2, ps2 = poolA.acquire()
                        for kc in range(8):
                            op = TE.matmul(
                                ps2[:, :],
                                w3b[:, w1o + kc * hid + hc * 128:
                                    w1o + kc * hid + hc * 128 + 128],
                                lnxT[:, kc * M + ts * 512: kc * M + ts * 512 + 512],
                                start=(kc == 0), stop=(kc == 7))
                        pe_c = em.inc(op, "pe")
                        w_last_pe["w3"][sl] = pe_c
                        sb = (ts * nhc + hc) % 2
                        em.wait("scalar", "ve", scrA_last_ve[sb])
                        em.wait("scalar", "pe", pe_c)
                        op = SC.activation(out=scrA[sb][:, :], in_=ps[:, :],
                                           func=AF.Silu)
                        sc_c = poolA.release(idx, op, "scalar")
                        em.wait("vector", "sc", sc_c)
                        em.wait("vector", "pe", hT_last_pe[sl])
                        op = V.tensor_tensor(
                            out=hT[:, hto + hc * 512:hto + (hc + 1) * 512],
                            in0=scrA[sb][:, :], in1=ps2[:, :],
                            op=ALU.mult)
                        scrA_last_ve[sb] = poolA.release(idx2, op, "vector")
                    else:
                        raise ValueError(u["act"])
                if u["act"] == "gelu":
                    em.wait("tensor", "sc", em.cnt["sc"])
                else:
                    em.wait("tensor", "ve", em.cnt["ve"])
                for tt in range(4):
                    tt_g = ts * 4 + tt
                    for dn in range(2):
                        idxo, pso = poolO.acquire()
                        nmm = nhc + (1 if u.get("b2") is not None else 0)
                        k = 0
                        for hc in range(nhc):
                            op = TE.matmul(
                                pso[:, :],
                                hT[:, hto + hc * 512 + tt * 128:
                                    hto + hc * 512 + tt * 128 + 128],
                                w2b[:, w2o + hc * D + dn * 512:
                                    w2o + hc * D + dn * 512 + 512],
                                start=(k == 0), stop=(k == nmm - 1))
                            k += 1
                        if u.get("b2") is not None:
                            op = TE.matmul(pso[:, :], ones[0:1, :],
                                           u["b2"][0:1, dn * 512:dn * 512 + 512],
                                           start=False, stop=True)
                            k += 1
                        pe_c = em.inc(op, "pe")
                        w_last_pe["w2"][sl] = pe_c
                        hT_last_pe[sl] = pe_c
                        em.wait("vector", "pe", pe_c)
                        a_sl = acc[:, tt_g * D + dn * 512: tt_g * D + dn * 512 + 512]
                        if u["e"] is not None:
                            cwcol = cwb[:, tt_g * E + u["e"]: tt_g * E + u["e"] + 1]
                            if ui == 0:
                                op = V.tensor_scalar(out=a_sl, in0=pso[:, :],
                                                     scalar1=cwcol, scalar2=None,
                                                     op0=ALU.mult)
                            else:
                                op = V.scalar_tensor_tensor(
                                    out=a_sl, in0=pso[:, :], scalar=cwcol,
                                    in1=a_sl, op0=ALU.mult, op1=ALU.add)
                        else:
                            if ui == 0:
                                op = V.tensor_copy(out=a_sl, in_=pso[:, :])
                            else:
                                op = V.tensor_tensor(out=a_sl, in0=pso[:, :],
                                                     in1=a_sl, op=ALU.add)
                        poolO.release(idxo, op, "vector")
        xt_used = [0, 0]
        for tt_g in range(ntile_m):
            b = tt_g % 2
            t0 = mi * M + tt_g * 128
            em.wait("sync", "ve", xt_used[b])
            em.wait("sync", "sc", ln_hist[b])
            c = em.dma("sync", bufs["xt"][b][:, :], xin[t0:t0 + 128, :],
                       sem=f"xa{b}")
            em.wait("vector", f"xa{b}", c)
            em.wait("vector", f"xs{b}", xo_store[b])
            op = V.tensor_tensor(out=xo[b][:, :], in0=bufs["xt"][b][:, :],
                                 in1=acc[:, tt_g * D:(tt_g + 1) * D], op=ALU.add)
            ve_c = em.inc(op, "ve")
            xt_used[b] = ve_c
            em.wait("sync", "ve", ve_c)
            xo_store[b] = em.dma("sync", xout[t0:t0 + 128, :], xo[b][:, :],
                                 sem=f"xs{b}")


def setup_ffn_generic(em, bufs, xin, xout, name, spec, decls):
    nc = em.nc
    M = spec["macro"]
    E = spec["E"]
    HU = spec["hid_unit"]
    lnxT = em.salloc(f"{name}_lnxT", [128, 8 * M], BF16)
    acc = em.salloc(f"{name}_acc", [128, (M // 128) * D], F32)
    cw = em.salloc(f"{name}_cw", [128, (M // 128) * E], F32)
    hT = em.salloc(f"{name}_hT", [128, 2 * (HU // 128) * 512], BF16)
    w1b = em.salloc(f"{name}_w1b", [128, 2 * 8 * HU], BF16)
    w2b = em.salloc(f"{name}_w2b", [128, 2 * (HU // 128) * D], BF16)
    w3b = None
    if spec.get("has_w3"):
        w3b = em.salloc(f"{name}_w3b", [128, 2 * 8 * HU], BF16)
    scrA = [em.salloc(f"{name}_scrA{i}", [128, 512], F32)
            for i in range(2)]
    xo = [em.salloc(f"{name}_xo{i}", [128, D], F32) for i in range(2)]
    poolA = bufs["poolA"]
    poolO = bufs["poolO"]

    cstS = f"cst_{name}"
    cstG = f"cstg_{name}"
    gate = em.salloc(f"{name}_gate", [128, 8 * E], F32)
    gname = decls.dram(em, f"{name}_gatew", [D, E])
    c = em.dma("sync", gate.ap().rearrange("p (c e) -> p c e", c=8),
               gname.rearrange("(c p) e -> p c e", p=128), sem=cstS)
    gbias = None
    if spec.get("gate_bias"):
        gbias = em.salloc(f"{name}_gbias", [1, E], F32)
        gb = decls.dram(em, f"{name}_gatebias", [1, E])
        c = max(c, em.dma("sync", gbias[0:1, :], gb[0:1, :], sem=cstS))
    c = 0
    b1t = None
    if spec.get("has_b1"):
        nhc_tot = spec["b1_len"] // 128
        b1t = em.salloc(f"{name}_b1ts", [128, nhc_tot * spec["b1_E"]], F32)
        bh = decls.dram(em, f"{name}_b1t", [spec["b1_len"], spec["b1_E"]])
        c = max(c, em.dma("gpsimd",
                          b1t.ap().rearrange("p (c e) -> p c e", c=nhc_tot),
                          bh.rearrange("(c p) e -> p c e", p=128), sem=cstG))
    b2r = None
    if spec.get("has_b2"):
        nb2 = spec["b2_n"]
        b2r = em.salloc(f"{name}_b2rs", [1, nb2 * D], BF16)
        bh = decls.dram(em, f"{name}_b2r", [1, nb2 * D])
        c = max(c, em.dma("gpsimd", b2r[0:1, :], bh[0:1, :], sem=cstG))
    bias_rep = None
    if spec.get("routing") == "l1group":
        bias_rep = em.salloc(f"{name}_brep", [128, E], F32)
        brow = em.salloc(f"{name}_brow", [1, E], F32)
        bh = decls.dram(em, f"{name}_biase", [1, E])
        cs = em.dma("sync", brow[0:1, :], bh[0:1, :], sem=cstS)
        em.wait("tensor", cstS, em.cnt[cstS])
        em.wait("tensor", "ve", em.cnt["ve"])
        idx, ps = poolA.acquire()
        op = em.on("tensor").matmul(ps[:, 0:E], bufs["ones32"][0:1, :],
                                    brow[0:1, :], start=True, stop=True)
        pe_c = em.inc(op, "pe")
        em.wait("vector", "pe", pe_c)
        op = em.on("vector").tensor_copy(out=bias_rep[:, :], in_=ps[:, 0:E])
        poolA.release(idx, op, "vector")
    em.wait("tensor", cstS, em.cnt.get(cstS, 0))
    if cstG in em.cnt:
        em.wait("tensor", cstG, em.cnt[cstG])
        em.wait("scalar", cstG, em.cnt[cstG])

    units = []
    for ud in spec["units_def"]:
        u = dict(ud)
        u["w1"] = decls.dram_ap(em, u.pop("w1name"))
        u["w2"] = decls.dram_ap(em, u.pop("w2name"))
        if "w3name" in u:
            u["w3"] = decls.dram_ap(em, u.pop("w3name"))
        if u.pop("use_b1", False):
            u["b1"] = (b1t, u.pop("b1col"), spec["b1_E"])
        if "b2slice" in u:
            s0 = u.pop("b2slice")
            u["b2"] = b2r[0:1, s0 * D:(s0 + 1) * D]
        units.append(u)

    if DEBUG_DUMP is not None and globals().get("DEBUG_ON"):
        DEBUG_DUMP.append((f"{name}_cw", cw, [128, (M // 128) * E]))
    router = dict(kind=spec["routing"], E=E, gate32=gate, gbias32=gbias,
                  cw=cw, extras={"bias_rep": bias_rep}
                  if bias_rep is not None else None)

    cfg = dict(macro=M, E=E, HU=HU, lnxT=lnxT, acc=acc, cw=cw, hT=hT,
               w1b=w1b, w2b=w2b, w3b=w3b, scrA=scrA, xo=xo, poolA=poolA,
               poolO=poolO, units=units, router=router)
    emit_ffn_block(em, bufs, xin, xout, cfg)


class Decls:

    def __init__(self):
        self.shapes = {}
        self.aps = {}

    def dram(self, em, name, shape):
        if name not in self.aps:
            self.shapes[name] = list(shape)
            self.aps[name] = em.nc.dram_tensor(
                name, list(shape), F32, kind="ExternalInput").ap()
        return self.aps[name]

    def register(self, em, name, shape):
        return self.dram(em, name, shape)

    def dram_ap(self, em, spec):
        name, shape, slicer = spec
        ap = self.dram(em, name, shape)
        return slicer(ap)


def ffn_spec(block):
    if block == "l0f":
        units = []
        for e in range(8):
            for h in range(2):
                units.append(dict(
                    e=e, hid=512, hid0=h * 512, act="gelu", use_b1=True,
                    b1col=e, **(dict(b2slice=e) if h == 1 else {}),
                    w1name=("l0f_W1", [8 * D, 1024],
                            lambda ap, e=e, h=h:
                            ap[e * D:(e + 1) * D, h * 512:(h + 1) * 512]),
                    w2name=("l0f_W2", [8 * 1024, D],
                            lambda ap, e=e, h=h:
                            ap[e * 1024 + h * 512: e * 1024 + (h + 1) * 512, :]),
                ))
        return dict(E=8, macro=2048, hid_unit=512, units_def=units,
                    routing="argmax1", gate_bias=True, has_b1=True,
                    b1_len=1024, b1_E=8, has_b2=True, b2_n=8)
    if block == "l1f":
        units = []
        for e in range(16):
            units.append(dict(
                e=e, hid=512, hid0=0, act="relu2",
                w1name=("l1f_up", [16 * D, 512], lambda ap, e=e: ap[e * D:(e + 1) * D, :]),
                w2name=("l1f_dn", [16 * 512, D], lambda ap, e=e: ap[e * 512:(e + 1) * 512, :]),
            ))
        for h in range(4):
            units.append(dict(
                e=None, hid=512, hid0=h * 512, act="relu2",
                w1name=("l1f_sup", [D, 2048],
                        lambda ap, h=h: ap[:, h * 512:(h + 1) * 512]),
                w2name=("l1f_sdn", [2048, D],
                        lambda ap, h=h: ap[h * 512:(h + 1) * 512, :]),
            ))
        return dict(E=16, macro=2048, hid_unit=512, units_def=units,
                    routing="l1group", gate_bias=False)
    if block == "l2f":
        units = []
        for e in range(8):
            for h in range(8):
                units.append(dict(
                    e=e, hid=512, hid0=h * 512, act="gelu", use_b1=True,
                    b1col=e, **(dict(b2slice=e) if h == 7 else {}),
                    w1name=("l2f_w1", [8 * D, 4096],
                            lambda ap, e=e, h=h:
                            ap[e * D:(e + 1) * D, h * 512:(h + 1) * 512]),
                    w2name=("l2f_w2", [8 * 4096, D],
                            lambda ap, e=e, h=h:
                            ap[e * 4096 + h * 512: e * 4096 + (h + 1) * 512, :]),
                ))
        for h in range(8):
            units.append(dict(
                e=None, hid=512, hid0=h * 512, act="gelu", use_b1=True,
                b1col=8, **(dict(b2slice=8) if h == 7 else {}),
                w1name=("l2f_ws1", [D, 4096],
                        lambda ap, h=h: ap[:, h * 512:(h + 1) * 512]),
                w2name=("l2f_ws2", [4096, D],
                        lambda ap, h=h: ap[h * 512:(h + 1) * 512, :]),
            ))
        return dict(E=8, macro=2048, hid_unit=512, units_def=units,
                    routing="top2softmax", gate_bias=True, has_b1=True,
                    b1_len=4096, b1_E=9, has_b2=True, b2_n=9)
    if block == "l3f":
        units = []
        for e in range(8):
            for h in range(4):
                units.append(dict(
                    e=e, hid=512, hid0=h * 512, act="silu_gate",
                    w1name=("l3f_w1", [8 * D, 2048],
                            lambda ap, e=e, h=h:
                            ap[e * D:(e + 1) * D, h * 512:(h + 1) * 512]),
                    w3name=("l3f_w3", [8 * D, 2048],
                            lambda ap, e=e, h=h:
                            ap[e * D:(e + 1) * D, h * 512:(h + 1) * 512]),
                    w2name=("l3f_w2", [8 * 2048, D],
                            lambda ap, e=e, h=h:
                            ap[e * 2048 + h * 512: e * 2048 + (h + 1) * 512, :]),
                ))
        for h in range(4):
            units.append(dict(
                e=None, hid=512, hid0=h * 512, act="silu_gate",
                w1name=("l3f_sw1", [D, 2048],
                        lambda ap, h=h: ap[:, h * 512:(h + 1) * 512]),
                w3name=("l3f_sw3", [D, 2048],
                        lambda ap, h=h: ap[:, h * 512:(h + 1) * 512]),
                w2name=("l3f_sw2", [2048, D],
                        lambda ap, h=h: ap[h * 512:(h + 1) * 512, :]),
            ))
        return dict(E=8, macro=2048, hid_unit=512, units_def=units,
                    routing="top2softmax", gate_bias=True, has_w3=True)
    raise ValueError(block)


def ffn_host_arrays(block, inp):
    a = {}
    if block == "l0f":
        a["l0f_gatew"] = inp["L0_ffn_Wg"]
        a["l0f_gatebias"] = inp["L0_ffn_bg"].reshape(1, 8)
        a["l0f_W1"] = inp["L0_ffn_W1"].reshape(8 * D, 1024)
        a["l0f_W2"] = inp["L0_ffn_W2"].reshape(8 * 1024, D)
        a["l0f_b1t"] = np.ascontiguousarray(inp["L0_ffn_b1"].T)
        a["l0f_b2r"] = inp["L0_ffn_b2"].reshape(1, 8 * D)
    elif block == "l1f":
        a["l1f_gatew"] = np.ascontiguousarray(inp["L1_ffn_gate_w"].T)
        a["l1f_biase"] = inp["L1_ffn_bias_e"].reshape(1, 16)
        a["l1f_up"] = np.ascontiguousarray(
            inp["L1_ffn_up_w"].transpose(0, 2, 1)).reshape(16 * D, 512)
        a["l1f_dn"] = np.ascontiguousarray(
            inp["L1_ffn_down_w"].transpose(0, 2, 1)).reshape(16 * 512, D)
        a["l1f_sup"] = np.ascontiguousarray(inp["L1_ffn_shared_up"].T)
        a["l1f_sdn"] = np.ascontiguousarray(inp["L1_ffn_shared_down"].T)
    elif block == "l2f":
        a["l2f_gatew"] = inp["L2_ffn_gate_w"]
        a["l2f_gatebias"] = inp["L2_ffn_gate_b"].reshape(1, 8)
        a["l2f_w1"] = inp["L2_ffn_w1"].reshape(8 * D, 4096)
        a["l2f_w2"] = inp["L2_ffn_w2"].reshape(8 * 4096, D)
        a["l2f_ws1"] = inp["L2_ffn_ws1"]
        a["l2f_ws2"] = inp["L2_ffn_ws2"]
        b1t = np.concatenate(
            [np.ascontiguousarray(inp["L2_ffn_b1"].T),
             inp["L2_ffn_bs1"].reshape(4096, 1)], axis=1)
        a["l2f_b1t"] = np.ascontiguousarray(b1t)
        b2r = np.concatenate(
            [inp["L2_ffn_b2"].reshape(8 * D), inp["L2_ffn_bs2"]], axis=0)
        a["l2f_b2r"] = b2r.reshape(1, 9 * D)
    elif block == "l3f":
        a["l3f_gatew"] = np.ascontiguousarray(inp["L3_ffn_router_w"].T)
        a["l3f_gatebias"] = inp["L3_ffn_experts_bias"].reshape(1, 8)
        a["l3f_w1"] = inp["L3_ffn_w1"].reshape(8 * D, 2048)
        a["l3f_w3"] = inp["L3_ffn_w3"].reshape(8 * D, 2048)
        a["l3f_w2"] = inp["L3_ffn_w2"].reshape(8 * 2048, D)
        a["l3f_sw1"] = inp["L3_ffn_sw1"].reshape(D, 2048)
        a["l3f_sw3"] = inp["L3_ffn_sw3"].reshape(D, 2048)
        a["l3f_sw2"] = inp["L3_ffn_sw2"].reshape(2048, D)
    return a


FFN_BLOCKS = {"l0f", "l1f", "l2f", "l3f"}


def build_program(blocks):
    nc = bass.Bass("TRN2", target_bir_lowering=False,
                   detect_race_conditions=globals().get("DETECT_RACES", True))
    es = contextlib.ExitStack()
    em = EM(nc, es)
    decls = Decls()
    xin = nc.dram_tensor("XIN", [T, D], F32, kind="ExternalInput").ap()
    xout = nc.dram_tensor("XOUT", [T, D], F32, kind="ExternalOutput").ap()
    eye_dram = decls.dram(em, "CEYE", [128, 128])
    konst_dram = decls.dram(em, "CONSTS", [128, 8])
    konst2_dram = decls.dram(em, "CIOTA", [128, 16])
    xs = [xin]
    for i in range(len(blocks) - 1):
        xs.append(nc.dram_tensor(f"XSCR{i}", [T, D], F32, kind="Internal").ap())
    xs.append(xout)

    bufs = alloc_common(em, 0)
    bufs.update(alloc_route_bufs(em))
    if globals().get("DEBUG_ON"):
        bufs["route_dbg"] = em.salloc("route_dbg", [128, NTILE * 32], F32)
        DEBUG_DUMP.append(("route_dbg", bufs["route_dbg"], [128, NTILE * 32]))
    emit_load_const(em, bufs, eye_dram, konst_dram, konst2_dram)

    for i, b in enumerate(blocks):
        em.barrier()
        with contextlib.ExitStack() as blk_es:
            em.cur_es = blk_es
            if b in FFN_BLOCKS:
                setup_ffn_generic(em, bufs, xs[i], xs[i + 1], b,
                                  ffn_spec(b), decls)
            else:
                ATTN_SETUP[b](em, bufs, xs[i], xs[i + 1], decls)
    emit_debug_dumps(em)
    em.barrier()
    em.finish()
    return nc, es, decls


def host_arrays_for(blocks, inp):
    a = {"CEYE": np.eye(128, dtype=np.float32),
         "CONSTS": np.tile(np.array(
             [1e-5, 1e-12, 1.0, 0.0, 0.0, 0.0, 0.0, 0.0], np.float32),
             (128, 1)),
         "CIOTA": np.tile(np.arange(16, dtype=np.float32), (128, 1))}
    for b in blocks:
        if b in FFN_BLOCKS:
            a.update(ffn_host_arrays(b, inp))
        else:
            a.update(ATTN_HOST[b](inp))
    return a


ATTN_SETUP = {}
ATTN_HOST = {}
DEBUG_DUMP = []


def emit_debug_dumps(em):
    nc = em.nc
    for name, t, shape in DEBUG_DUMP:
        o = nc.dram_tensor(f"DBG_{name}", shape, F32, kind="ExternalOutput").ap()
        em.dma("sync", o[:, :], t[0:shape[0], 0:shape[1]])


def check_deadlock(em, verbose=False):
    pcs = {e: 0 for e in ENGINES}
    cnt = {name: 0 for name in em.semh}
    by_id = {id(h): n for n, h in em.semh.items()}
    total = sum(len(em.rec[e].ops) for e in ENGINES)
    progress = True
    while progress:
        progress = False
        for e in ENGINES:
            ops = em.rec[e].ops
            while pcs[e] < len(ops):
                m, a, k, op = ops[pcs[e]]
                if m == "wait_ge":
                    sem, val = a[0], a[1]
                    if cnt[by_id[id(sem)]] < val:
                        break
                for sem, n in op.incs:
                    cnt[by_id[id(sem)]] += n
                pcs[e] += 1
                progress = True
    stuck = {e: (pcs[e], len(em.rec[e].ops)) for e in ENGINES
             if pcs[e] < len(em.rec[e].ops)}
    if stuck:
        print("DEADLOCK at:", stuck)
        for e, (pc, n) in stuck.items():
            m, a, k, op = em.rec[e].ops[pc]
            if m == "wait_ge":
                nm = by_id[id(a[0])]
                print(f"  {e}[{pc}] waits {nm} >= {a[1]}, have {cnt[nm]}")
            else:
                print(f"  {e}[{pc}] op {m}")
        return False
    if verbose:
        print(f"no deadlock; {total} ops drained")
    return True


def setup_l3_attn(em, bufs, xin, xout, decls):
    nc = em.nc
    name = "l3a"
    C = 128
    NCH = T // C
    NH, HD = 16, 64
    V = em.on("vector")
    SC = em.on("scalar")
    TE = em.on("tensor")
    poolA, poolO = bufs["poolA"], bufs["poolO"]

    MH = T // 4
    lnxT = em.salloc(f"{name}_lnxT", [128, 8 * MH], BF16)
    wq = em.salloc(f"{name}_wq", [128, 8 * D], BF16)
    wk = em.salloc(f"{name}_wk", [128, 8 * D], BF16)
    wv = em.salloc(f"{name}_wv", [128, 8 * D], BF16)
    wo = em.salloc(f"{name}_wo", [128, 8 * D], BF16)
    qT = em.salloc(f"{name}_qT", [64, 16 * C], BF16)
    kT = em.salloc(f"{name}_kT", [64, 16 * C], BF16)
    vtok = em.salloc(f"{name}_vtok", [128, D], BF16)
    ktok = em.salloc(f"{name}_ktok", [128, D], BF16)
    kw = em.salloc(f"{name}_kw", [128, D], BF16)
    qgT = em.salloc(f"{name}_qgT", [64, 16 * C], BF16)
    PT = em.salloc(f"{name}_PT", [128, 16 * C], BF16)
    ybuf = [em.salloc(f"{name}_ybuf{i}", [64, 16 * C], BF16)
            for i in range(2)]
    dloc = em.salloc(f"{name}_dlocs", [128, 16 * C], F32)
    gp = em.salloc(f"{name}_gps", [64, 16 * C], F32)
    gwc = em.salloc(f"{name}_gwcs", [128, 16], F32)
    st32 = em.salloc(f"{name}_st32", [64, 16 * HD], F32)
    stbf = em.salloc(f"{name}_stbf", [64, 16 * HD], BF16)
    ysum = em.salloc(f"{name}_ysum", [64, 16 * NCH], F32)
    ysq = em.salloc(f"{name}_ysq", [64, 16 * NCH], F32)
    yscr = em.salloc(f"{name}_yscr", [64, C], F32)
    scols = em.salloc(f"{name}_scols", [64, 32], F32)
    hstat = em.salloc(f"{name}_hstat", [32, 4], F32)
    hrow = em.salloc(f"{name}_hrow", [1, 64], F32)
    hrep = em.salloc(f"{name}_hrep", [128, 64], F32)
    abcol = em.salloc(f"{name}_abcol", [128, 32], F32)
    gnwb = em.salloc(f"{name}_gnwbs", [128, 16], F32)
    ynT = em.salloc(f"{name}_ynT", [128, 8 * 512], BF16)
    ynS = [em.salloc(f"{name}_ynS{i}", [128, 512], BF16)
           for i in range(2)]
    ysp = nc.dram_tensor(f"{name}_YSP", [8 * 128, T], BF16, kind="Internal").ap()
    xo = [em.salloc(f"{name}_xo{i}", [128, D], F32) for i in range(2)]

    cstS, cstG = f"cst_{name}", f"cstg_{name}"
    for wbuf, nm, shape in ((wq, "l3a_wqT", [D, D]), (wk, "l3a_wkT", [D, D]),
                            (wv, "l3a_wvT", [D, D]), (wo, "l3a_woT", [D, D])):
        h = decls.dram(em, nm, shape)
        em.dma("gpsimd", wbuf.ap().rearrange("p (c d) -> p c d", c=8),
               h.rearrange("(c p) d -> p c d", p=128), sem=cstG)
    h = decls.dram(em, "l3a_dloc", [128, 16 * C])
    em.dma("sync", dloc[:, :], h[:, :], sem=cstS)
    h = decls.dram(em, "l3a_gp", [64, 16 * C])
    em.dma("sync", gp[0:64, :], h[:, :], sem=cstS)
    h = decls.dram(em, "l3a_gw", [128, 16])
    em.dma("sync", gwc[:, :], h[:, :], sem=cstS)
    h = decls.dram(em, "l3a_gnwb", [128, 16])
    cS = em.dma("sync", gnwb[:, :], h[:, :], sem=cstS)
    em.wait("tensor", cstS, em.cnt[cstS])
    em.wait("tensor", cstG, em.cnt[cstG])
    em.wait("vector", cstS, em.cnt[cstS])

    op = V.memset(st32[0:64, :], 0.0)
    op = V.memset(stbf[0:64, :], 0.0)
    op = V.memset(scols[0:64, :], 0.0)
    st_ready = em.inc(op, "ve")
    em.wait("tensor", "ve", st_ready)

    GAM = [1.0 - 2.0 ** (-5.0 - h) for h in range(NH)]
    ysp_v = ysp.rearrange("(h p) t -> p h t", p=64)

    proj_free = {"qT": 0, "kT": 0, "vtok": 0, "ktok": 0, "PT": 0}
    ybuf_st = [0, 0]
    ln_hist = [0, 0]
    for c in range(NCH):
        if c % (MH // C) == 0:
            em.barrier()
            ln_hist = emit_ln_sweep(em, bufs, xin, lnxT,
                                    MH, (c // (MH // C)) * MH, MH)
            em.wait("tensor", "ve", em.cnt["ve"])
        tok0 = (c * C) % MH
        for dst, wsrc, key in ((qT, wq, "qT"), (kT, wk, "kT")):
            for hh in range(NH):
                idx, ps = poolA.acquire()
                for kc in range(8):
                    op = TE.matmul(
                        ps[0:64, 0:C],
                        wsrc[:, kc * D + hh * HD: kc * D + hh * HD + HD],
                        lnxT[:, kc * MH + tok0: kc * MH + tok0 + C],
                        start=(kc == 0), stop=(kc == 7))
                pe_c = em.inc(op, "pe")
                em.wait("vector", "pe", pe_c)
                em.wait("vector", "pe", proj_free[key])
                op = V.tensor_copy(out=dst[0:64, hh * C:(hh + 1) * C],
                                   in_=ps[0:64, 0:C])
                poolA.release(idx, op, "vector")
        for dst, wsrc, key in ((vtok, wv, "vtok"), (ktok, wk, "ktok")):
            for dn in range(2):
                idx, ps = poolO.acquire()
                for kc in range(8):
                    op = TE.matmul(
                        ps[:, :],
                        lnxT[:, kc * MH + tok0: kc * MH + tok0 + C],
                        wsrc[:, kc * D + dn * 512: kc * D + dn * 512 + 512],
                        start=(kc == 0), stop=(kc == 7))
                pe_c = em.inc(op, "pe")
                em.wait("vector", "pe", pe_c)
                em.wait("vector", "pe", proj_free[key])
                op = V.tensor_copy(out=dst[:, dn * 512:(dn + 1) * 512],
                                   in_=ps[:, :])
                poolO.release(idx, op, "vector")
        for h in range(NH):
            V.tensor_scalar(out=kw[:, h * HD:(h + 1) * HD],
                            in0=ktok[:, h * HD:(h + 1) * HD],
                            scalar1=gwc[:, h:h + 1], scalar2=None,
                            op0=ALU.mult)
        op = V.tensor_tensor(out=qgT[0:64, :], in0=qT[0:64, :],
                             in1=gp[0:64, :], op=ALU.mult)
        prep_done = em.inc(op, "ve")
        em.wait("tensor", "ve", prep_done)

        st_tiles = []
        for hb in range(4):
            idx, ps = poolA.acquire()
            for hh in range(4):
                h = 4 * hb + hh
                op = TE.matmul(
                    ps[:, hh * C:(hh + 1) * C],
                    kT[0:64, h * C:(h + 1) * C],
                    qT[0:64, h * C:(h + 1) * C],
                    start=True, stop=True)
            pe_c = em.inc(op, "pe")
            st_tiles.append((idx, ps, pe_c))
        lastPT = 0
        for hb in range(4):
            idx, ps, pe_c = st_tiles[hb]
            em.wait("vector", "pe", pe_c)
            em.wait("vector", "pe", proj_free["PT"])
            for hh in range(4):
                h = 4 * hb + hh
                op = V.tensor_tensor(out=PT[:, h * C:(h + 1) * C],
                                     in0=ps[:, hh * C:(hh + 1) * C],
                                     in1=dloc[:, h * C:(h + 1) * C],
                                     op=ALU.mult)
            lastPT = poolA.release(idx, op, "vector")
        em.wait("tensor", "ve", lastPT)
        yb = ybuf[c % 2]
        em.wait("scalar", f"ysp{c % 2}", ybuf_st[c % 2])
        ydone = 0
        for h in range(NH):
            idx, ps = poolO.acquire()
            sl = ps[0:64, (h % 4) * C:(h % 4) * C + C]
            op = TE.matmul(sl, vtok[:, h * HD:(h + 1) * HD],
                           PT[:, h * C:(h + 1) * C], start=True, stop=False)
            op = TE.matmul(sl, stbf[0:64, h * HD:(h + 1) * HD],
                           qgT[0:64, h * C:(h + 1) * C],
                           start=False, stop=True)
            pe_c = em.inc(op, "pe")
            em.wait("scalar", "pe", pe_c)
            scol = h * NCH + c
            op = SC.activation(out=yb[0:64, h * C:(h + 1) * C], in_=sl,
                               func=AF.Copy,
                               accum_out=ysum[0:64, scol:scol + 1])
            op = SC.activation(out=yscr[0:64, :], in_=sl, func=AF.Square,
                               accum_out=ysq[0:64, scol:scol + 1])
            ydone = poolO.release(idx, op, "scalar")
        proj_free["qT"] = proj_free["kT"] = em.cnt["pe"]
        ds_tiles = []
        for hb in range(2):
            idx, ps = poolA.acquire()
            for hh in range(8):
                h = 8 * hb + hh
                op = TE.matmul(ps[0:64, hh * HD:(hh + 1) * HD],
                               kw[:, h * HD:(h + 1) * HD],
                               vtok[:, h * HD:(h + 1) * HD],
                               start=True, stop=True)
            pe_c = em.inc(op, "pe")
            ds_tiles.append((idx, ps, pe_c))
        lastS = 0
        for hb in range(2):
            idx, ps, pe_c = ds_tiles[hb]
            em.wait("vector", "pe", pe_c)
            for hh in range(8):
                h = 8 * hb + hh
                gC = GAM[h] ** C
                dsl = st32[0:64, h * HD:(h + 1) * HD]
                op = V.scalar_tensor_tensor(
                    out=dsl, in0=dsl, scalar=gC,
                    in1=ps[0:64, hh * HD:(hh + 1) * HD],
                    op0=ALU.mult, op1=ALU.add)
            lastS = poolA.release(idx, op, "vector")
        em.wait("vector", "ve", lastS)
        op = V.tensor_copy(out=stbf[0:64, :], in_=st32[0:64, :])
        st_ready = em.inc(op, "ve")
        em.wait("tensor", "ve", st_ready)
        proj_free["vtok"] = proj_free["ktok"] = em.cnt["pe"]
        proj_free["PT"] = em.cnt["pe"]
        em.wait("sync", "sc", ydone)
        gtok = c * C
        ybuf_st[c % 2] = em.dma(
            "sync", ysp_v[:, :, gtok:gtok + C],
            yb.ap().rearrange("p (h t) -> p h t", h=16), sem=f"ysp{c % 2}")

    em.wait("vector", "sc", em.cnt["sc"])
    for h in range(NH):
        V.reduce_sum(scols[0:64, h:h + 1], ysum[0:64, h * NCH:(h + 1) * NCH],
                     axis=AX.X)
        op = V.reduce_sum(scols[0:64, 16 + h:17 + h],
                          ysq[0:64, h * NCH:(h + 1) * NCH], axis=AX.X)
    sc_ready = em.inc(op, "ve")
    em.wait("tensor", "ve", sc_ready)
    idx, ps = poolA.acquire()
    op = TE.matmul(ps[0:32, 0:1], scols[0:64, 0:32],
                   bufs["ones_col"][0:64, 0:1], start=True, stop=True)
    pe_c = em.inc(op, "pe")
    em.wait("vector", "pe", pe_c)
    op = V.tensor_copy(out=hstat[0:32, 0:1], in_=ps[0:32, 0:1])
    poolA.release(idx, op, "vector")
    fence(em, "vector")
    NTOT = float(T * HD)
    op = V.tensor_scalar(out=hstat[0:32, 1:2], in0=hstat[0:32, 0:1],
                         scalar1=1.0 / NTOT, scalar2=None, op0=ALU.mult)
    c2 = em.inc(op, "ve")
    em.wait("tensor", "ve", c2)
    idx, ps2 = poolA.acquire()
    op = TE.matmul(ps2[0:1, 0:32], hstat[0:32, 1:2], bufs["eye32"][0:32, 0:32],
                   start=True, stop=True)
    pe_c = em.inc(op, "pe")
    em.wait("vector", "pe", pe_c)
    op = V.tensor_copy(out=hrow[0:1, 0:32], in_=ps2[0:1, 0:32])
    poolA.release(idx, op, "vector")
    fence(em, "vector")
    V.tensor_tensor(out=hrow[0:1, 32:48], in0=hrow[0:1, 0:16],
                    in1=hrow[0:1, 0:16], op=ALU.mult)
    fence(em, "vector")
    op = V.tensor_tensor(out=hrow[0:1, 32:48], in0=hrow[0:1, 16:32],
                         in1=hrow[0:1, 32:48], op=ALU.subtract)
    c2 = em.inc(op, "ve")
    em.wait("scalar", "ve", c2)
    op = SC.activation(out=hrow[0:1, 32:48], in_=hrow[0:1, 32:48],
                       func=AF.Sqrt, bias=bufs["konst"][0:1, 0:1])
    c2 = em.inc(op, "sc")
    em.wait("vector", "sc", c2)
    V.reciprocal(hrow[0:1, 32:48], hrow[0:1, 32:48])
    fence(em, "vector")
    V.tensor_tensor(out=hrow[0:1, 48:64], in0=hrow[0:1, 0:16],
                    in1=hrow[0:1, 32:48], op=ALU.mult)
    fence(em, "vector")
    op = V.tensor_scalar(out=hrow[0:1, 48:64], in0=hrow[0:1, 48:64],
                         scalar1=-1.0, scalar2=None, op0=ALU.mult)
    c2 = em.inc(op, "ve")
    em.wait("tensor", "ve", c2)
    idx, ps3 = poolA.acquire()
    op = TE.matmul(ps3[:, 0:64], bufs["ones32"][0:1, :], hrow[0:1, 0:64],
                   start=True, stop=True)
    pe_c = em.inc(op, "pe")
    em.wait("vector", "pe", pe_c)
    op = V.tensor_copy(out=hrep[:, :], in_=ps3[:, 0:64])
    poolA.release(idx, op, "vector")
    fence(em, "vector")
    for t8 in range(8):
        V.tensor_copy(out=abcol.ap()[0:64, t8:t8 + 1],
                      in_=hrep.ap()[0:64, 32 + 2 * t8:33 + 2 * t8])
        V.tensor_copy(out=abcol.ap()[64:128, t8:t8 + 1],
                      in_=hrep.ap()[64:128, 33 + 2 * t8:34 + 2 * t8])
        V.tensor_copy(out=abcol.ap()[0:64, 8 + t8:9 + t8],
                      in_=hrep.ap()[0:64, 48 + 2 * t8:49 + 2 * t8])
        V.tensor_copy(out=abcol.ap()[64:128, 8 + t8:9 + t8],
                      in_=hrep.ap()[64:128, 49 + 2 * t8:50 + 2 * t8])
    fence(em, "vector")
    V.tensor_tensor(out=abcol[:, 16:24], in0=abcol[:, 0:8],
                    in1=gnwb[:, 0:8], op=ALU.mult)
    V.tensor_tensor(out=abcol[:, 24:32], in0=abcol[:, 8:16],
                    in1=gnwb[:, 0:8], op=ALU.mult)
    fence(em, "vector")
    op = V.tensor_tensor(out=abcol[:, 24:32], in0=abcol[:, 24:32],
                         in1=gnwb[:, 8:16], op=ALU.add)
    c2 = em.inc(op, "ve")
    em.wait("scalar", "ve", c2)

    xo_store = [0, 0]
    xt_used = [0, 0]
    ynS_used = [0, 0]
    for tsl in range(T // 512):
        for t8 in range(8):
            sb = t8 % 2
            em.wait("sync", "sc", ynS_used[sb])
            cdma = em.dma("sync", ynS[sb][:, :],
                          ysp[t8 * 128:(t8 + 1) * 128,
                              tsl * 512:(tsl + 1) * 512],
                          sem=f"yn{sb}")
            em.wait("scalar", f"yn{sb}", cdma)
            op = SC.activation(out=ynT[:, t8 * 512:(t8 + 1) * 512],
                               in_=ynS[sb][:, :],
                               func=AF.Identity,
                               scale=abcol[:, 16 + t8:17 + t8],
                               bias=abcol[:, 24 + t8:25 + t8])
            ynS_used[sb] = em.inc(op, "sc")
        em.wait("tensor", "sc", em.cnt["sc"])
        for tt in range(4):
            tg = tsl * 4 + tt
            b = tg % 2
            em.wait("sync", "ve", xt_used[b])
            em.wait("sync", "sc", ln_hist[b])
            c2 = em.dma("sync", bufs["xt"][b][:, :],
                        xin[tg * 128:(tg + 1) * 128, :], sem=f"xa{b}")
            ve_c = 0
            for dn in range(2):
                idx, ps = poolO.acquire()
                for ic in range(8):
                    op = TE.matmul(
                        ps[:, :],
                        ynT[:, ic * 512 + tt * 128: ic * 512 + tt * 128 + 128],
                        wo[:, ic * D + dn * 512: ic * D + dn * 512 + 512],
                        start=(ic == 0), stop=(ic == 7))
                pe_c = em.inc(op, "pe")
                em.wait("vector", "pe", pe_c)
                if dn == 0:
                    em.wait("vector", f"xa{b}", c2)
                    em.wait("vector", f"xs{b}", xo_store[b])
                op = V.tensor_tensor(
                    out=xo[b][:, dn * 512:(dn + 1) * 512], in0=ps[:, :],
                    in1=bufs["xt"][b][:, dn * 512:(dn + 1) * 512], op=ALU.add)
                ve_c = poolO.release(idx, op, "vector")
            xt_used[b] = ve_c
            em.wait("sync", "ve", ve_c)
            xo_store[b] = em.dma("sync", xout[tg * 128:(tg + 1) * 128, :],
                                 xo[b][:, :], sem=f"xs{b}")
        em.wait("sync", "pe", em.cnt["pe"])


def l3_attn_host(inp):
    a = {}
    a["l3a_wqT"] = np.ascontiguousarray(inp["L3_attn_W_q"].T)
    a["l3a_wkT"] = np.ascontiguousarray(inp["L3_attn_W_k"].T)
    a["l3a_wvT"] = np.ascontiguousarray(inp["L3_attn_W_v"].T)
    a["l3a_woT"] = np.ascontiguousarray(inp["L3_attn_W_o"].T)
    C, NH = 128, 16
    gam = (1.0 - 2.0 ** (-5.0 - np.arange(NH))).astype(np.float64)
    m = np.arange(C)[:, None]
    l = np.arange(C)[None, :]
    dloc = np.zeros((C, NH * C), np.float32)
    gp = np.zeros((64, NH * C), np.float32)
    gw = np.zeros((C, NH), np.float32)
    for h in range(NH):
        Dm = np.where(l >= m, gam[h] ** (l - m), 0.0) / 8.0
        dloc[:, h * C:(h + 1) * C] = Dm.astype(np.float32)
        gp[:, h * C:(h + 1) * C] = np.tile(
            (gam[h] ** (np.arange(C) + 1) / 8.0).astype(np.float32), (64, 1))
        gw[:, h] = (gam[h] ** (C - 1 - np.arange(C))).astype(np.float32)
    a["l3a_dloc"] = dloc
    a["l3a_gp"] = gp
    a["l3a_gw"] = gw
    gnwb = np.zeros((128, 16), np.float32)
    gnwb[:, 0:8] = inp["L3_attn_gn_weight"].reshape(8, 128).T
    gnwb[:, 8:16] = inp["L3_attn_gn_bias"].reshape(8, 128).T
    a["l3a_gnwb"] = gnwb
    return a


ATTN_SETUP["l3a"] = setup_l3_attn
ATTN_HOST["l3a"] = l3_attn_host


def setup_l1_attn(em, bufs, xin, xout, decls):
    nc = em.nc
    name = "l1a"
    C = 256
    NCH = T // C
    V = em.on("vector")
    SC = em.on("scalar")
    TE = em.on("tensor")
    poolA, poolO = bufs["poolA"], bufs["poolO"]

    MH = T // 4
    CPQ = MH // C
    NTQ = MH // 128
    lnxT = em.salloc(f"{name}_lnxT", [128, 8 * MH], BF16)
    lnxtok = em.salloc(f"{name}_lnxtok", [128, NTQ * D], BF16)
    wwT = em.salloc(f"{name}_wwTs", [128, 8 * D], BF16)
    wrT = em.salloc(f"{name}_wrTs", [128, 8 * D], BF16)
    WT32 = em.salloc(f"{name}_WT32", [128, 8 * D], F32)
    WTbf = em.salloc(f"{name}_WTbf", [128, 8 * D], BF16)
    vtok = em.salloc(f"{name}_vtok", [128, 2 * D], BF16)
    vw = em.salloc(f"{name}_vw", [128, 2 * D], BF16)
    wkTb = em.salloc(f"{name}_wkTb", [128, 8 * C], BF16)
    wktok = em.salloc(f"{name}_wktok", [128, 2 * D], BF16)
    SMt = em.salloc(f"{name}_SMt", [128, 2 * C], BF16)
    MT = em.salloc(f"{name}_MT", [128, 2 * C], F32)
    DIFFS = em.salloc(f"{name}_DIFFSs", [128, 2 * C], F32)
    CAUS = em.salloc(f"{name}_CAUSs", [128, 2 * C], F32)
    readsS = em.salloc(f"{name}_readsS", [128, 2 * D], F32)
    readsT = em.salloc(f"{name}_readsT", [128, 8 * C], BF16)
    savedc = em.salloc(f"{name}_savedc", [128, 8], BF16)
    iot = em.salloc(f"{name}_iots", [128, 4], F32)
    srow = em.salloc(f"{name}_srow", [1, 2], F32)
    cols = em.salloc(f"{name}_cols", [128, 8], F32)
    xo = [em.salloc(f"{name}_xo{i}", [128, D], F32) for i in range(2)]

    cstS, cstG = f"cst_{name}", f"cstg_{name}"
    h = decls.dram(em, "l1a_wwT", [D, D])
    em.dma("gpsimd", wwT.ap().rearrange("p (c d) -> p c d", c=8),
           h.rearrange("(c p) d -> p c d", p=128), sem=cstG)
    h = decls.dram(em, "l1a_wrT", [D, D])
    em.dma("gpsimd", wrT.ap().rearrange("p (c d) -> p c d", c=8),
           h.rearrange("(c p) d -> p c d", p=128), sem=cstG)
    h = decls.dram(em, "l1a_diffs", [128, 2 * C])
    em.dma("sync", DIFFS[:, :], h[:, :], sem=cstS)
    h = decls.dram(em, "l1a_caus", [128, 2 * C])
    em.dma("sync", CAUS[:, :], h[:, :], sem=cstS)
    h = decls.dram(em, "l1a_iota", [128, 4])
    em.dma("sync", iot[:, :], h[:, :], sem=cstS)
    h = decls.dram(em, "l1a_scal", [1, 2])
    cS = em.dma("sync", srow[0:1, :], h[0:1, :], sem=cstS)
    em.wait("tensor", cstS, em.cnt[cstS])
    em.wait("tensor", cstG, em.cnt[cstG])
    em.wait("scalar", cstS, em.cnt[cstS])
    em.wait("vector", cstS, em.cnt[cstS])

    op = SC.activation(out=srow[0:1, 0:1], in_=srow[0:1, 0:1], func=AF.Sigmoid)
    c = em.inc(op, "sc")
    em.wait("scalar", "sc", c)
    op = SC.activation(out=srow[0:1, 0:1], in_=srow[0:1, 0:1], func=AF.Ln)
    op = SC.activation(out=srow[0:1, 1:2], in_=srow[0:1, 1:2], func=AF.Exp)
    c = em.inc(op, "sc")
    em.wait("tensor", "sc", c)
    em.wait("tensor", "ve", em.cnt["ve"])
    idx, ps = poolA.acquire()
    op = TE.matmul(ps[:, 0:2], bufs["ones32"][0:1, :], srow[0:1, 0:2],
                   start=True, stop=True)
    pe_c = em.inc(op, "pe")
    em.wait("vector", "pe", pe_c)
    op = V.tensor_copy(out=cols[:, 0:2], in_=ps[:, 0:2])
    poolA.release(idx, op, "vector")
    fence(em, "vector")
    for j in range(4):
        V.tensor_tensor(out=cols[:, 2 + j:3 + j], in0=iot[:, j:j + 1],
                        in1=cols[:, 0:1], op=ALU.mult)
    op = V.tensor_scalar(out=cols[:, 6:7], in0=cols[:, 0:1],
                         scalar1=float(C), scalar2=None, op0=ALU.mult)
    c = em.inc(op, "ve")
    em.wait("scalar", "ve", c)
    op = SC.activation(out=cols[:, 2:7], in_=cols[:, 2:7], func=AF.Exp)
    c = em.inc(op, "sc")
    em.wait("vector", "sc", c)
    op = V.tensor_scalar(out=MT[:, :], in0=DIFFS[:, :], scalar1=cols[:, 0:1],
                         scalar2=None, op0=ALU.mult)
    c = em.inc(op, "ve")
    em.wait("scalar", "ve", c)
    op = SC.activation(out=MT[:, :], in_=MT[:, :], func=AF.Exp)
    c = em.inc(op, "sc")
    em.wait("vector", "sc", c)
    op = V.tensor_tensor(out=MT[:, :], in0=MT[:, :], in1=CAUS[:, :],
                         op=ALU.mult)
    op = V.memset(WT32[:, :], 0.0)
    op = V.memset(WTbf[:, :], 0.0)
    op = V.memset(savedc[:, :], 0.0)
    st_ready = em.inc(op, "ve")
    em.wait("tensor", "ve", st_ready)

    free = {"vtok": 0, "vw": 0, "wkTb": 0, "wktok": 0, "SMt": 0, "readsS": 0,
            "readsT": 0, "lnxT_used": 0}
    ln_hist = [0, 0]
    xo_store = [0, 0]
    xt_used = [0, 0]
    for ch in range(NCH):
        if ch % CPQ == 0:
            if ch > 0:
                em.wait("vector", "pe", em.cnt["pe"])
                for c8 in range(8):
                    op = V.tensor_copy(
                        out=savedc[:, c8:c8 + 1],
                        in_=lnxT[:, c8 * MH + MH - 1: c8 * MH + MH])
                em.inc(op, "ve")
            em.barrier()
            ln_hist = emit_ln_sweep(em, bufs, xin, lnxT, MH,
                                    (ch // CPQ) * MH, MH,
                                    with_tokmajor=lnxtok)
            em.wait("tensor", "ve", em.cnt["ve"])
        tok0 = (ch * C) % MH
        gt0 = ch * C
        em.wait("vector", "pe", free["wkTb"])
        for c8 in range(8):
            if tok0 == 0:
                V.tensor_copy(out=wkTb[:, c8 * C:c8 * C + 1],
                              in_=savedc[:, c8:c8 + 1])
                op = V.tensor_copy(
                    out=wkTb[:, c8 * C + 1:(c8 + 1) * C],
                    in_=lnxT[:, c8 * MH: c8 * MH + C - 1])
            else:
                op = V.tensor_copy(
                    out=wkTb[:, c8 * C:(c8 + 1) * C],
                    in_=lnxT[:, c8 * MH + tok0 - 1: c8 * MH + tok0 + C - 1])
        wk_ready = em.inc(op, "ve")
        em.wait("tensor", "ve", wk_ready)
        for t2 in range(2):
            for dn in range(2):
                idx, ps = poolO.acquire()
                for kc in range(8):
                    op = TE.matmul(
                        ps[:, :],
                        lnxT[:, kc * MH + tok0 + t2 * 128:
                             kc * MH + tok0 + t2 * 128 + 128],
                        wwT[:, kc * D + dn * 512: kc * D + dn * 512 + 512],
                        start=(kc == 0), stop=(kc == 7))
                pe_c = em.inc(op, "pe")
                em.wait("vector", "pe", pe_c)
                em.wait("vector", "pe", free["vtok"])
                op = V.tensor_copy(out=vtok[:, t2 * D + dn * 512:
                                            t2 * D + dn * 512 + 512],
                                   in_=ps[:, :])
                poolO.release(idx, op, "vector")
        for t2 in range(2):
            op = V.tensor_scalar(out=vw[:, t2 * D:(t2 + 1) * D],
                                 in0=vtok[:, t2 * D:(t2 + 1) * D],
                                 scalar1=cols[:, 4 + t2:5 + t2], scalar2=None,
                                 op0=ALU.mult)
        vw_ready = em.inc(op, "ve")
        em.wait("tensor", "ve", wk_ready)
        last = 0
        for c8 in range(8):
            for t2 in range(2):
                idx, pt = bufs["tp_pool"].acquire()
                op = TE.transpose(pt[:, 0:128],
                                  wkTb[:, c8 * C + t2 * 128:
                                       c8 * C + t2 * 128 + 128],
                                  bufs["eye"][:, :])
                pe_c = em.inc(op, "pe")
                em.wait("vector", "pe", pe_c)
                em.wait("vector", "pe", free["wktok"])
                op = V.tensor_copy(
                    out=wktok[:, t2 * D + c8 * 128: t2 * D + c8 * 128 + 128],
                    in_=pt[:, 0:128])
                last = bufs["tp_pool"].release(idx, op, "vector")
        em.wait("tensor", "ve", last)
        sm_tiles = []
        for mt in range(2):
            idx, ps = poolA.acquire()
            for kc in range(8):
                op = TE.matmul(
                    ps[:, 0:C],
                    wkTb[:, kc * C + mt * 128: kc * C + mt * 128 + 128],
                    lnxT[:, kc * MH + tok0: kc * MH + tok0 + C],
                    start=(kc == 0), stop=(kc == 7))
            pe_c = em.inc(op, "pe")
            sm_tiles.append((idx, ps, pe_c))
        lastSM = 0
        for mt in range(2):
            idx, ps, pe_c = sm_tiles[mt]
            em.wait("vector", "pe", pe_c)
            em.wait("vector", "pe", free["SMt"])
            op = V.tensor_tensor(out=SMt[:, mt * C:(mt + 1) * C],
                                 in0=ps[:, 0:C], in1=MT[:, mt * C:(mt + 1) * C],
                                 op=ALU.mult)
            lastSM = poolA.release(idx, op, "vector")
        em.wait("tensor", "ve", lastSM)
        em.wait("vector", "pe", free["readsS"])
        for t2 in range(2):
            for dn in range(2):
                idxI, psI = poolA.acquire()
                for kc in range(8):
                    op = TE.matmul(
                        psI[:, :],
                        lnxT[:, kc * MH + tok0 + t2 * 128:
                             kc * MH + tok0 + t2 * 128 + 128],
                        WTbf[:, kc * D + dn * 512: kc * D + dn * 512 + 512],
                        start=(kc == 0), stop=(kc == 7))
                pe_i = em.inc(op, "pe")
                idxJ, psJ = poolO.acquire()
                for mt in range(2):
                    op = TE.matmul(
                        psJ[:, :],
                        SMt[:, mt * C + t2 * 128: mt * C + t2 * 128 + 128],
                        vtok[:, mt * D + dn * 512: mt * D + dn * 512 + 512],
                        start=(mt == 0), stop=(mt == 1))
                pe_j = em.inc(op, "pe")
                em.wait("vector", "pe", pe_j)
                sl = readsS[:, t2 * D + dn * 512: t2 * D + dn * 512 + 512]
                op = V.tensor_scalar(out=sl, in0=psI[:, :],
                                     scalar1=cols[:, 2 + t2:3 + t2],
                                     scalar2=None, op0=ALU.mult)
                poolA.release(idxI, op, "vector")
                fence(em, "vector")
                op = V.tensor_tensor(out=sl, in0=psJ[:, :], in1=sl, op=ALU.add)
                poolO.release(idxJ, op, "vector")
        fence(em, "vector")
        em.wait("tensor", "ve", em.cnt["ve"])
        last = 0
        for t2 in range(2):
            for c8 in range(8):
                idx, pt = poolA.acquire()
                op = TE.transpose(pt[:, 0:128],
                                  readsS[:, t2 * D + c8 * 128:
                                         t2 * D + c8 * 128 + 128],
                                  bufs["eye32"][:, :])
                pe_c = em.inc(op, "pe")
                em.wait("vector", "pe", pe_c)
                em.wait("vector", "pe", free["readsT"])
                op = V.tensor_copy(
                    out=readsT[:, c8 * C + t2 * 128: c8 * C + t2 * 128 + 128],
                    in_=pt[:, 0:128])
                last = poolA.release(idx, op, "vector")
        em.wait("tensor", "ve", last)
        for t2 in range(2):
            tg = (gt0 // 128) + t2
            b = tg % 2
            em.wait("sync", "ve", xt_used[b])
            em.wait("sync", "sc", ln_hist[b])
            cdma = em.dma("sync", bufs["xt"][b][:, :],
                          xin[tg * 128:(tg + 1) * 128, :], sem=f"xa{b}")
            ve_c = 0
            for dn in range(2):
                idx, ps = poolO.acquire()
                for c8 in range(8):
                    op = TE.matmul(
                        ps[:, :],
                        readsT[:, c8 * C + t2 * 128: c8 * C + t2 * 128 + 128],
                        wrT[:, c8 * D + dn * 512: c8 * D + dn * 512 + 512],
                        start=(c8 == 0), stop=(c8 == 7))
                pe_c = em.inc(op, "pe")
                em.wait("vector", "pe", pe_c)
                if dn == 0:
                    em.wait("vector", f"xa{b}", cdma)
                    em.wait("vector", f"xs{b}", xo_store[b])
                sl = xo[b][:, dn * 512:(dn + 1) * 512]
                op = V.scalar_tensor_tensor(
                    out=sl, in0=ps[:, :], scalar=cols[:, 1:2],
                    in1=bufs["xt"][b][:, dn * 512:(dn + 1) * 512],
                    op0=ALU.mult, op1=ALU.add)
                poolO.release(idx, op, "vector")
                fence(em, "vector")
                lt = (tok0 // 128) + t2
                op = V.tensor_tensor(
                    out=sl, in0=lnxtok[:, lt * D + dn * 512:
                                       lt * D + dn * 512 + 512],
                    in1=sl, op=ALU.add)
                ve_c = em.inc(op, "ve")
            xt_used[b] = ve_c
            em.wait("sync", "ve", ve_c)
            xo_store[b] = em.dma("sync", xout[tg * 128:(tg + 1) * 128, :],
                                 xo[b][:, :], sem=f"xs{b}")
        free["readsS"] = free["readsT"] = em.cnt["pe"]
        free["SMt"] = em.cnt["pe"]
        em.wait("tensor", "ve", vw_ready)
        for jc in range(8):
            for dn in range(2):
                idx, ps = poolA.acquire()
                for t2 in range(2):
                    op = TE.matmul(
                        ps[:, :],
                        wktok[:, t2 * D + jc * 128: t2 * D + jc * 128 + 128],
                        vw[:, t2 * D + dn * 512: t2 * D + dn * 512 + 512],
                        start=(t2 == 0), stop=(t2 == 1))
                pe_c = em.inc(op, "pe")
                em.wait("vector", "pe", pe_c)
                sl = WT32[:, jc * D + dn * 512: jc * D + dn * 512 + 512]
                op = V.scalar_tensor_tensor(
                    out=sl, in0=sl, scalar=cols[:, 6:7], in1=ps[:, :],
                    op0=ALU.mult, op1=ALU.add)
                poolA.release(idx, op, "vector")
        fence(em, "vector")
        op = V.tensor_copy(out=WTbf[:, :], in_=WT32[:, :])
        wt_ready = em.inc(op, "ve")
        em.wait("tensor", "ve", wt_ready)
        free["vtok"] = free["vw"] = em.cnt["pe"]
        free["wkTb"] = free["wktok"] = em.cnt["pe"]


def l1_attn_host(inp):
    a = {}
    a["l1a_wwT"] = np.ascontiguousarray(inp["L1_attn_W_write"].T)
    a["l1a_wrT"] = np.ascontiguousarray(inp["L1_attn_W_read"].T)
    C = 256
    m = np.arange(C)
    diffs = np.zeros((128, 2 * C), np.float32)
    caus = np.zeros((128, 2 * C), np.float32)
    for mt in range(2):
        mm = (mt * 128 + np.arange(128))[:, None]
        ll = np.arange(C)[None, :]
        diffs[:, mt * C:(mt + 1) * C] = np.clip(ll - 1 - mm, 0, None)
        caus[:, mt * C:(mt + 1) * C] = (ll > mm).astype(np.float32)
    a["l1a_diffs"] = diffs
    a["l1a_caus"] = caus
    iota = np.zeros((128, 4), np.float32)
    iota[:, 0] = np.arange(128)
    iota[:, 1] = 128 + np.arange(128)
    iota[:, 2] = C - 1 - np.arange(128)
    iota[:, 3] = C - 1 - (128 + np.arange(128))
    a["l1a_iota"] = iota
    a["l1a_scal"] = np.array([[float(inp["L1_attn_decay"]),
                               float(inp["L1_attn_log_alpha"])]], np.float32)
    return a


ATTN_SETUP["l1a"] = setup_l1_attn
ATTN_HOST["l1a"] = l1_attn_host


def setup_l2_attn(em, bufs, xin, xout, decls):
    nc = em.nc
    name = "l2a"
    NH, HD = 16, 64
    V = em.on("vector")
    SC = em.on("scalar")
    TE = em.on("tensor")
    poolA, poolO = bufs["poolA"], bufs["poolO"]
    tp = bufs["tp_pool"]

    MH = T // 8
    NTQ = MH // 128
    lnxT = em.salloc(f"{name}_lnxT", [128, 8 * MH], BF16)
    wqv = em.salloc(f"{name}_wqs", [128, 8 * D], BF16)
    wkv = em.salloc(f"{name}_wkvs", [128, 8 * 2 * D], BF16)
    wov = em.salloc(f"{name}_wos", [128, 8 * D], BF16)
    wbil = em.salloc(f"{name}_wbil", [64, 16 * HD], BF16)
    costb = [em.salloc(f"{name}_coss{i}", [128, D], BF16) for i in range(2)]
    sintb = [em.salloc(f"{name}_sins{i}", [128, D], BF16) for i in range(2)]
    pf32 = [em.salloc(f"{name}_pf32_{i}", [128, D], F32) for i in range(2)]
    prot = em.salloc(f"{name}_prot", [128, D], F32)
    pt1 = em.salloc(f"{name}_pt1", [128, D], F32)
    pbf = [em.salloc(f"{name}_pbf{i}", [128, D], BF16) for i in range(2)]
    pbfv = em.salloc(f"{name}_pbfv", [128, D], BF16)
    kTh = em.salloc(f"{name}_kTh", [64, T], BF16)
    ktTh = em.salloc(f"{name}_ktTh", [64, T], BF16)
    qTh = em.salloc(f"{name}_qTh", [64, 512], BF16)
    vh = em.salloc(f"{name}_vh", [128, (T // 128) * HD], BF16)
    Pbuf = em.salloc(f"{name}_P", [128, T], BF16)
    PTb = em.salloc(f"{name}_PT", [128, T], BF16)
    mx = em.salloc(f"{name}_mx", [128, 32], F32)
    oth = em.salloc(f"{name}_oth", [64, 512], BF16)
    orow = em.salloc(f"{name}_orow", [1, 512], F32)
    orep = em.salloc(f"{name}_orep", [128, 512], F32)
    ynT = em.salloc(f"{name}_ynT", [128, 8 * 512], BF16)
    ynS = [em.salloc(f"{name}_ynS{i}", [128, 512], BF16) for i in range(2)]
    xo = [em.salloc(f"{name}_xo{i}", [128, D], F32) for i in range(2)]
    qsp = nc.dram_tensor(f"{name}_QSP", [D, T], BF16, kind="Internal").ap()
    ksp = nc.dram_tensor(f"{name}_KSP", [D, T], BF16, kind="Internal").ap()
    spT = em.salloc(f"{name}_spT", [128, 8 * 128], BF16)
    vsp = nc.dram_tensor(f"{name}_VSP", [T, D], BF16, kind="Internal").ap()
    osp = nc.dram_tensor(f"{name}_OSP", [D, T], BF16, kind="Internal").ap()

    cstS, cstG = f"cst_{name}", f"cstg_{name}"
    h = decls.dram(em, "l2a_wq", [D, D])
    em.dma("gpsimd", wqv.ap().rearrange("p (c d) -> p c d", c=8),
           h.rearrange("(c p) d -> p c d", p=128), sem=cstG)
    h = decls.dram(em, "l2a_wkv", [D, 2 * D])
    em.dma("gpsimd", wkv.ap().rearrange("p (c d) -> p c d", c=8),
           h.rearrange("(c p) d -> p c d", p=128), sem=cstG)
    h = decls.dram(em, "l2a_wo", [D, D])
    em.dma("gpsimd", wov.ap().rearrange("p (c d) -> p c d", c=8),
           h.rearrange("(c p) d -> p c d", p=128), sem=cstG)
    h = decls.dram(em, "l2a_wbilT", [64, 16 * HD])
    cG = em.dma("gpsimd", wbil[0:64, :], h[:, :], sem=cstG)
    gamc = em.salloc(f"{name}_gamc", [128, 8], F32)
    h = decls.dram(em, "l2a_gamcol", [128, 8])
    cS0 = em.dma("sync", gamc[:, :], h[:, :], sem=cstS)
    cosd = decls.dram(em, "l2a_cos", [T, D])
    sind = decls.dram(em, "l2a_sin", [T, D])
    em.wait("tensor", cstG, em.cnt[cstG])
    em.wait("vector", cstG, em.cnt[cstG])
    em.wait("vector", cstS, em.cnt[cstS])
    for kc in range(8):
        V.tensor_scalar(out=wqv[:, kc * D:(kc + 1) * D],
                        in0=wqv[:, kc * D:(kc + 1) * D],
                        scalar1=gamc[:, kc:kc + 1], scalar2=0.125,
                        op0=ALU.mult, op1=ALU.mult)
        V.tensor_scalar(out=wkv[:, kc * 2 * D:(kc + 1) * 2 * D],
                        in0=wkv[:, kc * 2 * D:(kc + 1) * 2 * D],
                        scalar1=gamc[:, kc:kc + 1], scalar2=None, op0=ALU.mult)
    op = em.rec["vector"].ops[-1][3]
    gam_done = em.inc(op, "ve")
    em.wait("tensor", "ve", gam_done)

    ln_hist = [0, 0]
    tab_used = [0, 0]
    pq_used = [0, 0]
    spills = {"q": [0, 0], "k": [0, 0], "v": [0, 0]}
    bil_pe = 0
    smm_pe = 0
    for ti in range(NTILE):
        if ti % NTQ == 0:
            em.barrier()
            ln_hist = emit_ln_sweep(em, bufs, xin, lnxT, MH,
                                    (ti // NTQ) * MH, MH, double_ln=True)
            em.wait("tensor", "ve", em.cnt["ve"])
        lt = ti % NTQ
        tb = ti % 2
        t0 = ti * 128
        em.wait("gpsimd", "ve", tab_used[tb])
        ctab = em.dma("gpsimd", costb[tb][:, :], cosd[t0:t0 + 128, :], sem=f"ct{tb}")
        ctab = em.dma("gpsimd", sintb[tb][:, :], sind[t0:t0 + 128, :], sem=f"ct{tb}")
        for which, wsrc, wc0, ctab_c, ctab_s, dosp, spname in (
                ("q", wqv, 0, costb[tb], sintb[tb], qsp, "q"),
                ("k", wkv, 0, costb[tb], sintb[tb], ksp, "k"),
                ("v", wkv, D, None, None, vsp, "v")):
            pfb = pf32[0] if which != "k" else pf32[1]
            evs = []
            for dn in range(2):
                idx, ps = poolO.acquire()
                for kc in range(8):
                    op = TE.matmul(
                        ps[:, :],
                        lnxT[:, kc * MH + lt * 128: kc * MH + lt * 128 + 128],
                        wsrc[:, kc * (2 * D if which != "q" else D) + wc0 +
                             dn * 512: kc * (2 * D if which != "q" else D) +
                             wc0 + dn * 512 + 512],
                        start=(kc == 0), stop=(kc == 7))
                pe_c = em.inc(op, "pe")
                em.wait("vector", "pe", pe_c)
                if which == "v":
                    if dn == 0:
                        em.wait("vector", "sp_v", spills["v"][tb])
                    op = V.tensor_copy(out=pbfv[:, dn * 512:(dn + 1) * 512],
                                       in_=ps[:, :])
                else:
                    op = V.tensor_copy(out=pfb[:, dn * 512:(dn + 1) * 512],
                                       in_=ps[:, :])
                evs.append(poolO.release(idx, op, "vector"))
            if which == "v":
                em.wait("sync", "ve", evs[-1])
                spills["v"][tb] = em.dma("sync", vsp[t0:t0 + 128, :],
                                         pbfv[:, :], sem="sp_v")
                continue
            em.wait("vector", f"ct{tb}", ctab)
            V.tensor_scalar(out=prot[:, 1:D:2], in0=pfb[:, 0:D:2],
                            scalar1=1.0, scalar2=None, op0=ALU.mult)
            op = V.tensor_scalar(out=prot[:, 0:D:2], in0=pfb[:, 1:D:2],
                                 scalar1=-1.0, scalar2=None, op0=ALU.mult)
            c = em.inc(op, "ve"); em.wait("vector", "ve", c)
            V.tensor_tensor(out=pt1[:, :], in0=pfb[:, :], in1=ctab_c[:, :],
                            op=ALU.mult)
            op = V.tensor_tensor(out=prot[:, :], in0=prot[:, :],
                                 in1=ctab_s[:, :], op=ALU.mult)
            c = em.inc(op, "ve"); em.wait("vector", "ve", c)
            bfo = pbf[0] if which == "q" else pbf[1]
            op = V.tensor_tensor(out=bfo[:, :], in0=pt1[:, :], in1=prot[:, :],
                                 op=ALU.add)
            c = em.inc(op, "ve")
            em.wait("tensor", "ve", c)
            lastev = 0
            for c8 in range(8):
                idx, pt_ = tp.acquire()
                op = TE.transpose(pt_[:, 0:128],
                                  bfo[:, c8 * 128:(c8 + 1) * 128],
                                  bufs["eye"][:, :])
                pe_c = em.inc(op, "pe")
                em.wait("vector", "pe", pe_c)
                if c8 == 0:
                    em.wait("vector", f"sp_{spname}", spills[spname][tb])
                op = V.tensor_copy(out=spT[:, c8 * 128:(c8 + 1) * 128],
                                   in_=pt_[:, 0:128])
                lastev = tp.release(idx, op, "vector")
            em.wait("sync", "ve", lastev)
            spills[spname][tb] = em.dma(
                "sync",
                dosp.rearrange("(c p) t -> p c t", p=128)[:, :, t0:t0 + 128],
                spT.ap().rearrange("p (c t) -> p c t", c=8),
                sem=f"sp_{spname}")
        tab_used[tb] = em.cnt["ve"]
    em.barrier()

    NQM = T // 512
    NKS = T // 512
    NKT = T // 128
    free = {"kTh": 0, "ktTh": 0, "vh": 0, "qTh": 0, "P": 0, "PT": 0,
            "oth": 0}
    osp_st = 0
    for hh in range(NH):
        em.wait("sync", "pe", bil_pe)
        c1 = em.dma("sync", kTh[0:64, :], ksp[hh * HD:(hh + 1) * HD, :],
                    sem="ld_k")
        em.wait("scalar", "pe", free["vh"])
        c2 = em.dma("scalar", vh[:, :],
                    vsp.rearrange("(n p) d -> p n d", p=128)[:, :, hh * HD:(hh + 1) * HD],
                    sem="ld_v")
        em.wait("tensor", "ld_k", c1)
        em.wait("tensor", "ld_v", c2)
        for ks in range(T // 512):
            idx, ps = poolA.acquire()
            op = TE.matmul(ps[0:64, :], wbil[0:64, hh * HD:(hh + 1) * HD],
                           kTh[0:64, ks * 512:(ks + 1) * 512],
                           start=True, stop=True)
            pe_c = em.inc(op, "pe")
            bil_pe = pe_c
            em.wait("vector", "pe", pe_c)
            if ks == 0:
                em.wait("vector", "pe", free["ktTh"])
            op = V.tensor_copy(out=ktTh[0:64, ks * 512:(ks + 1) * 512],
                               in_=ps[0:64, :])
            free["kTh"] = poolA.release(idx, op, "vector")
        em.wait("tensor", "ve", free["kTh"])
        for qm in range(NQM):
            q0 = qm * 512
            em.wait("sync", "pe", free["qTh"])
            cq = em.dma("sync", qTh[0:64, :],
                        qsp[hh * HD:(hh + 1) * HD, q0:q0 + 512], sem="ld_q")
            em.wait("tensor", "ld_q", cq)
            negm_cs = [0, 0]

            def stage1(qs):
                so = (qs % 2) * 16
                for ks in range(NKS):
                    idx, ps = poolA.acquire() if ks % 2 == 0 else poolO.acquire()
                    pool_ = poolA if ks % 2 == 0 else poolO
                    op = TE.matmul(ps[:, :], qTh[0:64, qs * 128:(qs + 1) * 128],
                                   ktTh[0:64, ks * 512:(ks + 1) * 512],
                                   start=True, stop=True)
                    pe_c = em.inc(op, "pe")
                    em.wait("vector", "pe", pe_c)
                    op = V.reduce_max(mx[:, so + ks:so + ks + 1], ps[:, :],
                                      axis=AX.X)
                    pool_.release(idx, op, "vector")
                fence(em, "vector")
                op = V.reduce_max(mx[:, so + 12:so + 13], mx[:, so:so + NKS],
                                  axis=AX.X)
                c = em.inc(op, "ve"); em.wait("vector", "ve", c)
                op = V.tensor_scalar(out=mx[:, so + 9:so + 10],
                                     in0=mx[:, so + 12:so + 13],
                                     scalar1=-1.0, scalar2=None, op0=ALU.mult)
                negm_cs[qs % 2] = em.inc(op, "ve")

            def stage2(qs):
                so = (qs % 2) * 16
                em.wait("scalar", "ve", negm_cs[qs % 2])
                for ks in range(NKS):
                    idx, ps = poolA.acquire() if ks % 2 == 0 else poolO.acquire()
                    pool_ = poolA if ks % 2 == 0 else poolO
                    op = TE.matmul(ps[:, :], qTh[0:64, qs * 128:(qs + 1) * 128],
                                   ktTh[0:64, ks * 512:(ks + 1) * 512],
                                   start=True, stop=True)
                    pe_c = em.inc(op, "pe")
                    em.wait("scalar", "pe", pe_c)
                    em.wait("scalar", "ve", free["P"])
                    op = SC.activation(out=Pbuf[:, ks * 512:(ks + 1) * 512],
                                       in_=ps[:, :], func=AF.Exp,
                                       bias=mx[:, so + 9:so + 10],
                                       accum_out=mx[:, so + ks:so + ks + 1])
                    pool_.release(idx, op, "scalar")
                em.wait("vector", "sc", em.cnt["sc"])
                op = V.reduce_sum(mx[:, so + 10:so + 11], mx[:, so:so + NKS],
                                  axis=AX.X)
                c = em.inc(op, "ve"); em.wait("vector", "ve", c)
                op = V.reciprocal(mx[:, so + 11:so + 12], mx[:, so + 10:so + 11])
                inv_c = em.inc(op, "ve")
                em.wait("tensor", "sc", em.cnt["sc"])
                nev = 0
                for kt8 in range(NKT // 4):
                    idx, pt_ = tp.acquire()
                    if kt8 == 0:
                        em.wait("vector", "pe", free["PT"])
                    for k4 in range(4):
                        kt = kt8 * 4 + k4
                        op = TE.transpose(pt_[:, k4 * 128:(k4 + 1) * 128],
                                          Pbuf[:, kt * 128:(kt + 1) * 128],
                                          bufs["eye"][:, :])
                    pe_c = em.inc(op, "pe")
                    em.wait("vector", "pe", pe_c)
                    op = V.tensor_copy(
                        out=PTb[:, kt8 * 512:(kt8 + 1) * 512],
                        in_=pt_[:, 0:512])
                    nev = tp.release(idx, op, "vector")
                free["P"] = em.cnt["ve"]
                em.wait("tensor", "ve", nev)
                idx, ps = poolA.acquire()
                for kt in range(NKT):
                    op = TE.matmul(ps[0:64, 0:128],
                                   vh[:, kt * HD:(kt + 1) * HD],
                                   PTb[:, kt * 128:(kt + 1) * 128],
                                   start=(kt == 0), stop=(kt == NKT - 1))
                pe_c = em.inc(op, "pe")
                free["PT"] = pe_c
                em.wait("tensor", "ve", inv_c)
                idx2, ps2_ = poolO.acquire()
                op = TE.transpose(ps2_[0:1, 0:128], mx[:, so + 11:so + 12],
                                  bufs["eye32"][:, :])
                pe2 = em.inc(op, "pe")
                em.wait("vector", "pe", pe2)
                op = V.tensor_copy(out=orow[0:1, qs * 128:qs * 128 + 128],
                                   in_=ps2_[0:1, 0:128])
                poolO.release(idx2, op, "vector")
                fence(em, "vector")
                em.wait("vector", "pe", pe_c)
                if qs == 0:
                    em.wait("vector", "osp", osp_st)
                op = V.tensor_copy(out=oth[0:64, qs * 128:(qs + 1) * 128],
                                   in_=ps[0:64, 0:128])
                poolA.release(idx, op, "vector")

            stage1(0)
            for qs in range(4):
                if qs < 3:
                    stage1(qs + 1)
                stage2(qs)
            idx3, ps3 = poolO.acquire()
            em.wait("tensor", "ve", em.cnt["ve"])
            op = TE.matmul(ps3[:, :], bufs["ones32"][0:1, :], orow[0:1, :],
                           start=True, stop=True)
            pe3 = em.inc(op, "pe")
            em.wait("vector", "pe", pe3)
            op = V.tensor_copy(out=orep[:, :], in_=ps3[:, :])
            poolO.release(idx3, op, "vector")
            fence(em, "vector")
            op = V.tensor_tensor(out=oth[0:64, :], in0=oth[0:64, :],
                                 in1=orep[0:64, :], op=ALU.mult)
            oc = em.inc(op, "ve")
            em.wait("sync", "ve", oc)
            osp_st = em.dma("sync", osp[hh * HD:(hh + 1) * HD, q0:q0 + 512],
                            oth[0:64, :], sem="osp")
            free["qTh"] = em.cnt["pe"]
        free["ktTh"] = em.cnt["pe"]
        free["vh"] = em.cnt["pe"]
    em.barrier()

    xo_store = [0, 0]
    xt_used = [0, 0]
    ynS_used = [0, 0]
    for tsl in range(T // 512):
        for t8 in range(8):
            sb = t8 % 2
            em.wait("sync", "ve", ynS_used[sb])
            cdma = em.dma("sync", ynS[sb][:, :],
                          osp[t8 * 128:(t8 + 1) * 128,
                              tsl * 512:(tsl + 1) * 512], sem=f"yn{sb}")
            em.wait("vector", f"yn{sb}", cdma)
            op = V.tensor_copy(out=ynT[:, t8 * 512:(t8 + 1) * 512],
                               in_=ynS[sb][:, :])
            ynS_used[sb] = em.inc(op, "ve")
        em.wait("tensor", "ve", em.cnt["ve"])
        for tt in range(4):
            tg = tsl * 4 + tt
            b = tg % 2
            em.wait("sync", "ve", xt_used[b])
            em.wait("sync", "sc", ln_hist[b])
            c2 = em.dma("sync", bufs["xt"][b][:, :],
                        xin[tg * 128:(tg + 1) * 128, :], sem=f"xa{b}")
            ve_c = 0
            for dn in range(2):
                idx, ps = poolO.acquire()
                for ic in range(8):
                    op = TE.matmul(
                        ps[:, :],
                        ynT[:, ic * 512 + tt * 128: ic * 512 + tt * 128 + 128],
                        wov[:, ic * D + dn * 512: ic * D + dn * 512 + 512],
                        start=(ic == 0), stop=(ic == 7))
                pe_c = em.inc(op, "pe")
                em.wait("vector", "pe", pe_c)
                if dn == 0:
                    em.wait("vector", f"xa{b}", c2)
                    em.wait("vector", f"xs{b}", xo_store[b])
                op = V.tensor_tensor(
                    out=xo[b][:, dn * 512:(dn + 1) * 512], in0=ps[:, :],
                    in1=bufs["xt"][b][:, dn * 512:(dn + 1) * 512], op=ALU.add)
                ve_c = poolO.release(idx, op, "vector")
            xt_used[b] = ve_c
            em.wait("sync", "ve", ve_c)
            xo_store[b] = em.dma("sync", xout[tg * 128:(tg + 1) * 128, :],
                                 xo[b][:, :], sem=f"xs{b}")
        em.wait("sync", "pe", em.cnt["pe"])


def l2_attn_host(inp):
    a = {}
    a["l2a_gamcol"] = np.ascontiguousarray(
        inp["L2_attn_gamma"].reshape(8, 128).T)
    a["l2a_wq"] = inp["L2_attn_Wq"]
    a["l2a_wkv"] = inp["L2_attn_Wkv"]
    a["l2a_wo"] = inp["L2_attn_Wo"]
    wb = np.zeros((64, 16 * 64), np.float32)
    for h in range(16):
        wb[:, h * 64:(h + 1) * 64] = inp["L2_attn_W_bilinear"][h]
    a["l2a_wbilT"] = wb
    ROPE_BASE = 10000.0
    dh = 64
    theta = 1.0 / ROPE_BASE ** (np.arange(0, dh, 2, dtype=np.float64) / dh)
    pos = np.arange(T, dtype=np.float64)
    freqs = pos[:, None] * theta[None, :]
    emb = np.concatenate([freqs, freqs], axis=-1)
    cos1 = np.cos(emb).astype(np.float32)
    sin1 = np.sin(emb).astype(np.float32)
    a["l2a_cos"] = np.tile(cos1, (1, 16))
    a["l2a_sin"] = np.tile(sin1, (1, 16))
    return a


ATTN_SETUP["l2a"] = setup_l2_attn
ATTN_HOST["l2a"] = l2_attn_host


def setup_l0_attn(em, bufs, xin, xout, decls):
    nc = em.nc
    name = "l0a"
    CH = 256
    NCH = T // CH
    NH, HD, DST = 32, 64, 128
    DSSM = 2048
    NBC = 18
    DIP = 4384
    V = em.on("vector")
    SC = em.on("scalar")
    TE = em.on("tensor")
    poolA, poolO = bufs["poolA"], bufs["poolO"]
    tp = bufs["tp_pool"]

    lnxT = em.salloc(f"{name}_lnxT", [128, 8 * CH], BF16)
    winT = em.salloc(f"{name}_winTs", [128, 8 * DIP], BF16)
    woutT = em.salloc(f"{name}_woutTs", [128, 16 * D], BF16)
    raw = [em.salloc(f"{name}_raw{i}", [128, CH], F32)
           for i in range(2)]
    tails = em.salloc(f"{name}_tails", [128, NBC * 4], F32)
    xbcc = em.salloc(f"{name}_xbcc", [128, NBC * CH], BF16)
    xtok = em.salloc(f"{name}_xtok", [128, 2 * DSSM], BF16)
    xdt = em.salloc(f"{name}_xdt", [128, 4 * HD], BF16)
    xdw = em.salloc(f"{name}_xdw", [128, 4 * HD], BF16)
    btok = em.salloc(f"{name}_btok", [128, 2 * DST], BF16)
    ytok = em.salloc(f"{name}_ytok", [128, 2 * DSSM], BF16)
    ygate = em.salloc(f"{name}_yg", [128, 768], F32)
    zsil = em.salloc(f"{name}_zsil", [128, 1024], BF16)
    dtb = em.salloc(f"{name}_dtb", [32, 4 * CH], F32)
    acolw = em.salloc(f"{name}_acolw", [128, 64], F32)
    dtcw = em.salloc(f"{name}_dtcw", [128, 64], F32)
    wdcw = em.salloc(f"{name}_wdcw", [128, 64], F32)
    aflat = em.salloc(f"{name}_aflat", [1, 2 * CH], F32)
    Mbf = em.salloc(f"{name}_Mbf", [128, 4 * CH], BF16)
    cexp = em.salloc(f"{name}_cexp", [128, 2 * CH], BF16)
    maskb = em.salloc(f"{name}_maskbs", [128, 2 * CH], BF16)
    S32 = em.salloc(f"{name}_S32", [128, NH * HD], F32)
    Sbf = em.salloc(f"{name}_Sbf", [128, NH * HD], BF16)
    cdrep = em.salloc(f"{name}_cdrep", [128, 32], F32)
    drep = em.salloc(f"{name}_drep", [128, 32], F32)
    hrow = em.salloc(f"{name}_hrow2", [1, 64], F32)
    ccols = em.salloc(f"{name}_ccolss", [128, NBC * 6], F32)
    hcols = em.salloc(f"{name}_hcolss", [32, 4], F32)
    nwcol = em.salloc(f"{name}_nwcols", [128, 16], F32)
    binz = em.salloc(f"{name}_binzs", [1, DSSM], BF16)
    bout = em.salloc(f"{name}_bouts", [1, D], BF16)
    rmsc = em.salloc(f"{name}_rmsc", [128, 8], F32)
    xo = [em.salloc(f"{name}_xo0", [128, D], F32)] * 2
    asp = nc.dram_tensor(f"{name}_ASP", [32, CH], F32, kind="Internal").ap()
    ynT = xtok

    cstS, cstG = f"cst_{name}", f"cstg_{name}"
    h = decls.dram(em, "l0a_winT", [D, DIP])
    em.dma("gpsimd", winT.ap().rearrange("p (c d) -> p c d", c=8),
           h.rearrange("(c p) d -> p c d", p=128), sem=cstG)
    h = decls.dram(em, "l0a_woutT", [2048, D])
    em.dma("gpsimd", woutT.ap().rearrange("p (c d) -> p c d", c=16),
           h.rearrange("(c p) d -> p c d", p=128), sem=cstG)
    h = decls.dram(em, "l0a_binz", [1, DSSM])
    em.dma("gpsimd", binz[0:1, :], h[0:1, :], sem=cstG)
    h = decls.dram(em, "l0a_bout", [1, D])
    em.dma("gpsimd", bout[0:1, :], h[0:1, :], sem=cstG)
    h = decls.dram(em, "l0a_maskb", [128, 2 * CH])
    em.dma("gpsimd", maskb[:, :], h[:, :], sem=cstG)
    h = decls.dram(em, "l0a_ccols", [128, NBC * 6])
    em.dma("sync", ccols[:, :], h[:, :], sem=cstS)
    h = decls.dram(em, "l0a_hcols", [32, 4])
    em.dma("sync", hcols[0:32, :], h[:, :], sem=cstS)
    h = decls.dram(em, "l0a_nwcol", [128, 16])
    em.dma("sync", nwcol[:, :], h[:, :], sem=cstS)
    em.wait("tensor", cstS, em.cnt[cstS])
    em.wait("tensor", cstG, em.cnt[cstG])
    em.wait("vector", cstS, em.cnt[cstS])
    em.wait("vector", cstG, em.cnt[cstG])
    em.wait("scalar", cstS, em.cnt[cstS])
    em.wait("scalar", cstG, em.cnt[cstG])

    op = SC.activation(out=hcols[0:32, 1:2], in_=hcols[0:32, 1:2], func=AF.Exp)
    c = em.inc(op, "sc")
    em.wait("vector", "sc", c)
    op = V.tensor_scalar(out=hcols[0:32, 1:2], in0=hcols[0:32, 1:2],
                         scalar1=-1.0, scalar2=None, op0=ALU.mult)
    c = em.inc(op, "ve")
    em.wait("tensor", "ve", c)
    idx, ps = poolA.acquire()
    op = TE.transpose(ps[0:1, 0:32], hcols[0:32, 2:3],
                      bufs["eye32"][0:32, 0:32])
    pe_c = em.inc(op, "pe")
    em.wait("vector", "pe", pe_c)
    op = V.tensor_copy(out=hrow[0:1, 32:64], in_=ps[0:1, 0:32])
    poolA.release(idx, op, "vector")
    fence(em, "vector")
    em.wait("tensor", "ve", em.cnt["ve"])
    idx, ps = poolA.acquire()
    op = TE.matmul(ps[:, 0:32], bufs["ones32"][0:1, :], hrow[0:1, 32:64],
                   start=True, stop=True)
    pe_c = em.inc(op, "pe")
    em.wait("vector", "pe", pe_c)
    op = V.tensor_copy(out=drep[:, :], in_=ps[:, 0:32])
    poolA.release(idx, op, "vector")
    for kc in range(16):
        op = V.tensor_scalar(out=woutT[:, kc * D:(kc + 1) * D],
                             in0=woutT[:, kc * D:(kc + 1) * D],
                             scalar1=nwcol[:, kc:kc + 1], scalar2=None,
                             op0=ALU.mult)
    op = V.memset(S32[:, :], 0.0)
    op = V.memset(Sbf[:, :], 0.0)
    op = V.memset(tails[:, :], 0.0)
    st_ready = em.inc(op, "ve")
    em.wait("tensor", "ve", st_ready)
    em.wait("scalar", "ve", st_ready)

    xo_store = [0, 0]
    xtg = [0, 0]
    zsil_free = [0]
    pend = {}

    def phase_front(ch):
        tok0 = ch * CH
        lnx = lnxT[:, 0:8 * CH]
        ln_hist = emit_ln_sweep(em, bufs, xin, lnx, CH, tok0, CH,
                                xt_gate=list(xtg))
        em.wait("tensor", "ve", em.cnt["ve"])
        raw_done = [0, 0]
        for cc in range(NBC):
            rb = cc % 2
            idx, ps = poolA.acquire()
            for kc in range(8):
                op = TE.matmul(
                    ps[:, 0:CH],
                    winT[:, kc * DIP + DSSM + cc * 128:
                         kc * DIP + DSSM + cc * 128 + 128],
                    lnx[:, kc * CH: kc * CH + CH],
                    start=(kc == 0), stop=(kc == 7))
            pe_c = em.inc(op, "pe")
            em.wait("scalar", "pe", pe_c)
            em.wait("scalar", "ve", raw_done[rb])
            op = SC.activation(out=raw[rb][:, :], in_=ps[:, 0:CH],
                               func=AF.Identity,
                               bias=ccols[:, 5 * NBC + cc: 5 * NBC + cc + 1])
            rdy = poolA.release(idx, op, "scalar")
            em.wait("vector", "sc", rdy)
            sl = xbcc[:, cc * CH:(cc + 1) * CH]
            V.tensor_scalar(out=ygate[:, 0:CH], in0=raw[rb][:, :],
                            scalar1=ccols[:, 4 * cc + 3:4 * cc + 4],
                            scalar2=None, op0=ALU.mult)
            for k in range(3):
                sh = 3 - k
                V.scalar_tensor_tensor(
                    out=ygate[:, sh:CH], in0=raw[rb][:, 0:CH - sh],
                    scalar=ccols[:, 4 * cc + k:4 * cc + k + 1],
                    in1=ygate[:, sh:CH], op0=ALU.mult, op1=ALU.add)
                V.scalar_tensor_tensor(
                    out=ygate[:, 0:sh],
                    in0=tails[:, cc * 4 + 4 - sh: cc * 4 + 4],
                    scalar=ccols[:, 4 * cc + k:4 * cc + k + 1],
                    in1=ygate[:, 0:sh], op0=ALU.mult, op1=ALU.add)
            fence(em, "vector")
            op = V.tensor_copy(out=tails[:, cc * 4 + 1: cc * 4 + 4],
                               in_=raw[rb][:, CH - 3:CH])
            raw_done[rb] = em.inc(op, "ve")
            em.wait("scalar", "ve", raw_done[rb])
            op = SC.activation(out=sl, in_=ygate[:, 0:CH], func=AF.Silu,
                               bias=ccols[:, 4 * NBC + cc: 4 * NBC + cc + 1])
            em.inc(op, "sc")
        idx, psd = poolA.acquire()
        for kc in range(8):
            op = TE.matmul(
                psd[0:32, 0:CH],
                winT[:, kc * DIP + 4352: kc * DIP + 4352 + 32],
                lnx[:, kc * CH: kc * CH + CH],
                start=(kc == 0), stop=(kc == 7))
        pe_c = em.inc(op, "pe")
        em.wait("vector", "pe", pe_c)
        op = V.tensor_scalar(out=dtb[0:32, 0:CH], in0=psd[0:32, 0:CH],
                             scalar1=hcols[0:32, 0:1], scalar2=None,
                             op0=ALU.add)
        poolA.release(idx, op, "vector")
        fence(em, "vector")
        op = V.tensor_scalar(out=dtb[0:32, CH:2 * CH], in0=dtb[0:32, 0:CH],
                             scalar1=0.0, scalar2=None, op0=ALU.max)
        c = em.inc(op, "ve")
        em.wait("scalar", "ve", c)
        op = SC.activation(out=dtb[0:32, 2 * CH:3 * CH], in_=dtb[0:32, 0:CH],
                           func=AF.Abs)
        c = em.inc(op, "sc")
        em.wait("scalar", "sc", c)
        op = SC.activation(out=dtb[0:32, 2 * CH:3 * CH],
                           in_=dtb[0:32, 2 * CH:3 * CH], func=AF.Exp,
                           scale=-1.0)
        c = em.inc(op, "sc")
        em.wait("scalar", "sc", c)
        op = SC.activation(out=dtb[0:32, 2 * CH:3 * CH],
                           in_=dtb[0:32, 2 * CH:3 * CH], func=AF.Ln,
                           bias=bufs["konst"][0:32, 2:3])
        c = em.inc(op, "sc")
        em.wait("vector", "sc", c)
        op = V.tensor_tensor(out=dtb[0:32, CH:2 * CH],
                             in0=dtb[0:32, CH:2 * CH],
                             in1=dtb[0:32, 2 * CH:3 * CH], op=ALU.add)
        fence(em, "vector")
        op = V.tensor_scalar(out=dtb[0:32, 0:CH], in0=dtb[0:32, CH:2 * CH],
                             scalar1=hcols[0:32, 1:2], scalar2=None,
                             op0=ALU.mult)
        fence(em, "vector")
        cur, nxt = 0, 2 * CH
        for st in range(8):
            sh = 1 << st
            V.tensor_copy(out=dtb[0:32, nxt:nxt + sh],
                          in_=dtb[0:32, cur:cur + sh])
            op = V.tensor_tensor(out=dtb[0:32, nxt + sh:nxt + CH],
                                 in0=dtb[0:32, cur + sh:cur + CH],
                                 in1=dtb[0:32, cur:cur + CH - sh], op=ALU.add)
            cur, nxt = nxt, cur
        fence(em, "vector")
        op = V.tensor_copy(out=dtb[0:32, 2 * CH:3 * CH],
                           in_=dtb[0:32, cur:cur + CH])
        fence(em, "vector")
        op = V.tensor_copy(out=hcols[0:32, 3:4],
                           in_=dtb[0:32, 3 * CH - 1:3 * CH])
        c = em.inc(op, "ve")
        em.wait("scalar", "ve", c)
        op = SC.activation(out=dtb[0:32, 3 * CH:4 * CH],
                           in_=dtb[0:32, 2 * CH:3 * CH], func=AF.Exp,
                           scale=-1.0, bias=hcols[0:32, 3:4])
        c = em.inc(op, "sc")
        em.wait("vector", "sc", c)
        op = V.tensor_tensor(out=dtb[0:32, 3 * CH:4 * CH],
                             in0=dtb[0:32, 3 * CH:4 * CH],
                             in1=dtb[0:32, CH:2 * CH], op=ALU.mult)
        adone = em.inc(op, "ve")
        em.wait("tensor", "ve", adone)
        idx, ps = poolA.acquire()
        op = TE.transpose(ps[0:1, 0:32], hcols[0:32, 3:4],
                          bufs["eye32"][0:32, 0:32])
        pe_c = em.inc(op, "pe")
        em.wait("scalar", "pe", pe_c)
        op = SC.activation(out=hrow[0:1, 0:32], in_=ps[0:1, 0:32], func=AF.Exp)
        cdr = poolA.release(idx, op, "scalar")
        em.wait("tensor", "sc", cdr)
        idx, ps = poolA.acquire()
        op = TE.matmul(ps[:, 0:32], bufs["ones32"][0:1, :], hrow[0:1, 0:32],
                       start=True, stop=True)
        pe_c = em.inc(op, "pe")
        em.wait("vector", "pe", pe_c)
        op = V.tensor_copy(out=cdrep[:, :], in_=ps[:, 0:32])
        poolA.release(idx, op, "vector")
        for src0, dst in ((2 * CH, acolw), (CH, dtcw), (3 * CH, wdcw)):
            for jt in range(2):
                idx, ps = poolA.acquire()
                op = TE.transpose(
                    ps[0:128, 0:32],
                    dtb[0:32, src0 + jt * 128: src0 + jt * 128 + 128],
                    bufs["eye32"][0:32, 0:32])
                pe_c = em.inc(op, "pe")
                em.wait("vector", "pe", pe_c)
                op = V.tensor_copy(out=dst[:, jt * 32:(jt + 1) * 32],
                                   in_=ps[0:128, 0:32])
                poolA.release(idx, op, "vector")
        fence(em, "vector")
        em.wait("sync", "ve", adone)
        ca1 = em.dma("sync", asp[:, :], dtb[0:32, 2 * CH:3 * CH], sem="asw")
        pend[ch] = (ln_hist, ca1)

    def phase_xb(ch):
        tok0 = ch * CH
        em.wait("tensor", "sc", em.cnt["sc"])
        last = 0
        for cc in range(16):
            for jt in range(2):
                idx, pt_ = tp.acquire()
                op = TE.transpose(pt_[:, 0:128],
                                  xbcc[:, cc * CH + jt * 128:
                                       cc * CH + jt * 128 + 128],
                                  bufs["eye"][:, :])
                pe_c = em.inc(op, "pe")
                em.wait("vector", "pe", pe_c)
                op = V.tensor_copy(
                    out=xtok[:, jt * DSSM + cc * 128: jt * DSSM + cc * 128 + 128],
                    in_=pt_[:, 0:128])
                last = tp.release(idx, op, "vector")
        for jt in range(2):
            idx, pt_ = tp.acquire()
            op = TE.transpose(pt_[:, 0:128],
                              xbcc[:, 16 * CH + jt * 128: 16 * CH + jt * 128 + 128],
                              bufs["eye"][:, :])
            pe_c = em.inc(op, "pe")
            em.wait("vector", "pe", pe_c)
            op = V.tensor_copy(out=btok[:, jt * DST:jt * DST + 128],
                               in_=pt_[:, 0:128])
            last = tp.release(idx, op, "vector")
        em.wait("tensor", "ve", last)

    def phase_ssm(ch):
        tok0 = ch * CH
        ca1 = pend[ch][1]
        expa_free = 0
        slot_tok = [None, None]

        def stageA(hh):
            nonlocal ca1, expa_free
            s = hh % 2
            if hh % 2 == 0:
                em.wait("sync", "asw", ca1)
                em.wait("sync", "pe", em.cnt["pe"])
                ca2 = em.dma("sync", aflat[0:1, :],
                             asp.rearrange("h t -> (h t)")[None,
                             (hh // 2) * 2 * CH:(hh // 2 + 1) * 2 * CH],
                             sem="asr")
                em.wait("tensor", "asr", ca2)
            idxR, psR = poolO.acquire()
            op = TE.matmul(psR[:, 0:CH], bufs["ones32"][0:1, :],
                           aflat[0:1, (hh % 2) * CH:(hh % 2 + 1) * CH],
                           start=True, stop=True)
            peR = em.inc(op, "pe")
            em.wait("scalar", "pe", peR)
            em.wait("scalar", "ve", expa_free)
            op = SC.activation(out=ygate[:, 0:CH], in_=psR[:, 0:CH],
                               func=AF.Exp)
            expa_c = em.inc(op, "sc")
            em.wait("vector", "sc", expa_c)
            op = V.tensor_tensor(out=cexp[:, s * CH:(s + 1) * CH],
                                 in0=xbcc[:, 17 * CH:18 * CH],
                                 in1=ygate[:, 0:CH], op=ALU.mult)
            cexp_c = em.inc(op, "ve")
            expa_free = cexp_c
            Ms = 0
            for jt in range(2):
                op = V.tensor_scalar(
                    out=ygate[:, CH:2 * CH], in0=psR[:, 0:CH],
                    scalar1=acolw[:, jt * 32 + hh:jt * 32 + hh + 1],
                    scalar2=None, op0=ALU.subtract)
                c = em.inc(op, "ve")
                em.wait("vector", "ve", c)
                op = V.tensor_tensor(out=ygate[:, CH:2 * CH],
                                     in0=ygate[:, CH:2 * CH],
                                     in1=maskb[:, jt * CH:(jt + 1) * CH],
                                     op=ALU.min)
                c = em.inc(op, "ve")
                em.wait("scalar", "ve", c)
                op = SC.activation(out=ygate[:, 2 * CH:3 * CH],
                                   in_=ygate[:, CH:2 * CH], func=AF.Exp)
                c = em.inc(op, "sc")
                idxG, psG = poolA.acquire()
                op = TE.matmul(psG[:, 0:CH],
                               xbcc[:, 16 * CH + jt * 128: 16 * CH + jt * 128 + 128],
                               xbcc[:, 17 * CH:18 * CH],
                               start=True, stop=True)
                peG = em.inc(op, "pe")
                em.wait("vector", "pe", peG)
                em.wait("vector", "sc", c)
                op = V.tensor_tensor(
                    out=Mbf[:, s * 2 * CH + jt * CH: s * 2 * CH + (jt + 1) * CH],
                    in0=ygate[:, 2 * CH:3 * CH],
                    in1=psG[:, 0:CH], op=ALU.mult)
                Ms = poolA.release(idxG, op, "vector")
            for jt in range(2):
                V.tensor_scalar(out=xdt[:, s * 2 * HD + jt * HD:
                                        s * 2 * HD + (jt + 1) * HD],
                                in0=xtok[:, jt * DSSM + hh * HD:
                                         jt * DSSM + hh * HD + HD],
                                scalar1=dtcw[:, jt * 32 + hh:jt * 32 + hh + 1],
                                scalar2=None, op0=ALU.mult)
                op = V.tensor_scalar(out=xdw[:, s * 2 * HD + jt * HD:
                                             s * 2 * HD + (jt + 1) * HD],
                                     in0=xtok[:, jt * DSSM + hh * HD:
                                              jt * DSSM + hh * HD + HD],
                                     scalar1=wdcw[:, jt * 32 + hh:jt * 32 + hh + 1],
                                     scalar2=None, op0=ALU.mult)
            xd_c = poolO.release(idxR, op, "vector")
            slot_tok[s] = max(xd_c, Ms, cexp_c)

        def stageB(hh):
            s = hh % 2
            em.wait("tensor", "ve", slot_tok[s])
            for ii in range(2):
                idxY, psY = poolO.acquire()
                for jt in range(2):
                    op = TE.matmul(psY[:, 0:HD],
                                   Mbf[:, s * 2 * CH + jt * CH + ii * 128:
                                       s * 2 * CH + jt * CH + ii * 128 + 128],
                                   xdt[:, s * 2 * HD + jt * HD:
                                       s * 2 * HD + (jt + 1) * HD],
                                   start=(jt == 0), stop=False)
                op = TE.matmul(psY[:, 0:HD],
                               cexp[:, s * CH + ii * 128: s * CH + ii * 128 + 128],
                               Sbf[:, hh * HD:(hh + 1) * HD],
                               start=False, stop=True)
                peY = em.inc(op, "pe")
                em.wait("vector", "pe", peY)
                op = V.scalar_tensor_tensor(
                    out=ytok[:, ii * DSSM + hh * HD: ii * DSSM + hh * HD + HD],
                    in0=xtok[:, ii * DSSM + hh * HD: ii * DSSM + hh * HD + HD],
                    scalar=drep[:, hh:hh + 1], in1=psY[:, 0:HD],
                    op0=ALU.mult, op1=ALU.add)
                poolO.release(idxY, op, "vector")
            idxS, psS = poolA.acquire()
            for jt in range(2):
                op = TE.matmul(psS[:, 0:HD],
                               btok[:, jt * DST:jt * DST + 128],
                               xdw[:, s * 2 * HD + jt * HD:
                                   s * 2 * HD + (jt + 1) * HD],
                               start=(jt == 0), stop=(jt == 1))
            peS = em.inc(op, "pe")
            em.wait("vector", "pe", peS)
            op = V.scalar_tensor_tensor(
                out=S32[:, hh * HD:(hh + 1) * HD],
                in0=S32[:, hh * HD:(hh + 1) * HD],
                scalar=cdrep[:, hh:hh + 1], in1=psS[:, 0:HD],
                op0=ALU.mult, op1=ALU.add)
            poolA.release(idxS, op, "vector")

        for hh in range(NH):
            stageA(hh)
            if hh > 0:
                stageB(hh - 1)
        stageB(NH - 1)
        fence(em, "vector")
        op = V.tensor_copy(out=Sbf[:, :], in_=S32[:, :])
        c = em.inc(op, "ve")
        em.wait("tensor", "ve", c)

    def phase_gate(ch, ln_hist):
        tok0 = ch * CH
        lnx = lnxT[:, 0:8 * CH]
        for tt in range(2):
            for half in range(2):
                for sub in range(2):
                    nn = half * 2 + sub
                    idx, ps = poolO.acquire()
                    for kc in range(8):
                        op = TE.matmul(
                            ps[:, :],
                            lnx[:, kc * CH + tt * 128: kc * CH + tt * 128 + 128],
                            winT[:, kc * DIP + nn * 512: kc * DIP + nn * 512 + 512],
                            start=(kc == 0), stop=False)
                    op = TE.matmul(ps[:, :], bufs["ones_row"][0:1, :],
                                   binz[0:1, nn * 512:(nn + 1) * 512],
                                   start=False, stop=True)
                    pe_c = em.inc(op, "pe")
                    em.wait("scalar", "pe", pe_c)
                    em.wait("scalar", "ve", zsil_free[0])
                    op = SC.activation(out=zsil[:, sub * 512:(sub + 1) * 512],
                                       in_=ps[:, :], func=AF.Silu)
                    poolO.release(idx, op, "scalar")
                fence(em, "scalar")
                c = em.cnt["sc"]
                em.wait("vector", "sc", c)
                hsl = slice(tt * DSSM + half * 1024,
                            tt * DSSM + half * 1024 + 1024)
                op = V.tensor_tensor(out=ytok[:, hsl], in0=ytok[:, hsl],
                                     in1=zsil[:, :], op=ALU.mult)
                c = em.inc(op, "ve")
                zsil_free[0] = c
                em.wait("scalar", "ve", c)
                op = SC.activation(out=zsil[:, :], in_=ytok[:, hsl],
                                   func=AF.Square,
                                   accum_out=rmsc[:, half:half + 1])
                em.inc(op, "sc")
            em.wait("vector", "sc", em.cnt["sc"])
            op = V.tensor_tensor(out=rmsc[:, 2:3], in0=rmsc[:, 0:1],
                                 in1=rmsc[:, 1:2], op=ALU.add)
            fence(em, "vector")
            op = V.tensor_scalar(out=rmsc[:, 2:3], in0=rmsc[:, 2:3],
                                 scalar1=1.0 / DSSM, scalar2=None,
                                 op0=ALU.mult)
            c = em.inc(op, "ve")
            em.wait("scalar", "ve", c)
            op = SC.activation(out=rmsc[:, 2:3], in_=rmsc[:, 2:3],
                               func=AF.Sqrt, bias=bufs["konst"][:, 1:2])
            c = em.inc(op, "sc")
            em.wait("vector", "sc", c)
            op = V.reciprocal(rmsc[:, 3:4], rmsc[:, 2:3])
            c = em.inc(op, "ve")
            em.wait("vector", "ve", c)
            op = V.tensor_scalar(out=ytok[:, tt * DSSM:(tt + 1) * DSSM],
                                 in0=ytok[:, tt * DSSM:(tt + 1) * DSSM],
                                 scalar1=rmsc[:, 3:4], scalar2=None,
                                 op0=ALU.mult)
            c = em.inc(op, "ve")
            em.wait("tensor", "ve", c)
            last = 0
            for kc in range(16):
                idx, pt_ = tp.acquire()
                op = TE.transpose(pt_[:, 0:128],
                                  ytok[:, tt * DSSM + kc * 128:
                                       tt * DSSM + kc * 128 + 128],
                                  bufs["eye"][:, :])
                pe_c = em.inc(op, "pe")
                em.wait("vector", "pe", pe_c)
                op = V.tensor_copy(
                    out=ynT[:, kc * CH + tt * 128: kc * CH + tt * 128 + 128],
                    in_=pt_[:, 0:128])
                last = tp.release(idx, op, "vector")
            em.wait("tensor", "ve", last)
            tg = (tok0 // 128) + tt
            b = tg % 2
            em.wait("sync", "ve", em.cnt["ve"])
            em.wait("sync", "sc", ln_hist[b])
            cx = em.dma("sync", bufs["xt"][b][:, :],
                        xin[tg * 128:(tg + 1) * 128, :], sem=f"xa{b}")
            ve_c = 0
            for dn in range(2):
                idx, ps = poolO.acquire()
                for kc in range(16):
                    op = TE.matmul(
                        ps[:, :],
                        ynT[:, kc * CH + tt * 128: kc * CH + tt * 128 + 128],
                        woutT[:, kc * D + dn * 512: kc * D + dn * 512 + 512],
                        start=(kc == 0), stop=False)
                op = TE.matmul(ps[:, :], bufs["ones_row"][0:1, :],
                               bout[0:1, dn * 512:(dn + 1) * 512],
                               start=False, stop=True)
                pe_c = em.inc(op, "pe")
                em.wait("vector", "pe", pe_c)
                if dn == 0:
                    em.wait("vector", f"xa{b}", cx)
                    em.wait("vector", f"xs{b}", xo_store[b])
                op = V.tensor_tensor(
                    out=xo[b][:, dn * 512:(dn + 1) * 512], in0=ps[:, :],
                    in1=bufs["xt"][b][:, dn * 512:(dn + 1) * 512], op=ALU.add)
                ve_c = poolO.release(idx, op, "vector")
            xtg[b] = ve_c
            em.wait("sync", "ve", ve_c)
            xo_store[b] = em.dma("sync", xout[tg * 128:(tg + 1) * 128, :],
                                 xo[b][:, :], sem=f"xs{b}")

    phase_front(0)
    for ch in range(NCH):
        phase_xb(ch)
        phase_ssm(ch)
        phase_gate(ch, pend[ch][0])
        if ch + 1 < NCH:
            phase_front(ch + 1)

def l0_attn_host(inp):
    a = {}
    NBC = 18
    a["l0a_winT"] = np.ascontiguousarray(inp["L0_attn_W_in"].T)
    a["l0a_woutT"] = np.ascontiguousarray(inp["L0_attn_W_out"].T)
    a["l0a_binz"] = inp["L0_attn_b_in"][0:2048].reshape(1, 2048)
    a["l0a_bout"] = inp["L0_attn_b_out"].reshape(1, 1024)
    ccols = np.zeros((128, NBC * 6), np.float32)
    cw = inp["L0_attn_conv_w"].reshape(NBC, 128, 4)
    cb = inp["L0_attn_conv_b"].reshape(NBC, 128)
    bx = inp["L0_attn_b_in"][2048:4352].reshape(NBC, 128)
    for cc in range(NBC):
        for k in range(4):
            ccols[:, 4 * cc + k] = cw[cc, :, k]
        ccols[:, 4 * NBC + cc] = cb[cc]
        ccols[:, 5 * NBC + cc] = bx[cc]
    a["l0a_ccols"] = ccols
    hcols = np.zeros((32, 4), np.float32)
    hcols[:, 0] = inp["L0_attn_b_in"][4352:4384] + inp["L0_attn_dt_bias"]
    hcols[:, 1] = inp["L0_attn_A_log"]
    hcols[:, 2] = inp["L0_attn_D"]
    a["l0a_hcols"] = hcols
    a["l0a_nwcol"] = np.ascontiguousarray(
        inp["L0_attn_norm_w"].reshape(16, 128).T)
    CH = 256
    mask = np.zeros((128, 2 * CH), np.float32)
    for jt in range(2):
        jj = (jt * 128 + np.arange(128))[:, None]
        ii = np.arange(CH)[None, :]
        mask[:, jt * CH:(jt + 1) * CH] = np.where(ii >= jj, 0.0, -1e30)
    a["l0a_maskb"] = mask
    return a


ATTN_SETUP["l0a"] = setup_l0_attn
ATTN_HOST["l0a"] = l0_attn_host



ALL_BLOCKS = ["l0a", "l0f", "l1a", "l1f", "l2a", "l2f", "l3a", "l3f"]


def kernel(**inputs):
    blocks = ALL_BLOCKS
    nc, es, decls = build_program(blocks)
    arrays = host_arrays_for(blocks, inputs)
    u = np.asarray(inputs["L0_attn_u"], np.float32)
    in_maps = []
    for core in range(NCORE):
        m = dict(arrays)
        m["XIN"] = np.ascontiguousarray(u[core])
        in_maps.append(m)
    res = run_bass_kernel_spmd(nc, in_maps, core_ids=list(range(NCORE)))
    return np.stack([np.asarray(r["XOUT"], np.float32)
                     for r in res.results], axis=0)


# version 06
# speedup vs baseline: 1.1995x; 1.1995x over previous
import contextlib
import math
import numpy as np
import ml_dtypes
import concourse.bass as bass
import concourse.mybir as mybir
from concourse.bass_utils import run_bass_kernel_spmd

F32 = mybir.dt.float32
BF16 = mybir.dt.bfloat16
AF = mybir.ActivationFunctionType
ALU = mybir.AluOpType
AX = mybir.AxisListType

T = 4096
D = 1024
NCORE = 8
NTILE = T // 128

ENGINES = ("sync", "scalar", "vector", "tensor", "gpsimd")


class Op:
    __slots__ = ("incs",)

    def __init__(self):
        self.incs = []

    def then_inc(self, sem, n):
        self.incs.append((sem, n))
        return self


class Rec:
    def __init__(self):
        self.ops = []

    def __getattr__(self, m):
        def f(*a, **k):
            op = Op()
            self.ops.append((m, a, k, op))
            return op

        return f

    def replay(self, eng):
        for m, a, k, op in self.ops:
            ins = getattr(eng, m)(*a, **k)
            for sem, n in op.incs:
                ins.then_inc(sem, n)


class EM:

    def __init__(self, nc, es):
        self.nc = nc
        self.es = es
        self.rec = {e: Rec() for e in ENGINES}
        self.semh = {}
        self.cnt = {}
        self.qsem = {"sync": "dS", "scalar": "dC", "gpsimd": "dG"}
        self.dma_sems = set()
        for s in ("dS", "dC", "dG", "pe", "ve", "sc", "gp", "sp"):
            self._mk(s)
        for s in ("dS", "dC", "dG"):
            self.dma_sems.add(s)
        self.own = {"sync": "sp", "scalar": "sc", "vector": "ve",
                    "tensor": "pe", "gpsimd": "gp"}
        self.pools = []

    def _mk(self, name):
        self.semh[name] = self.es.enter_context(self.nc.semaphore(name))
        self.cnt[name] = 0

    def on(self, engine):
        return self.rec[engine]

    def salloc(self, *a, **k):
        return self.cur_es.enter_context(self.nc.sbuf_tensor(*a, **k))

    def inc(self, op, sem, k=1):
        op.then_inc(self.semh[sem], k)
        self.cnt[sem] += k
        return self.cnt[sem]

    def dma(self, queue, out, in_, sem=None, **kw):
        if sem is None:
            sem = self.qsem[queue]
        if sem not in self.semh:
            self._mk(sem)
        self.dma_sems.add(sem)
        op = self.rec[queue].dma_start(out=out, in_=in_, **kw)
        return self.inc(op, sem, 16)

    def wait(self, engine, sem, count):
        if count > 0:
            self.rec[engine].wait_ge(self.semh[sem], count)

    def barrier(self):
        for s in sorted(self.dma_sems):
            self.wait("sync", s, self.cnt[s])
        lvl = {}
        for e in ENGINES:
            if e == "sync":
                op = self.rec[e].nop()
            else:
                op = self.rec[e].drain()
            lvl[e] = self.inc(op, self.own[e])
        for e in ENGINES:
            for o in ENGINES:
                if o != e:
                    self.wait(e, self.own[o], lvl[o])
        for p in self.pools:
            p.reset()

    def finish(self):
        with self.nc.Block() as blk:
            blk.sync(lambda e: self.rec["sync"].replay(e))
            blk.scalar(lambda e: self.rec["scalar"].replay(e))
            blk.vector(lambda e: self.rec["vector"].replay(e))
            blk.tensor(lambda e: self.rec["tensor"].replay(e))
            blk.gpsimd(lambda e: self.rec["gpsimd"].replay(e))


class PsumPool:

    def __init__(self, em, tiles):
        self.em = em
        self.tiles = list(tiles)
        self.last_free = [0] * len(self.tiles)
        self.i = 0
        em.pools.append(self)

    def reset(self):
        self.last_free = [0] * len(self.tiles)
        self.i = 0

    def acquire(self):
        idx = self.i % len(self.tiles)
        self.i += 1
        free_at = self.last_free[idx]
        if free_at:
            sem, cnt = free_at
            self.em.wait("tensor", sem, cnt)
        return idx, self.tiles[idx]

    def release(self, idx, evac_op, evac_engine):
        sem = self.em.own[evac_engine]
        cnt = self.em.inc(evac_op, sem)
        self.last_free[idx] = (sem, cnt)
        return cnt


def ceil_div(a, b):
    return (a + b - 1) // b


def emit_ln_sweep(em, bufs, x_dram, lnxT, ncols, col0, stride, router=None,
                  with_tokmajor=None, double_ln=False):
    nc = em.nc
    ntile = ncols // 128
    xt = bufs["xt"]
    scr = bufs["scr"]
    stats = bufs["stats"]
    eye = bufs["eye"]
    eye32 = bufs["eye32"]
    tp = bufs["tp_pool"]
    poolA = bufs["poolA"]
    poolO = bufs["poolO"]
    lnx_tile = bufs["lnx_tile"]
    norm32 = bufs["norm32"]
    f32T = bufs["f32T"]

    SC = em.on("scalar")
    V = em.on("vector")
    TE = em.on("tensor")

    ld = []
    for i in range(ntile):
        t0 = col0 + i * 128
        b = i % 2
        if i >= 2:
            em.wait("sync", "sc", ld[i - 2][1])
        c = em.dma("sync", xt[b][:, :], x_dram[t0:t0 + 128, :], sem=f"xa{b}")
        em.wait("scalar", f"xa{b}", c)
        SC.activation(out=scr[:, :], in_=xt[b][:, :], func=AF.Copy,
                      accum_out=stats[:, 4 * i:4 * i + 1])
        op = SC.activation(out=bufs["scr2"][:, :], in_=xt[b][:, :],
                           func=AF.Square,
                           accum_out=stats[:, 4 * i + 1:4 * i + 2])
        ld.append((c, em.inc(op, "sc")))
    sum_ap = stats.ap()[:, 0:4 * ntile:4]
    ssq_ap = stats.ap()[:, 1:4 * ntile:4]
    rstd_ap = stats.ap()[:, 2:4 * ntile:4]
    negmr_ap = stats.ap()[:, 3:4 * ntile:4]
    mean = bufs["mean"]
    em.wait("vector", "sc", ld[-1][1])
    V.tensor_scalar(out=mean[:, :ntile], in0=sum_ap, scalar1=1.0 / D,
                    scalar2=None, op0=ALU.mult)
    fence(em, "vector")
    V.tensor_tensor(out=scr[:, :ntile], in0=mean[:, :ntile],
                    in1=mean[:, :ntile], op=ALU.mult)
    V.tensor_scalar(out=rstd_ap, in0=ssq_ap, scalar1=1.0 / D, scalar2=None,
                    op0=ALU.mult)
    fence(em, "vector")
    op = V.tensor_tensor(out=rstd_ap, in0=rstd_ap, in1=scr[:, :ntile],
                         op=ALU.subtract)
    vready = em.inc(op, "ve")
    if double_ln:
        V.tensor_copy(out=bufs["mean"].ap()[:, NTILE:NTILE + ntile],
                      in_=rstd_ap)
        fence(em, "vector")
    em.wait("scalar", "ve", em.cnt["ve"])
    op = SC.activation(out=rstd_ap, in_=rstd_ap, func=AF.Sqrt,
                       bias=bufs["konst"][:, 0:1])
    sready = em.inc(op, "sc")
    em.wait("vector", "sc", sready)
    V.reciprocal(rstd_ap, rstd_ap)
    fence(em, "vector")
    if double_ln:
        var_ap = bufs["mean"].ap()[:, NTILE:NTILE + ntile]
        tmp_ap = bufs["mean"].ap()[:, NTILE + ntile:NTILE + 2 * ntile]
        V.tensor_scalar(out=tmp_ap, in0=var_ap, scalar1=1e-5, scalar2=None,
                        op0=ALU.add)
        fence(em, "vector")
        V.reciprocal(tmp_ap, tmp_ap)
        fence(em, "vector")
        op = V.tensor_tensor(out=tmp_ap, in0=var_ap, in1=tmp_ap, op=ALU.mult)
        c0 = em.inc(op, "ve")
        em.wait("scalar", "ve", c0)
        op = SC.activation(out=tmp_ap, in_=tmp_ap, func=AF.Sqrt,
                           bias=bufs["konst"][:, 0:1])
        c0 = em.inc(op, "sc")
        em.wait("vector", "sc", c0)
        V.reciprocal(tmp_ap, tmp_ap)
        fence(em, "vector")
        V.tensor_tensor(out=rstd_ap, in0=rstd_ap, in1=tmp_ap, op=ALU.mult)
        fence(em, "vector")
    V.tensor_tensor(out=scr[:, :ntile], in0=mean[:, :ntile],
                    in1=rstd_ap, op=ALU.mult)
    fence(em, "vector")
    op = V.tensor_scalar(out=negmr_ap, in0=scr[:, :ntile], scalar1=-1.0,
                         scalar2=None, op0=ALU.mult)
    nready = em.inc(op, "ve")
    em.wait("scalar", "ve", nready)

    norm_free = [0, 0]
    f32T_busy = [0, 0]
    sweep1_last = ld[-1][1]
    norm_done_hist = [0, 0]
    for i in range(ntile):
        t0 = col0 + i * 128
        b = i % 2
        if i < 2:
            em.wait("sync", "sc", sweep1_last)
        else:
            em.wait("sync", "sc", norm_done_hist[b])
        c = em.dma("sync", xt[b][:, :], x_dram[t0:t0 + 128, :], sem=f"xa{b}")
        em.wait("scalar", f"xa{b}", c)
        em.wait("scalar", "pe", norm_free[b])
        op = SC.activation(out=norm32[b][:, :], in_=xt[b][:, :],
                           func=AF.Identity,
                           scale=stats.ap()[:, 4 * i + 2:4 * i + 3],
                           bias=stats.ap()[:, 4 * i + 3:4 * i + 4])
        norm_done = em.inc(op, "sc")
        norm_done_hist[b] = norm_done
        if with_tokmajor is not None:
            em.wait("vector", "sc", norm_done)
            op = V.tensor_copy(out=with_tokmajor[:, i * D:(i + 1) * D],
                               in_=norm32[b][:, :])
            em.inc(op, "ve")
        em.wait("tensor", "sc", norm_done)
        last_pe = 0
        evac_ves = 0
        for c8 in range(8):
            idx, pt = poolA.acquire()
            op = TE.transpose(pt[:, 0:128], norm32[b][:, c8 * 128:(c8 + 1) * 128],
                              eye32[:, :])
            pe_c = em.inc(op, "pe")
            last_pe = pe_c
            em.wait("vector", "pe", pe_c)
            if router is not None and c8 == 0:
                em.wait("vector", "pe", f32T_busy[i % 2])
            V.tensor_copy(
                out=lnxT[:, c8 * stride + i * 128: c8 * stride + i * 128 + 128],
                in_=pt[:, 0:128])
            if router is not None:
                op = V.tensor_copy(out=f32T[i % 2][:, c8 * 128:(c8 + 1) * 128],
                                   in_=pt[:, 0:128])
                evac_ves = poolA.release(idx, op, "vector")
            else:
                m_, a_, k_, op2 = em.rec["vector"].ops[-1]
                evac_ves = poolA.release(idx, op2, "vector")
        norm_free[b] = last_pe
        if router is None or router.get("skip_mm"):
            continue
        E = router["E"]
        idxo, pso = poolO.acquire()
        em.wait("tensor", "ve", evac_ves)
        if globals().get("DEBUG_ON") and i == globals().get("DEBUG_F32T_TILE", -1):
            o = em.nc.dram_tensor("DBG_f32T", [128, 1024], F32,
                                  kind="ExternalOutput").ap()
            em.wait("sync", "ve", evac_ves)
            cdma = em.dma("sync", o[:, :], f32T[i % 2][:, :], sem="dbgs")
            em.wait("vector", "dbgs", cdma)
        for c8 in range(8):
            op = TE.matmul(pso[:, 0:E], f32T[i % 2][:, c8 * 128:(c8 + 1) * 128],
                           router["gate32"][:, c8 * E:(c8 + 1) * E],
                           start=(c8 == 0),
                           stop=(router["gbias32"] is None and c8 == 7))
        if router["gbias32"] is not None:
            op = TE.matmul(pso[:, 0:E], bufs["ones32"][0:1, :],
                           router["gbias32"][0:1, :], start=False, stop=True)
        pe_c = em.inc(op, "pe")
        f32T_busy[i % 2] = pe_c
        extras = router.get("extras") or {}
        if globals().get("DEBUG_ON") and "route_dbg" in bufs:
            extras = dict(extras)
            extras["dbg"] = bufs["route_dbg"]
            extras["dbg_i"] = i
        route_tile(em, bufs, router["kind"], pso, pe_c,
                   router["cw"][:, i * E:(i + 1) * E], E, poolO, idxo,
                   extras=extras or None)
    return norm_done_hist


def alloc_common(em, lnxT_stride=0):
    nc = em.nc
    bufs = {}
    bufs["xt"] = [nc.alloc_sbuf_tensor(f"xt{i}", [128, D], F32) for i in range(2)]
    bufs["scr"] = nc.alloc_sbuf_tensor("scr", [128, D], F32)
    bufs["scr2"] = bufs["scr"]
    bufs["stats"] = nc.alloc_sbuf_tensor("stats", [128, 4 * NTILE], F32)
    bufs["mean"] = nc.alloc_sbuf_tensor("meanb", [128, 3 * NTILE], F32)
    bufs["eye"] = nc.alloc_sbuf_tensor("eye", [128, 128], BF16)
    bufs["eye32"] = nc.alloc_sbuf_tensor("eye32", [128, 128], F32)
    bufs["lnx_tile"] = [nc.alloc_sbuf_tensor(f"lnxt{i}", [128, D], BF16)
                        for i in range(2)]
    bufs["norm32"] = [nc.alloc_sbuf_tensor(f"norm32_{i}", [128, D], F32)
                      for i in range(2)]
    bufs["f32T"] = [nc.alloc_sbuf_tensor(f"f32T{i}", [128, 8 * 128], F32)
                    for i in range(2)]
    bufs["ones_row"] = nc.alloc_sbuf_tensor("ones_row", [1, 128], BF16)
    bufs["konst"] = nc.alloc_sbuf_tensor("konst", [128, 8], F32)
    bufs["iotabig"] = nc.alloc_sbuf_tensor("iotabig", [128, 16], F32)
    bufs["ones32"] = nc.alloc_sbuf_tensor("ones32", [1, 128], F32)
    bufs["ones_col"] = nc.alloc_sbuf_tensor("ones_col", [128, 1], F32)
    pb = [nc.alloc_psum_tensor(f"PB{i}", [128, 512], F32) for i in range(6)]
    tb = [nc.alloc_psum_tensor(f"PT{i}", [128, 1024], BF16) for i in range(2)]
    em.pbanks = pb
    em.tbanks = tb
    bufs["poolA"] = PsumPool(em, pb[0:4])
    bufs["poolO"] = PsumPool(em, pb[4:6])
    bufs["tp_pool"] = PsumPool(em, tb)
    return bufs


def emit_load_const(em, bufs, eye_dram, konst_dram, konst_dram2):
    cg = em.dma("gpsimd", bufs["eye"][:, :], eye_dram[:, :], sem="cstg")
    em.dma("sync", bufs["eye32"][:, :], eye_dram[:, :], sem="cst")
    em.dma("sync", bufs["konst"][:, :], konst_dram[:, :], sem="cst")
    c2 = em.dma("sync", bufs["iotabig"][:, :], konst_dram2[:, :], sem="cst")
    em.wait("vector", "cst", c2)
    em.wait("vector", "cstg", cg)
    em.wait("tensor", "cst", c2)
    em.wait("tensor", "cstg", cg)
    em.wait("scalar", "cst", c2)
    op = em.on("vector").memset(bufs["ones_row"][0:1, :], 1.0)
    op = em.on("vector").memset(bufs["ones32"][0:1, :], 1.0)
    op = em.on("vector").memset(bufs["ones_col"][:, :], 1.0)
    em.inc(op, "ve")
    return em.cnt["ve"]


def fence(em, engine):
    rec = em.rec[engine]
    m, a, k, op = rec.ops[-1]
    own = em.own[engine]
    for sem, n in op.incs:
        if sem is em.semh[own]:
            em.wait(engine, own, em.cnt[own])
            return
    c = em.inc(op, own)
    em.wait(engine, own, c)


def first_mask(em, bufs, eq_ap, out_ap, tmp_ap, mcol, width):
    V = em.on("vector")
    BIG = 1.0e30
    op = V.scalar_tensor_tensor(out=tmp_ap, in0=eq_ap, scalar=-BIG,
                                in1=bufs["iotabig"][:, 0:width],
                                op0=ALU.mult, op1=ALU.add)
    c = em.inc(op, "ve"); em.wait("vector", "ve", c)
    op = V.tensor_reduce(mcol, tmp_ap, axis=AX.X, op=ALU.min)
    c = em.inc(op, "ve"); em.wait("vector", "ve", c)
    op = V.tensor_scalar(out=out_ap, in0=tmp_ap, scalar1=mcol, scalar2=None,
                         op0=ALU.is_equal)
    c = em.inc(op, "ve"); em.wait("vector", "ve", c)


def pick_max(em, bufs, val_ap, mask_out, tmp_ap, mcols, width):
    V = em.on("vector")
    op = V.reduce_max(mcols[:, 0:1], val_ap, axis=AX.X)
    c = em.inc(op, "ve"); em.wait("vector", "ve", c)
    op = V.tensor_scalar(out=tmp_ap, in0=val_ap, scalar1=mcols[:, 0:1],
                         scalar2=None, op0=ALU.is_equal)
    c = em.inc(op, "ve"); em.wait("vector", "ve", c)
    first_mask(em, bufs, tmp_ap, mask_out, tmp_ap, mcols[:, 1:2], width)


def route_tile(em, bufs, kind, ps, pe_c, cw, E, pool, idx, extras=None):
    V = em.on("vector")
    SC = em.on("scalar")
    lg = bufs["route_lg"]
    t0c = bufs["route_t0"]
    t1c = bufs["route_t1"]
    t2c = bufs["route_t2"]
    colA = bufs["route_cA"]
    BIG = 1.0e30
    em.wait("vector", "pe", pe_c)

    if kind == "argmax1":
        op = V.tensor_copy(out=lg[:, 0:E], in_=ps[:, 0:E])
        V.reduce_max(colA[:, 0:1], ps[:, 0:E], axis=AX.X)
        op = V.reduce_max(colA[:, 1:2], ps[:, 0:E], axis=AX.X, negate=True)
        c = em.inc(op, "ve"); em.wait("vector", "ve", c)
        pool.release(idx, em.rec["vector"].ops[-1][3], "vector")
        op = V.tensor_scalar(out=t0c[:, 0:E], in0=lg[:, 0:E],
                             scalar1=colA[:, 0:1], scalar2=None,
                             op0=ALU.is_equal)
        c = em.inc(op, "ve"); em.wait("vector", "ve", c)
        first_mask(em, bufs, t0c[:, 0:E], t0c[:, 0:E], t1c[:, 0:E],
                   colA[:, 4:5], E)
        em.wait("scalar", "ve", em.cnt["ve"])
        op = SC.activation(out=t2c[:, 0:E], in_=lg[:, 0:E], func=AF.Exp,
                           bias=colA[:, 1:2])
        sc_c = em.inc(op, "sc")
        em.wait("vector", "sc", sc_c)
        op = V.reduce_sum(colA[:, 2:3], t2c[:, 0:E], axis=AX.X)
        c = em.inc(op, "ve"); em.wait("vector", "ve", c)
        op = V.reciprocal(colA[:, 3:4], colA[:, 2:3])
        c = em.inc(op, "ve"); em.wait("vector", "ve", c)
        op = V.tensor_tensor(out=t2c[:, 0:E], in0=t2c[:, 0:E],
                             in1=t0c[:, 0:E], op=ALU.mult)
        c = em.inc(op, "ve"); em.wait("vector", "ve", c)
        op = V.tensor_scalar(out=cw, in0=t2c[:, 0:E], scalar1=colA[:, 3:4],
                             scalar2=None, op0=ALU.mult)
        em.inc(op, "ve")
        if extras and "dbg" in extras:
            d, ti = extras["dbg"], extras["dbg_i"]
            V.tensor_copy(out=d[:, ti * 32 + 0:ti * 32 + 8], in_=lg[:, 0:E])
            V.tensor_copy(out=d[:, ti * 32 + 8:ti * 32 + 16], in_=t0c[:, 0:E])
            V.tensor_copy(out=d[:, ti * 32 + 16:ti * 32 + 24], in_=t2c[:, 0:E])
            op = V.tensor_copy(out=d[:, ti * 32 + 24:ti * 32 + 28], in_=colA[:, 0:4])
            em.inc(op, "ve")

    elif kind == "top2softmax":
        op = V.tensor_copy(out=lg[:, 0:E], in_=ps[:, 0:E])
        c = em.inc(op, "ve"); em.wait("vector", "ve", c)
        pool.release(idx, em.rec["vector"].ops[-1][3], "vector")
        pick_max(em, bufs, lg[:, 0:E], t0c[:, 0:E], t1c[:, 0:E], colA[:, 0:2], E)
        op = V.reduce_max(colA[:, 0:1], lg[:, 0:E], axis=AX.X)
        c = em.inc(op, "ve"); em.wait("vector", "ve", c)
        op = V.scalar_tensor_tensor(out=t1c[:, 0:E], in0=t0c[:, 0:E],
                                    scalar=-BIG, in1=lg[:, 0:E],
                                    op0=ALU.mult, op1=ALU.add)
        c = em.inc(op, "ve"); em.wait("vector", "ve", c)
        op = V.reduce_max(colA[:, 1:2], t1c[:, 0:E], axis=AX.X)
        c = em.inc(op, "ve"); em.wait("vector", "ve", c)
        pick_max(em, bufs, t1c[:, 0:E], t2c[:, 0:E], t1c[:, 16:16 + E],
                 colA[:, 2:4], E)
        op = V.tensor_tensor(out=colA[:, 4:5], in0=colA[:, 0:1],
                             in1=colA[:, 1:2], op=ALU.subtract)
        c = em.inc(op, "ve")
        em.wait("scalar", "ve", c)
        op = SC.activation(out=colA[:, 5:6], in_=colA[:, 4:5], func=AF.Sigmoid)
        sc_c = em.inc(op, "sc")
        em.wait("vector", "sc", sc_c)
        op = V.tensor_scalar(out=colA[:, 6:7], in0=colA[:, 5:6],
                             scalar1=-1.0, scalar2=1.0, op0=ALU.mult,
                             op1=ALU.add)
        c = em.inc(op, "ve"); em.wait("vector", "ve", c)
        op = V.tensor_scalar(out=t0c[:, 0:E], in0=t0c[:, 0:E],
                             scalar1=colA[:, 5:6], scalar2=None, op0=ALU.mult)
        c = em.inc(op, "ve"); em.wait("vector", "ve", c)
        op = V.scalar_tensor_tensor(out=cw, in0=t2c[:, 0:E],
                                    scalar=colA[:, 6:7], in1=t0c[:, 0:E],
                                    op0=ALU.mult, op1=ALU.add)
        em.inc(op, "ve")

    elif kind == "l1group":
        NG, GS, TOPK = 4, 4, 4
        selm = bufs["route_sel"]
        em.wait("scalar", "pe", pe_c)
        op = SC.activation(out=lg[:, 0:E], in_=ps[:, 0:E], func=AF.Sigmoid)
        sc_c = em.inc(op, "sc")
        em.wait("vector", "sc", sc_c)
        pool.release(idx, em.rec["vector"].ops[-1][3], "vector")
        op = V.tensor_tensor(out=t0c[:, 0:E], in0=lg[:, 0:E],
                             in1=extras["bias_rep"][:, 0:E], op=ALU.add)
        c = em.inc(op, "ve"); em.wait("vector", "ve", c)
        for g in range(NG):
            sl = t0c[:, g * GS:(g + 1) * GS]
            op = V.reduce_max(colA[:, 0:1], sl, axis=AX.X)
            c = em.inc(op, "ve"); em.wait("vector", "ve", c)
            pick_max(em, bufs, sl, t1c[:, 0:GS], t1c[:, 8:8 + GS],
                     colA[:, 6:8], GS)
            op = V.scalar_tensor_tensor(out=t1c[:, 0:GS], in0=t1c[:, 0:GS],
                                        scalar=-BIG, in1=sl,
                                        op0=ALU.mult, op1=ALU.add)
            c = em.inc(op, "ve"); em.wait("vector", "ve", c)
            op = V.reduce_max(colA[:, 1:2], t1c[:, 0:GS], axis=AX.X)
            c = em.inc(op, "ve"); em.wait("vector", "ve", c)
            op = V.tensor_tensor(out=colA[:, 2 + g:3 + g], in0=colA[:, 0:1],
                                 in1=colA[:, 1:2], op=ALU.add)
            c = em.inc(op, "ve"); em.wait("vector", "ve", c)
        gs_ap = colA[:, 2:6]
        pick_max(em, bufs, gs_ap, t2c[:, 0:4], t1c[:, 0:4], colA[:, 6:8], 4)
        op = V.scalar_tensor_tensor(out=t1c[:, 0:4], in0=t2c[:, 0:4],
                                    scalar=-BIG, in1=gs_ap,
                                    op0=ALU.mult, op1=ALU.add)
        c = em.inc(op, "ve"); em.wait("vector", "ve", c)
        pick_max(em, bufs, t1c[:, 0:4], t1c[:, 4:8], t1c[:, 8:12],
                 colA[:, 6:8], 4)
        op = V.tensor_tensor(out=t2c[:, 0:4], in0=t2c[:, 0:4],
                             in1=t1c[:, 4:8], op=ALU.max)
        c = em.inc(op, "ve"); em.wait("vector", "ve", c)
        for g in range(NG):
            V.tensor_scalar(out=t1c[:, g * GS:(g + 1) * GS],
                            in0=t0c[:, g * GS:(g + 1) * GS],
                            scalar1=t2c[:, g:g + 1], scalar2=None,
                            op0=ALU.mult)
        V.memset(selm[:, 0:E], 0.0)
        fence(em, "vector")
        for kk in range(TOPK):
            pick_max(em, bufs, t1c[:, 0:E], t2c[:, 0:E], t2c[:, 16:16 + E],
                     colA[:, 6:8], E)
            V.tensor_tensor(out=selm[:, 0:E], in0=selm[:, 0:E],
                            in1=t2c[:, 0:E], op=ALU.max)
            op = V.scalar_tensor_tensor(out=t1c[:, 0:E], in0=t2c[:, 0:E],
                                        scalar=-BIG, in1=t1c[:, 0:E],
                                        op0=ALU.mult, op1=ALU.add)
            c = em.inc(op, "ve"); em.wait("vector", "ve", c)
        op = V.tensor_tensor(out=t0c[:, 0:E], in0=lg[:, 0:E],
                             in1=selm[:, 0:E], op=ALU.mult)
        c = em.inc(op, "ve"); em.wait("vector", "ve", c)
        op = V.reduce_sum(colA[:, 0:1], t0c[:, 0:E], axis=AX.X)
        c = em.inc(op, "ve"); em.wait("vector", "ve", c)
        V.tensor_scalar(out=colA[:, 0:1], in0=colA[:, 0:1], scalar1=1e-20,
                        scalar2=None, op0=ALU.add)
        fence(em, "vector")
        op = V.reciprocal(colA[:, 1:2], colA[:, 0:1])
        c = em.inc(op, "ve"); em.wait("vector", "ve", c)
        op = V.tensor_scalar(out=cw, in0=t0c[:, 0:E], scalar1=colA[:, 1:2],
                             scalar2=2.5, op0=ALU.mult, op1=ALU.mult)
        em.inc(op, "ve")
    else:
        raise ValueError(kind)



def alloc_route_bufs(em):
    nc = em.nc
    b = {}
    b["route_lg"] = nc.alloc_sbuf_tensor("route_lg", [128, 16], F32)
    b["route_t0"] = nc.alloc_sbuf_tensor("route_t0", [128, 32], F32)
    b["route_t1"] = nc.alloc_sbuf_tensor("route_t1", [128, 32], F32)
    b["route_t2"] = nc.alloc_sbuf_tensor("route_t2", [128, 32], F32)
    b["route_cA"] = nc.alloc_sbuf_tensor("route_cA", [128, 8], F32)
    b["route_zr"] = nc.alloc_sbuf_tensor("route_zr", [128, 16], F32)
    b["route_sel"] = nc.alloc_sbuf_tensor("route_sel", [128, 16], F32)
    op = em.on("vector").memset(b["route_zr"][:, :], 0.0)
    em.inc(op, "ve")
    return b


def emit_ffn_block(em, bufs, xin, xout, cfg):
    nc = em.nc
    M = cfg["macro"]
    E = cfg["E"]
    nmac = T // M
    ntile_m = M // 128
    nts = M // 512
    lnxT = cfg["lnxT"]
    acc = cfg["acc"]
    cwb = cfg["cw"]
    hT = cfg["hT"]
    w1b, w2b = cfg["w1b"], cfg["w2b"]
    w3b = cfg.get("w3b")
    scrA = cfg["scrA"]
    poolA, poolO = cfg["poolA"], cfg["poolO"]
    ones = bufs["ones_row"]
    TE = em.on("tensor")
    V = em.on("vector")
    SC = em.on("scalar")

    HU = cfg["HU"]
    w_last_pe = {"w1": [0, 0], "w2": [0, 0], "w3": [0, 0]}
    hT_last_pe = [0, 0]
    uctr = 0
    scrA_last_ve = [0, 0]
    xo = cfg["xo"]
    xo_store = [0, 0]

    for mi in range(nmac):
        em.barrier()
        ln_hist = emit_ln_sweep(em, bufs, xin, lnxT, M, mi * M, M,
                                router=cfg["router"])
        fence(em, "vector")
        for ui, u in enumerate(units_list := cfg["units"]):
            hid = u["hid"]
            nhc = hid // 128
            sl = uctr % 2
            uctr += 1
            w1o = sl * 8 * HU
            w2o = sl * (HU // 128) * D
            hto = sl * (HU // 128) * 512
            em.wait("gpsimd", "pe", w_last_pe["w1"][sl])
            c1 = em.dma("gpsimd",
                        w1b.ap()[:, w1o:w1o + 8 * hid].rearrange(
                            "p (c h) -> p c h", c=8),
                        u["w1"].rearrange("(c p) h -> p c h", p=128),
                        sem=f"w1s{sl}")
            em.wait("gpsimd", "pe", w_last_pe["w2"][sl])
            c2 = em.dma("gpsimd",
                        w2b.ap()[:, w2o:w2o + nhc * D].rearrange(
                            "p (c d) -> p c d", c=nhc),
                        u["w2"].rearrange("(c p) d -> p c d", p=128),
                        sem=f"w2s{sl}")
            if u.get("w3") is not None:
                em.wait("gpsimd", "pe", w_last_pe["w3"][sl])
                c3 = em.dma("gpsimd",
                            w3b.ap()[:, w1o:w1o + 8 * hid].rearrange(
                                "p (c h) -> p c h", c=8),
                            u["w3"].rearrange("(c p) h -> p c h", p=128),
                            sem=f"w3s{sl}")
                em.wait("tensor", f"w3s{sl}", c3)
            em.wait("tensor", f"w1s{sl}", c1)
            em.wait("tensor", f"w2s{sl}", c2)
            for ts in range(nts):
                for hc in range(nhc):
                    idx, ps = poolA.acquire()
                    for kc in range(8):
                        op = TE.matmul(
                            ps[:, :],
                            w1b[:, w1o + kc * hid + hc * 128:
                                w1o + kc * hid + hc * 128 + 128],
                            lnxT[:, kc * M + ts * 512: kc * M + ts * 512 + 512],
                            start=(kc == 0), stop=(kc == 7))
                    pe_c = em.inc(op, "pe")
                    w_last_pe["w1"][sl] = pe_c
                    hcol = u["hid0"] + hc * 128
                    if u["act"] == "gelu":
                        em.wait("scalar", "pe", pe_c)
                        em.wait("scalar", "pe", hT_last_pe[sl])
                        kw = {}
                        if u.get("b1") is not None:
                            bt, be, bE = u["b1"]
                            bc = (hcol // 128) * bE + be
                            kw["bias"] = bt[:, bc:bc + 1]
                        op = SC.activation(
                            out=hT[:, hto + hc * 512:hto + (hc + 1) * 512],
                            in_=ps[:, :], func=AF.Gelu, **kw)
                        poolA.release(idx, op, "scalar")
                    elif u["act"] == "relu2":
                        sb = (ts * nhc + hc) % 2
                        em.wait("scalar", "ve", scrA_last_ve[sb])
                        em.wait("scalar", "pe", pe_c)
                        op = SC.activation(out=scrA[sb][:, :], in_=ps[:, :],
                                           func=AF.Relu)
                        sc_c = poolA.release(idx, op, "scalar")
                        em.wait("vector", "sc", sc_c)
                        em.wait("vector", "pe", hT_last_pe[sl])
                        op = V.tensor_tensor(
                            out=hT[:, hto + hc * 512:hto + (hc + 1) * 512],
                            in0=scrA[sb][:, :],
                            in1=scrA[sb][:, :], op=ALU.mult)
                        scrA_last_ve[sb] = em.inc(op, "ve")
                    elif u["act"] == "silu_gate":
                        idx2, ps2 = poolA.acquire()
                        for kc in range(8):
                            op = TE.matmul(
                                ps2[:, :],
                                w3b[:, w1o + kc * hid + hc * 128:
                                    w1o + kc * hid + hc * 128 + 128],
                                lnxT[:, kc * M + ts * 512: kc * M + ts * 512 + 512],
                                start=(kc == 0), stop=(kc == 7))
                        pe_c = em.inc(op, "pe")
                        w_last_pe["w3"][sl] = pe_c
                        sb = (ts * nhc + hc) % 2
                        em.wait("scalar", "ve", scrA_last_ve[sb])
                        em.wait("scalar", "pe", pe_c)
                        op = SC.activation(out=scrA[sb][:, :], in_=ps[:, :],
                                           func=AF.Silu)
                        sc_c = poolA.release(idx, op, "scalar")
                        em.wait("vector", "sc", sc_c)
                        em.wait("vector", "pe", hT_last_pe[sl])
                        op = V.tensor_tensor(
                            out=hT[:, hto + hc * 512:hto + (hc + 1) * 512],
                            in0=scrA[sb][:, :], in1=ps2[:, :],
                            op=ALU.mult)
                        scrA_last_ve[sb] = poolA.release(idx2, op, "vector")
                    else:
                        raise ValueError(u["act"])
                if u["act"] == "gelu":
                    em.wait("tensor", "sc", em.cnt["sc"])
                else:
                    em.wait("tensor", "ve", em.cnt["ve"])
                for tt in range(4):
                    tt_g = ts * 4 + tt
                    for dn in range(2):
                        idxo, pso = poolO.acquire()
                        nmm = nhc + (1 if u.get("b2") is not None else 0)
                        k = 0
                        for hc in range(nhc):
                            op = TE.matmul(
                                pso[:, :],
                                hT[:, hto + hc * 512 + tt * 128:
                                    hto + hc * 512 + tt * 128 + 128],
                                w2b[:, w2o + hc * D + dn * 512:
                                    w2o + hc * D + dn * 512 + 512],
                                start=(k == 0), stop=(k == nmm - 1))
                            k += 1
                        if u.get("b2") is not None:
                            op = TE.matmul(pso[:, :], ones[0:1, :],
                                           u["b2"][0:1, dn * 512:dn * 512 + 512],
                                           start=False, stop=True)
                            k += 1
                        pe_c = em.inc(op, "pe")
                        w_last_pe["w2"][sl] = pe_c
                        hT_last_pe[sl] = pe_c
                        em.wait("vector", "pe", pe_c)
                        a_sl = acc[:, tt_g * D + dn * 512: tt_g * D + dn * 512 + 512]
                        if u["e"] is not None:
                            cwcol = cwb[:, tt_g * E + u["e"]: tt_g * E + u["e"] + 1]
                            if ui == 0:
                                op = V.tensor_scalar(out=a_sl, in0=pso[:, :],
                                                     scalar1=cwcol, scalar2=None,
                                                     op0=ALU.mult)
                            else:
                                op = V.scalar_tensor_tensor(
                                    out=a_sl, in0=pso[:, :], scalar=cwcol,
                                    in1=a_sl, op0=ALU.mult, op1=ALU.add)
                        else:
                            if ui == 0:
                                op = V.tensor_copy(out=a_sl, in_=pso[:, :])
                            else:
                                op = V.tensor_tensor(out=a_sl, in0=pso[:, :],
                                                     in1=a_sl, op=ALU.add)
                        poolO.release(idxo, op, "vector")
        xt_used = [0, 0]
        for tt_g in range(ntile_m):
            b = tt_g % 2
            t0 = mi * M + tt_g * 128
            em.wait("sync", "ve", xt_used[b])
            em.wait("sync", "sc", ln_hist[b])
            c = em.dma("sync", bufs["xt"][b][:, :], xin[t0:t0 + 128, :],
                       sem=f"xa{b}")
            em.wait("vector", f"xa{b}", c)
            em.wait("vector", f"xs{b}", xo_store[b])
            op = V.tensor_tensor(out=xo[b][:, :], in0=bufs["xt"][b][:, :],
                                 in1=acc[:, tt_g * D:(tt_g + 1) * D], op=ALU.add)
            ve_c = em.inc(op, "ve")
            xt_used[b] = ve_c
            em.wait("sync", "ve", ve_c)
            xo_store[b] = em.dma("sync", xout[t0:t0 + 128, :], xo[b][:, :],
                                 sem=f"xs{b}")


def setup_ffn_generic(em, bufs, xin, xout, name, spec, decls):
    nc = em.nc
    M = spec["macro"]
    E = spec["E"]
    HU = spec["hid_unit"]
    lnxT = em.salloc(f"{name}_lnxT", [128, 8 * M], BF16)
    acc = em.salloc(f"{name}_acc", [128, (M // 128) * D], F32)
    cw = em.salloc(f"{name}_cw", [128, (M // 128) * E], F32)
    hT = em.salloc(f"{name}_hT", [128, 2 * (HU // 128) * 512], BF16)
    w1b = em.salloc(f"{name}_w1b", [128, 2 * 8 * HU], BF16)
    w2b = em.salloc(f"{name}_w2b", [128, 2 * (HU // 128) * D], BF16)
    w3b = None
    if spec.get("has_w3"):
        w3b = em.salloc(f"{name}_w3b", [128, 2 * 8 * HU], BF16)
    scrA = [em.salloc(f"{name}_scrA{i}", [128, 512], F32)
            for i in range(2)]
    xo = [em.salloc(f"{name}_xo{i}", [128, D], F32) for i in range(2)]
    poolA = bufs["poolA"]
    poolO = bufs["poolO"]

    cstS = f"cst_{name}"
    cstG = f"cstg_{name}"
    gate = em.salloc(f"{name}_gate", [128, 8 * E], F32)
    gname = decls.dram(em, f"{name}_gatew", [D, E])
    c = em.dma("sync", gate.ap().rearrange("p (c e) -> p c e", c=8),
               gname.rearrange("(c p) e -> p c e", p=128), sem=cstS)
    gbias = None
    if spec.get("gate_bias"):
        gbias = em.salloc(f"{name}_gbias", [1, E], F32)
        gb = decls.dram(em, f"{name}_gatebias", [1, E])
        c = max(c, em.dma("sync", gbias[0:1, :], gb[0:1, :], sem=cstS))
    c = 0
    b1t = None
    if spec.get("has_b1"):
        nhc_tot = spec["b1_len"] // 128
        b1t = em.salloc(f"{name}_b1ts", [128, nhc_tot * spec["b1_E"]], F32)
        bh = decls.dram(em, f"{name}_b1t", [spec["b1_len"], spec["b1_E"]])
        c = max(c, em.dma("gpsimd",
                          b1t.ap().rearrange("p (c e) -> p c e", c=nhc_tot),
                          bh.rearrange("(c p) e -> p c e", p=128), sem=cstG))
    b2r = None
    if spec.get("has_b2"):
        nb2 = spec["b2_n"]
        b2r = em.salloc(f"{name}_b2rs", [1, nb2 * D], BF16)
        bh = decls.dram(em, f"{name}_b2r", [1, nb2 * D])
        c = max(c, em.dma("gpsimd", b2r[0:1, :], bh[0:1, :], sem=cstG))
    bias_rep = None
    if spec.get("routing") == "l1group":
        bias_rep = em.salloc(f"{name}_brep", [128, E], F32)
        brow = em.salloc(f"{name}_brow", [1, E], F32)
        bh = decls.dram(em, f"{name}_biase", [1, E])
        cs = em.dma("sync", brow[0:1, :], bh[0:1, :], sem=cstS)
        em.wait("tensor", cstS, em.cnt[cstS])
        em.wait("tensor", "ve", em.cnt["ve"])
        idx, ps = poolA.acquire()
        op = em.on("tensor").matmul(ps[:, 0:E], bufs["ones32"][0:1, :],
                                    brow[0:1, :], start=True, stop=True)
        pe_c = em.inc(op, "pe")
        em.wait("vector", "pe", pe_c)
        op = em.on("vector").tensor_copy(out=bias_rep[:, :], in_=ps[:, 0:E])
        poolA.release(idx, op, "vector")
    em.wait("tensor", cstS, em.cnt.get(cstS, 0))
    if cstG in em.cnt:
        em.wait("tensor", cstG, em.cnt[cstG])
        em.wait("scalar", cstG, em.cnt[cstG])

    units = []
    for ud in spec["units_def"]:
        u = dict(ud)
        u["w1"] = decls.dram_ap(em, u.pop("w1name"))
        u["w2"] = decls.dram_ap(em, u.pop("w2name"))
        if "w3name" in u:
            u["w3"] = decls.dram_ap(em, u.pop("w3name"))
        if u.pop("use_b1", False):
            u["b1"] = (b1t, u.pop("b1col"), spec["b1_E"])
        if "b2slice" in u:
            s0 = u.pop("b2slice")
            u["b2"] = b2r[0:1, s0 * D:(s0 + 1) * D]
        units.append(u)

    if DEBUG_DUMP is not None and globals().get("DEBUG_ON"):
        DEBUG_DUMP.append((f"{name}_cw", cw, [128, (M // 128) * E]))
    router = dict(kind=spec["routing"], E=E, gate32=gate, gbias32=gbias,
                  cw=cw, extras={"bias_rep": bias_rep}
                  if bias_rep is not None else None)

    cfg = dict(macro=M, E=E, HU=HU, lnxT=lnxT, acc=acc, cw=cw, hT=hT,
               w1b=w1b, w2b=w2b, w3b=w3b, scrA=scrA, xo=xo, poolA=poolA,
               poolO=poolO, units=units, router=router)
    emit_ffn_block(em, bufs, xin, xout, cfg)


class Decls:

    def __init__(self):
        self.shapes = {}
        self.aps = {}

    def dram(self, em, name, shape):
        if name not in self.aps:
            self.shapes[name] = list(shape)
            self.aps[name] = em.nc.dram_tensor(
                name, list(shape), F32, kind="ExternalInput").ap()
        return self.aps[name]

    def register(self, em, name, shape):
        return self.dram(em, name, shape)

    def dram_ap(self, em, spec):
        name, shape, slicer = spec
        ap = self.dram(em, name, shape)
        return slicer(ap)


def ffn_spec(block):
    if block == "l0f":
        units = []
        for e in range(8):
            for h in range(2):
                units.append(dict(
                    e=e, hid=512, hid0=h * 512, act="gelu", use_b1=True,
                    b1col=e, **(dict(b2slice=e) if h == 1 else {}),
                    w1name=("l0f_W1", [8 * D, 1024],
                            lambda ap, e=e, h=h:
                            ap[e * D:(e + 1) * D, h * 512:(h + 1) * 512]),
                    w2name=("l0f_W2", [8 * 1024, D],
                            lambda ap, e=e, h=h:
                            ap[e * 1024 + h * 512: e * 1024 + (h + 1) * 512, :]),
                ))
        return dict(E=8, macro=2048, hid_unit=512, units_def=units,
                    routing="argmax1", gate_bias=True, has_b1=True,
                    b1_len=1024, b1_E=8, has_b2=True, b2_n=8)
    if block == "l1f":
        units = []
        for e in range(16):
            units.append(dict(
                e=e, hid=512, hid0=0, act="relu2",
                w1name=("l1f_up", [16 * D, 512], lambda ap, e=e: ap[e * D:(e + 1) * D, :]),
                w2name=("l1f_dn", [16 * 512, D], lambda ap, e=e: ap[e * 512:(e + 1) * 512, :]),
            ))
        for h in range(4):
            units.append(dict(
                e=None, hid=512, hid0=h * 512, act="relu2",
                w1name=("l1f_sup", [D, 2048],
                        lambda ap, h=h: ap[:, h * 512:(h + 1) * 512]),
                w2name=("l1f_sdn", [2048, D],
                        lambda ap, h=h: ap[h * 512:(h + 1) * 512, :]),
            ))
        return dict(E=16, macro=2048, hid_unit=512, units_def=units,
                    routing="l1group", gate_bias=False)
    if block == "l2f":
        units = []
        for e in range(8):
            for h in range(8):
                units.append(dict(
                    e=e, hid=512, hid0=h * 512, act="gelu", use_b1=True,
                    b1col=e, **(dict(b2slice=e) if h == 7 else {}),
                    w1name=("l2f_w1", [8 * D, 4096],
                            lambda ap, e=e, h=h:
                            ap[e * D:(e + 1) * D, h * 512:(h + 1) * 512]),
                    w2name=("l2f_w2", [8 * 4096, D],
                            lambda ap, e=e, h=h:
                            ap[e * 4096 + h * 512: e * 4096 + (h + 1) * 512, :]),
                ))
        for h in range(8):
            units.append(dict(
                e=None, hid=512, hid0=h * 512, act="gelu", use_b1=True,
                b1col=8, **(dict(b2slice=8) if h == 7 else {}),
                w1name=("l2f_ws1", [D, 4096],
                        lambda ap, h=h: ap[:, h * 512:(h + 1) * 512]),
                w2name=("l2f_ws2", [4096, D],
                        lambda ap, h=h: ap[h * 512:(h + 1) * 512, :]),
            ))
        return dict(E=8, macro=2048, hid_unit=512, units_def=units,
                    routing="top2softmax", gate_bias=True, has_b1=True,
                    b1_len=4096, b1_E=9, has_b2=True, b2_n=9)
    if block == "l3f":
        units = []
        for e in range(8):
            for h in range(4):
                units.append(dict(
                    e=e, hid=512, hid0=h * 512, act="silu_gate",
                    w1name=("l3f_w1", [8 * D, 2048],
                            lambda ap, e=e, h=h:
                            ap[e * D:(e + 1) * D, h * 512:(h + 1) * 512]),
                    w3name=("l3f_w3", [8 * D, 2048],
                            lambda ap, e=e, h=h:
                            ap[e * D:(e + 1) * D, h * 512:(h + 1) * 512]),
                    w2name=("l3f_w2", [8 * 2048, D],
                            lambda ap, e=e, h=h:
                            ap[e * 2048 + h * 512: e * 2048 + (h + 1) * 512, :]),
                ))
        for h in range(4):
            units.append(dict(
                e=None, hid=512, hid0=h * 512, act="silu_gate",
                w1name=("l3f_sw1", [D, 2048],
                        lambda ap, h=h: ap[:, h * 512:(h + 1) * 512]),
                w3name=("l3f_sw3", [D, 2048],
                        lambda ap, h=h: ap[:, h * 512:(h + 1) * 512]),
                w2name=("l3f_sw2", [2048, D],
                        lambda ap, h=h: ap[h * 512:(h + 1) * 512, :]),
            ))
        return dict(E=8, macro=2048, hid_unit=512, units_def=units,
                    routing="top2softmax", gate_bias=True, has_w3=True)
    raise ValueError(block)


def ffn_host_arrays(block, inp):
    a = {}
    if block == "l0f":
        a["l0f_gatew"] = inp["L0_ffn_Wg"]
        a["l0f_gatebias"] = inp["L0_ffn_bg"].reshape(1, 8)
        a["l0f_W1"] = inp["L0_ffn_W1"].reshape(8 * D, 1024)
        a["l0f_W2"] = inp["L0_ffn_W2"].reshape(8 * 1024, D)
        a["l0f_b1t"] = np.ascontiguousarray(inp["L0_ffn_b1"].T)
        a["l0f_b2r"] = inp["L0_ffn_b2"].reshape(1, 8 * D)
    elif block == "l1f":
        a["l1f_gatew"] = np.ascontiguousarray(inp["L1_ffn_gate_w"].T)
        a["l1f_biase"] = inp["L1_ffn_bias_e"].reshape(1, 16)
        a["l1f_up"] = np.ascontiguousarray(
            inp["L1_ffn_up_w"].transpose(0, 2, 1)).reshape(16 * D, 512)
        a["l1f_dn"] = np.ascontiguousarray(
            inp["L1_ffn_down_w"].transpose(0, 2, 1)).reshape(16 * 512, D)
        a["l1f_sup"] = np.ascontiguousarray(inp["L1_ffn_shared_up"].T)
        a["l1f_sdn"] = np.ascontiguousarray(inp["L1_ffn_shared_down"].T)
    elif block == "l2f":
        a["l2f_gatew"] = inp["L2_ffn_gate_w"]
        a["l2f_gatebias"] = inp["L2_ffn_gate_b"].reshape(1, 8)
        a["l2f_w1"] = inp["L2_ffn_w1"].reshape(8 * D, 4096)
        a["l2f_w2"] = inp["L2_ffn_w2"].reshape(8 * 4096, D)
        a["l2f_ws1"] = inp["L2_ffn_ws1"]
        a["l2f_ws2"] = inp["L2_ffn_ws2"]
        b1t = np.concatenate(
            [np.ascontiguousarray(inp["L2_ffn_b1"].T),
             inp["L2_ffn_bs1"].reshape(4096, 1)], axis=1)
        a["l2f_b1t"] = np.ascontiguousarray(b1t)
        b2r = np.concatenate(
            [inp["L2_ffn_b2"].reshape(8 * D), inp["L2_ffn_bs2"]], axis=0)
        a["l2f_b2r"] = b2r.reshape(1, 9 * D)
    elif block == "l3f":
        a["l3f_gatew"] = np.ascontiguousarray(inp["L3_ffn_router_w"].T)
        a["l3f_gatebias"] = inp["L3_ffn_experts_bias"].reshape(1, 8)
        a["l3f_w1"] = inp["L3_ffn_w1"].reshape(8 * D, 2048)
        a["l3f_w3"] = inp["L3_ffn_w3"].reshape(8 * D, 2048)
        a["l3f_w2"] = inp["L3_ffn_w2"].reshape(8 * 2048, D)
        a["l3f_sw1"] = inp["L3_ffn_sw1"].reshape(D, 2048)
        a["l3f_sw3"] = inp["L3_ffn_sw3"].reshape(D, 2048)
        a["l3f_sw2"] = inp["L3_ffn_sw2"].reshape(2048, D)
    return a


FFN_BLOCKS = {"l0f", "l1f", "l2f", "l3f"}


def build_program(blocks):
    nc = bass.Bass("TRN2", target_bir_lowering=False,
                   detect_race_conditions=globals().get("DETECT_RACES", True))
    es = contextlib.ExitStack()
    em = EM(nc, es)
    decls = Decls()
    xin = nc.dram_tensor("XIN", [T, D], F32, kind="ExternalInput").ap()
    xout = nc.dram_tensor("XOUT", [T, D], F32, kind="ExternalOutput").ap()
    eye_dram = decls.dram(em, "CEYE", [128, 128])
    konst_dram = decls.dram(em, "CONSTS", [128, 8])
    konst2_dram = decls.dram(em, "CIOTA", [128, 16])
    xs = [xin]
    for i in range(len(blocks) - 1):
        xs.append(nc.dram_tensor(f"XSCR{i}", [T, D], F32, kind="Internal").ap())
    xs.append(xout)

    bufs = alloc_common(em, 0)
    bufs.update(alloc_route_bufs(em))
    if globals().get("DEBUG_ON"):
        bufs["route_dbg"] = em.salloc("route_dbg", [128, NTILE * 32], F32)
        DEBUG_DUMP.append(("route_dbg", bufs["route_dbg"], [128, NTILE * 32]))
    emit_load_const(em, bufs, eye_dram, konst_dram, konst2_dram)

    for i, b in enumerate(blocks):
        em.barrier()
        with contextlib.ExitStack() as blk_es:
            em.cur_es = blk_es
            if b in FFN_BLOCKS:
                setup_ffn_generic(em, bufs, xs[i], xs[i + 1], b,
                                  ffn_spec(b), decls)
            else:
                ATTN_SETUP[b](em, bufs, xs[i], xs[i + 1], decls)
    emit_debug_dumps(em)
    em.barrier()
    em.finish()
    return nc, es, decls


def host_arrays_for(blocks, inp):
    a = {"CEYE": np.eye(128, dtype=np.float32),
         "CONSTS": np.tile(np.array(
             [1e-5, 1e-12, 1.0, 0.0, 0.0, 0.0, 0.0, 0.0], np.float32),
             (128, 1)),
         "CIOTA": np.tile(np.arange(16, dtype=np.float32), (128, 1))}
    for b in blocks:
        if b in FFN_BLOCKS:
            a.update(ffn_host_arrays(b, inp))
        else:
            a.update(ATTN_HOST[b](inp))
    return a


ATTN_SETUP = {}
ATTN_HOST = {}
DEBUG_DUMP = []


def emit_debug_dumps(em):
    nc = em.nc
    for name, t, shape in DEBUG_DUMP:
        o = nc.dram_tensor(f"DBG_{name}", shape, F32, kind="ExternalOutput").ap()
        em.dma("sync", o[:, :], t[0:shape[0], 0:shape[1]])


def check_deadlock(em, verbose=False):
    pcs = {e: 0 for e in ENGINES}
    cnt = {name: 0 for name in em.semh}
    by_id = {id(h): n for n, h in em.semh.items()}
    total = sum(len(em.rec[e].ops) for e in ENGINES)
    progress = True
    while progress:
        progress = False
        for e in ENGINES:
            ops = em.rec[e].ops
            while pcs[e] < len(ops):
                m, a, k, op = ops[pcs[e]]
                if m == "wait_ge":
                    sem, val = a[0], a[1]
                    if cnt[by_id[id(sem)]] < val:
                        break
                for sem, n in op.incs:
                    cnt[by_id[id(sem)]] += n
                pcs[e] += 1
                progress = True
    stuck = {e: (pcs[e], len(em.rec[e].ops)) for e in ENGINES
             if pcs[e] < len(em.rec[e].ops)}
    if stuck:
        print("DEADLOCK at:", stuck)
        for e, (pc, n) in stuck.items():
            m, a, k, op = em.rec[e].ops[pc]
            if m == "wait_ge":
                nm = by_id[id(a[0])]
                print(f"  {e}[{pc}] waits {nm} >= {a[1]}, have {cnt[nm]}")
            else:
                print(f"  {e}[{pc}] op {m}")
        return False
    if verbose:
        print(f"no deadlock; {total} ops drained")
    return True


def setup_l3_attn(em, bufs, xin, xout, decls):
    nc = em.nc
    name = "l3a"
    C = 128
    NCH = T // C
    NH, HD = 16, 64
    V = em.on("vector")
    SC = em.on("scalar")
    TE = em.on("tensor")
    poolA, poolO = bufs["poolA"], bufs["poolO"]

    MH = T // 4
    lnxT = em.salloc(f"{name}_lnxT", [128, 8 * MH], BF16)
    wq = em.salloc(f"{name}_wq", [128, 8 * D], BF16)
    wk = em.salloc(f"{name}_wk", [128, 8 * D], BF16)
    wv = em.salloc(f"{name}_wv", [128, 8 * D], BF16)
    wo = em.salloc(f"{name}_wo", [128, 8 * D], BF16)
    qT = em.salloc(f"{name}_qT", [64, 16 * C], BF16)
    kT = em.salloc(f"{name}_kT", [64, 16 * C], BF16)
    vtok = em.salloc(f"{name}_vtok", [128, D], BF16)
    ktok = em.salloc(f"{name}_ktok", [128, D], BF16)
    kw = em.salloc(f"{name}_kw", [128, D], BF16)
    qgT = em.salloc(f"{name}_qgT", [64, 16 * C], BF16)
    PT = em.salloc(f"{name}_PT", [128, 16 * C], BF16)
    ybuf = [em.salloc(f"{name}_ybuf{i}", [64, 16 * C], BF16)
            for i in range(2)]
    dloc = em.salloc(f"{name}_dlocs", [128, 16 * C], F32)
    gp = em.salloc(f"{name}_gps", [64, 16 * C], F32)
    gwc = em.salloc(f"{name}_gwcs", [128, 16], F32)
    st32 = em.salloc(f"{name}_st32", [64, 16 * HD], F32)
    stbf = em.salloc(f"{name}_stbf", [64, 16 * HD], BF16)
    ysum = em.salloc(f"{name}_ysum", [64, 16 * NCH], F32)
    ysq = em.salloc(f"{name}_ysq", [64, 16 * NCH], F32)
    yscr = em.salloc(f"{name}_yscr", [64, C], F32)
    scols = em.salloc(f"{name}_scols", [64, 32], F32)
    hstat = em.salloc(f"{name}_hstat", [32, 4], F32)
    hrow = em.salloc(f"{name}_hrow", [1, 64], F32)
    hrep = em.salloc(f"{name}_hrep", [128, 64], F32)
    abcol = em.salloc(f"{name}_abcol", [128, 32], F32)
    gnwb = em.salloc(f"{name}_gnwbs", [128, 16], F32)
    ynT = em.salloc(f"{name}_ynT", [128, 8 * 512], BF16)
    ynS = [em.salloc(f"{name}_ynS{i}", [128, 512], BF16)
           for i in range(2)]
    ysp = nc.dram_tensor(f"{name}_YSP", [8 * 128, T], BF16, kind="Internal").ap()
    xo = [em.salloc(f"{name}_xo{i}", [128, D], F32) for i in range(2)]

    cstS, cstG = f"cst_{name}", f"cstg_{name}"
    for wbuf, nm, shape in ((wq, "l3a_wqT", [D, D]), (wk, "l3a_wkT", [D, D]),
                            (wv, "l3a_wvT", [D, D]), (wo, "l3a_woT", [D, D])):
        h = decls.dram(em, nm, shape)
        em.dma("gpsimd", wbuf.ap().rearrange("p (c d) -> p c d", c=8),
               h.rearrange("(c p) d -> p c d", p=128), sem=cstG)
    h = decls.dram(em, "l3a_dloc", [128, 16 * C])
    em.dma("sync", dloc[:, :], h[:, :], sem=cstS)
    h = decls.dram(em, "l3a_gp", [64, 16 * C])
    em.dma("sync", gp[0:64, :], h[:, :], sem=cstS)
    h = decls.dram(em, "l3a_gw", [128, 16])
    em.dma("sync", gwc[:, :], h[:, :], sem=cstS)
    h = decls.dram(em, "l3a_gnwb", [128, 16])
    cS = em.dma("sync", gnwb[:, :], h[:, :], sem=cstS)
    em.wait("tensor", cstS, em.cnt[cstS])
    em.wait("tensor", cstG, em.cnt[cstG])
    em.wait("vector", cstS, em.cnt[cstS])

    op = V.memset(st32[0:64, :], 0.0)
    op = V.memset(stbf[0:64, :], 0.0)
    op = V.memset(scols[0:64, :], 0.0)
    st_ready = em.inc(op, "ve")
    em.wait("tensor", "ve", st_ready)

    GAM = [1.0 - 2.0 ** (-5.0 - h) for h in range(NH)]
    ysp_v = ysp.rearrange("(h p) t -> p h t", p=64)

    proj_free = {"qT": 0, "kT": 0, "vtok": 0, "ktok": 0, "PT": 0}
    ybuf_st = [0, 0]
    ln_hist = [0, 0]
    for c in range(NCH):
        if c % (MH // C) == 0:
            em.barrier()
            ln_hist = emit_ln_sweep(em, bufs, xin, lnxT,
                                    MH, (c // (MH // C)) * MH, MH)
            em.wait("tensor", "ve", em.cnt["ve"])
        tok0 = (c * C) % MH
        for dst, wsrc, key in ((qT, wq, "qT"), (kT, wk, "kT")):
            for hh in range(NH):
                idx, ps = poolA.acquire()
                for kc in range(8):
                    op = TE.matmul(
                        ps[0:64, 0:C],
                        wsrc[:, kc * D + hh * HD: kc * D + hh * HD + HD],
                        lnxT[:, kc * MH + tok0: kc * MH + tok0 + C],
                        start=(kc == 0), stop=(kc == 7))
                pe_c = em.inc(op, "pe")
                em.wait("vector", "pe", pe_c)
                em.wait("vector", "pe", proj_free[key])
                op = V.tensor_copy(out=dst[0:64, hh * C:(hh + 1) * C],
                                   in_=ps[0:64, 0:C])
                poolA.release(idx, op, "vector")
        for dst, wsrc, key in ((vtok, wv, "vtok"), (ktok, wk, "ktok")):
            for dn in range(2):
                idx, ps = poolO.acquire()
                for kc in range(8):
                    op = TE.matmul(
                        ps[:, :],
                        lnxT[:, kc * MH + tok0: kc * MH + tok0 + C],
                        wsrc[:, kc * D + dn * 512: kc * D + dn * 512 + 512],
                        start=(kc == 0), stop=(kc == 7))
                pe_c = em.inc(op, "pe")
                em.wait("vector", "pe", pe_c)
                em.wait("vector", "pe", proj_free[key])
                op = V.tensor_copy(out=dst[:, dn * 512:(dn + 1) * 512],
                                   in_=ps[:, :])
                poolO.release(idx, op, "vector")
        for h in range(NH):
            V.tensor_scalar(out=kw[:, h * HD:(h + 1) * HD],
                            in0=ktok[:, h * HD:(h + 1) * HD],
                            scalar1=gwc[:, h:h + 1], scalar2=None,
                            op0=ALU.mult)
        op = V.tensor_tensor(out=qgT[0:64, :], in0=qT[0:64, :],
                             in1=gp[0:64, :], op=ALU.mult)
        prep_done = em.inc(op, "ve")
        em.wait("tensor", "ve", prep_done)

        st_tiles = []
        for hb in range(4):
            idx, ps = poolA.acquire()
            for hh in range(4):
                h = 4 * hb + hh
                op = TE.matmul(
                    ps[:, hh * C:(hh + 1) * C],
                    kT[0:64, h * C:(h + 1) * C],
                    qT[0:64, h * C:(h + 1) * C],
                    start=True, stop=True)
            pe_c = em.inc(op, "pe")
            st_tiles.append((idx, ps, pe_c))
        lastPT = 0
        for hb in range(4):
            idx, ps, pe_c = st_tiles[hb]
            em.wait("vector", "pe", pe_c)
            em.wait("vector", "pe", proj_free["PT"])
            for hh in range(4):
                h = 4 * hb + hh
                op = V.tensor_tensor(out=PT[:, h * C:(h + 1) * C],
                                     in0=ps[:, hh * C:(hh + 1) * C],
                                     in1=dloc[:, h * C:(h + 1) * C],
                                     op=ALU.mult)
            lastPT = poolA.release(idx, op, "vector")
        em.wait("tensor", "ve", lastPT)
        yb = ybuf[c % 2]
        em.wait("scalar", f"ysp{c % 2}", ybuf_st[c % 2])
        ydone = 0
        for h in range(NH):
            idx, ps = poolO.acquire()
            sl = ps[0:64, (h % 4) * C:(h % 4) * C + C]
            op = TE.matmul(sl, vtok[:, h * HD:(h + 1) * HD],
                           PT[:, h * C:(h + 1) * C], start=True, stop=False)
            op = TE.matmul(sl, stbf[0:64, h * HD:(h + 1) * HD],
                           qgT[0:64, h * C:(h + 1) * C],
                           start=False, stop=True)
            pe_c = em.inc(op, "pe")
            em.wait("scalar", "pe", pe_c)
            scol = h * NCH + c
            op = SC.activation(out=yb[0:64, h * C:(h + 1) * C], in_=sl,
                               func=AF.Copy,
                               accum_out=ysum[0:64, scol:scol + 1])
            op = SC.activation(out=yscr[0:64, :], in_=sl, func=AF.Square,
                               accum_out=ysq[0:64, scol:scol + 1])
            ydone = poolO.release(idx, op, "scalar")
        proj_free["qT"] = proj_free["kT"] = em.cnt["pe"]
        ds_tiles = []
        for hb in range(2):
            idx, ps = poolA.acquire()
            for hh in range(8):
                h = 8 * hb + hh
                op = TE.matmul(ps[0:64, hh * HD:(hh + 1) * HD],
                               kw[:, h * HD:(h + 1) * HD],
                               vtok[:, h * HD:(h + 1) * HD],
                               start=True, stop=True)
            pe_c = em.inc(op, "pe")
            ds_tiles.append((idx, ps, pe_c))
        lastS = 0
        for hb in range(2):
            idx, ps, pe_c = ds_tiles[hb]
            em.wait("vector", "pe", pe_c)
            for hh in range(8):
                h = 8 * hb + hh
                gC = GAM[h] ** C
                dsl = st32[0:64, h * HD:(h + 1) * HD]
                op = V.scalar_tensor_tensor(
                    out=dsl, in0=dsl, scalar=gC,
                    in1=ps[0:64, hh * HD:(hh + 1) * HD],
                    op0=ALU.mult, op1=ALU.add)
            lastS = poolA.release(idx, op, "vector")
        em.wait("vector", "ve", lastS)
        op = V.tensor_copy(out=stbf[0:64, :], in_=st32[0:64, :])
        st_ready = em.inc(op, "ve")
        em.wait("tensor", "ve", st_ready)
        proj_free["vtok"] = proj_free["ktok"] = em.cnt["pe"]
        proj_free["PT"] = em.cnt["pe"]
        em.wait("sync", "sc", ydone)
        gtok = c * C
        ybuf_st[c % 2] = em.dma(
            "sync", ysp_v[:, :, gtok:gtok + C],
            yb.ap().rearrange("p (h t) -> p h t", h=16), sem=f"ysp{c % 2}")

    em.wait("vector", "sc", em.cnt["sc"])
    for h in range(NH):
        V.reduce_sum(scols[0:64, h:h + 1], ysum[0:64, h * NCH:(h + 1) * NCH],
                     axis=AX.X)
        op = V.reduce_sum(scols[0:64, 16 + h:17 + h],
                          ysq[0:64, h * NCH:(h + 1) * NCH], axis=AX.X)
    sc_ready = em.inc(op, "ve")
    em.wait("tensor", "ve", sc_ready)
    idx, ps = poolA.acquire()
    op = TE.matmul(ps[0:32, 0:1], scols[0:64, 0:32],
                   bufs["ones_col"][0:64, 0:1], start=True, stop=True)
    pe_c = em.inc(op, "pe")
    em.wait("vector", "pe", pe_c)
    op = V.tensor_copy(out=hstat[0:32, 0:1], in_=ps[0:32, 0:1])
    poolA.release(idx, op, "vector")
    fence(em, "vector")
    NTOT = float(T * HD)
    op = V.tensor_scalar(out=hstat[0:32, 1:2], in0=hstat[0:32, 0:1],
                         scalar1=1.0 / NTOT, scalar2=None, op0=ALU.mult)
    c2 = em.inc(op, "ve")
    em.wait("tensor", "ve", c2)
    idx, ps2 = poolA.acquire()
    op = TE.matmul(ps2[0:1, 0:32], hstat[0:32, 1:2], bufs["eye32"][0:32, 0:32],
                   start=True, stop=True)
    pe_c = em.inc(op, "pe")
    em.wait("vector", "pe", pe_c)
    op = V.tensor_copy(out=hrow[0:1, 0:32], in_=ps2[0:1, 0:32])
    poolA.release(idx, op, "vector")
    fence(em, "vector")
    V.tensor_tensor(out=hrow[0:1, 32:48], in0=hrow[0:1, 0:16],
                    in1=hrow[0:1, 0:16], op=ALU.mult)
    fence(em, "vector")
    op = V.tensor_tensor(out=hrow[0:1, 32:48], in0=hrow[0:1, 16:32],
                         in1=hrow[0:1, 32:48], op=ALU.subtract)
    c2 = em.inc(op, "ve")
    em.wait("scalar", "ve", c2)
    op = SC.activation(out=hrow[0:1, 32:48], in_=hrow[0:1, 32:48],
                       func=AF.Sqrt, bias=bufs["konst"][0:1, 0:1])
    c2 = em.inc(op, "sc")
    em.wait("vector", "sc", c2)
    V.reciprocal(hrow[0:1, 32:48], hrow[0:1, 32:48])
    fence(em, "vector")
    V.tensor_tensor(out=hrow[0:1, 48:64], in0=hrow[0:1, 0:16],
                    in1=hrow[0:1, 32:48], op=ALU.mult)
    fence(em, "vector")
    op = V.tensor_scalar(out=hrow[0:1, 48:64], in0=hrow[0:1, 48:64],
                         scalar1=-1.0, scalar2=None, op0=ALU.mult)
    c2 = em.inc(op, "ve")
    em.wait("tensor", "ve", c2)
    idx, ps3 = poolA.acquire()
    op = TE.matmul(ps3[:, 0:64], bufs["ones32"][0:1, :], hrow[0:1, 0:64],
                   start=True, stop=True)
    pe_c = em.inc(op, "pe")
    em.wait("vector", "pe", pe_c)
    op = V.tensor_copy(out=hrep[:, :], in_=ps3[:, 0:64])
    poolA.release(idx, op, "vector")
    fence(em, "vector")
    for t8 in range(8):
        V.tensor_copy(out=abcol.ap()[0:64, t8:t8 + 1],
                      in_=hrep.ap()[0:64, 32 + 2 * t8:33 + 2 * t8])
        V.tensor_copy(out=abcol.ap()[64:128, t8:t8 + 1],
                      in_=hrep.ap()[64:128, 33 + 2 * t8:34 + 2 * t8])
        V.tensor_copy(out=abcol.ap()[0:64, 8 + t8:9 + t8],
                      in_=hrep.ap()[0:64, 48 + 2 * t8:49 + 2 * t8])
        V.tensor_copy(out=abcol.ap()[64:128, 8 + t8:9 + t8],
                      in_=hrep.ap()[64:128, 49 + 2 * t8:50 + 2 * t8])
    fence(em, "vector")
    V.tensor_tensor(out=abcol[:, 16:24], in0=abcol[:, 0:8],
                    in1=gnwb[:, 0:8], op=ALU.mult)
    V.tensor_tensor(out=abcol[:, 24:32], in0=abcol[:, 8:16],
                    in1=gnwb[:, 0:8], op=ALU.mult)
    fence(em, "vector")
    op = V.tensor_tensor(out=abcol[:, 24:32], in0=abcol[:, 24:32],
                         in1=gnwb[:, 8:16], op=ALU.add)
    c2 = em.inc(op, "ve")
    em.wait("scalar", "ve", c2)

    xo_store = [0, 0]
    xt_used = [0, 0]
    ynS_used = [0, 0]
    for tsl in range(T // 512):
        for t8 in range(8):
            sb = t8 % 2
            em.wait("sync", "sc", ynS_used[sb])
            cdma = em.dma("sync", ynS[sb][:, :],
                          ysp[t8 * 128:(t8 + 1) * 128,
                              tsl * 512:(tsl + 1) * 512],
                          sem=f"yn{sb}")
            em.wait("scalar", f"yn{sb}", cdma)
            op = SC.activation(out=ynT[:, t8 * 512:(t8 + 1) * 512],
                               in_=ynS[sb][:, :],
                               func=AF.Identity,
                               scale=abcol[:, 16 + t8:17 + t8],
                               bias=abcol[:, 24 + t8:25 + t8])
            ynS_used[sb] = em.inc(op, "sc")
        em.wait("tensor", "sc", em.cnt["sc"])
        for tt in range(4):
            tg = tsl * 4 + tt
            b = tg % 2
            em.wait("sync", "ve", xt_used[b])
            em.wait("sync", "sc", ln_hist[b])
            c2 = em.dma("sync", bufs["xt"][b][:, :],
                        xin[tg * 128:(tg + 1) * 128, :], sem=f"xa{b}")
            ve_c = 0
            for dn in range(2):
                idx, ps = poolO.acquire()
                for ic in range(8):
                    op = TE.matmul(
                        ps[:, :],
                        ynT[:, ic * 512 + tt * 128: ic * 512 + tt * 128 + 128],
                        wo[:, ic * D + dn * 512: ic * D + dn * 512 + 512],
                        start=(ic == 0), stop=(ic == 7))
                pe_c = em.inc(op, "pe")
                em.wait("vector", "pe", pe_c)
                if dn == 0:
                    em.wait("vector", f"xa{b}", c2)
                    em.wait("vector", f"xs{b}", xo_store[b])
                op = V.tensor_tensor(
                    out=xo[b][:, dn * 512:(dn + 1) * 512], in0=ps[:, :],
                    in1=bufs["xt"][b][:, dn * 512:(dn + 1) * 512], op=ALU.add)
                ve_c = poolO.release(idx, op, "vector")
            xt_used[b] = ve_c
            em.wait("sync", "ve", ve_c)
            xo_store[b] = em.dma("sync", xout[tg * 128:(tg + 1) * 128, :],
                                 xo[b][:, :], sem=f"xs{b}")
        em.wait("sync", "pe", em.cnt["pe"])


def l3_attn_host(inp):
    a = {}
    a["l3a_wqT"] = np.ascontiguousarray(inp["L3_attn_W_q"].T)
    a["l3a_wkT"] = np.ascontiguousarray(inp["L3_attn_W_k"].T)
    a["l3a_wvT"] = np.ascontiguousarray(inp["L3_attn_W_v"].T)
    a["l3a_woT"] = np.ascontiguousarray(inp["L3_attn_W_o"].T)
    C, NH = 128, 16
    gam = (1.0 - 2.0 ** (-5.0 - np.arange(NH))).astype(np.float64)
    m = np.arange(C)[:, None]
    l = np.arange(C)[None, :]
    dloc = np.zeros((C, NH * C), np.float32)
    gp = np.zeros((64, NH * C), np.float32)
    gw = np.zeros((C, NH), np.float32)
    for h in range(NH):
        Dm = np.where(l >= m, gam[h] ** (l - m), 0.0) / 8.0
        dloc[:, h * C:(h + 1) * C] = Dm.astype(np.float32)
        gp[:, h * C:(h + 1) * C] = np.tile(
            (gam[h] ** (np.arange(C) + 1) / 8.0).astype(np.float32), (64, 1))
        gw[:, h] = (gam[h] ** (C - 1 - np.arange(C))).astype(np.float32)
    a["l3a_dloc"] = dloc
    a["l3a_gp"] = gp
    a["l3a_gw"] = gw
    gnwb = np.zeros((128, 16), np.float32)
    gnwb[:, 0:8] = inp["L3_attn_gn_weight"].reshape(8, 128).T
    gnwb[:, 8:16] = inp["L3_attn_gn_bias"].reshape(8, 128).T
    a["l3a_gnwb"] = gnwb
    return a


ATTN_SETUP["l3a"] = setup_l3_attn
ATTN_HOST["l3a"] = l3_attn_host


def setup_l1_attn(em, bufs, xin, xout, decls):
    nc = em.nc
    name = "l1a"
    C = 256
    NCH = T // C
    V = em.on("vector")
    SC = em.on("scalar")
    TE = em.on("tensor")
    poolA, poolO = bufs["poolA"], bufs["poolO"]

    MH = T // 4
    CPQ = MH // C
    NTQ = MH // 128
    lnxT = em.salloc(f"{name}_lnxT", [128, 8 * MH], BF16)
    lnxtok = em.salloc(f"{name}_lnxtok", [128, NTQ * D], BF16)
    wwT = em.salloc(f"{name}_wwTs", [128, 8 * D], BF16)
    wrT = em.salloc(f"{name}_wrTs", [128, 8 * D], BF16)
    WT32 = em.salloc(f"{name}_WT32", [128, 8 * D], F32)
    WTbf = em.salloc(f"{name}_WTbf", [128, 8 * D], BF16)
    vtok = em.salloc(f"{name}_vtok", [128, 2 * D], BF16)
    vw = em.salloc(f"{name}_vw", [128, 2 * D], BF16)
    wkTb = em.salloc(f"{name}_wkTb", [128, 8 * C], BF16)
    wktok = em.salloc(f"{name}_wktok", [128, 2 * D], BF16)
    SMt = em.salloc(f"{name}_SMt", [128, 2 * C], BF16)
    MT = em.salloc(f"{name}_MT", [128, 2 * C], F32)
    DIFFS = em.salloc(f"{name}_DIFFSs", [128, 2 * C], F32)
    CAUS = em.salloc(f"{name}_CAUSs", [128, 2 * C], F32)
    readsS = em.salloc(f"{name}_readsS", [128, 2 * D], F32)
    readsT = em.salloc(f"{name}_readsT", [128, 8 * C], BF16)
    savedc = em.salloc(f"{name}_savedc", [128, 8], BF16)
    iot = em.salloc(f"{name}_iots", [128, 4], F32)
    srow = em.salloc(f"{name}_srow", [1, 2], F32)
    cols = em.salloc(f"{name}_cols", [128, 8], F32)
    xo = [em.salloc(f"{name}_xo{i}", [128, D], F32) for i in range(2)]

    cstS, cstG = f"cst_{name}", f"cstg_{name}"
    h = decls.dram(em, "l1a_wwT", [D, D])
    em.dma("gpsimd", wwT.ap().rearrange("p (c d) -> p c d", c=8),
           h.rearrange("(c p) d -> p c d", p=128), sem=cstG)
    h = decls.dram(em, "l1a_wrT", [D, D])
    em.dma("gpsimd", wrT.ap().rearrange("p (c d) -> p c d", c=8),
           h.rearrange("(c p) d -> p c d", p=128), sem=cstG)
    h = decls.dram(em, "l1a_diffs", [128, 2 * C])
    em.dma("sync", DIFFS[:, :], h[:, :], sem=cstS)
    h = decls.dram(em, "l1a_caus", [128, 2 * C])
    em.dma("sync", CAUS[:, :], h[:, :], sem=cstS)
    h = decls.dram(em, "l1a_iota", [128, 4])
    em.dma("sync", iot[:, :], h[:, :], sem=cstS)
    h = decls.dram(em, "l1a_scal", [1, 2])
    cS = em.dma("sync", srow[0:1, :], h[0:1, :], sem=cstS)
    em.wait("tensor", cstS, em.cnt[cstS])
    em.wait("tensor", cstG, em.cnt[cstG])
    em.wait("scalar", cstS, em.cnt[cstS])
    em.wait("vector", cstS, em.cnt[cstS])

    op = SC.activation(out=srow[0:1, 0:1], in_=srow[0:1, 0:1], func=AF.Sigmoid)
    c = em.inc(op, "sc")
    em.wait("scalar", "sc", c)
    op = SC.activation(out=srow[0:1, 0:1], in_=srow[0:1, 0:1], func=AF.Ln)
    op = SC.activation(out=srow[0:1, 1:2], in_=srow[0:1, 1:2], func=AF.Exp)
    c = em.inc(op, "sc")
    em.wait("tensor", "sc", c)
    em.wait("tensor", "ve", em.cnt["ve"])
    idx, ps = poolA.acquire()
    op = TE.matmul(ps[:, 0:2], bufs["ones32"][0:1, :], srow[0:1, 0:2],
                   start=True, stop=True)
    pe_c = em.inc(op, "pe")
    em.wait("vector", "pe", pe_c)
    op = V.tensor_copy(out=cols[:, 0:2], in_=ps[:, 0:2])
    poolA.release(idx, op, "vector")
    fence(em, "vector")
    for j in range(4):
        V.tensor_tensor(out=cols[:, 2 + j:3 + j], in0=iot[:, j:j + 1],
                        in1=cols[:, 0:1], op=ALU.mult)
    op = V.tensor_scalar(out=cols[:, 6:7], in0=cols[:, 0:1],
                         scalar1=float(C), scalar2=None, op0=ALU.mult)
    c = em.inc(op, "ve")
    em.wait("scalar", "ve", c)
    op = SC.activation(out=cols[:, 2:7], in_=cols[:, 2:7], func=AF.Exp)
    c = em.inc(op, "sc")
    em.wait("vector", "sc", c)
    op = V.tensor_scalar(out=MT[:, :], in0=DIFFS[:, :], scalar1=cols[:, 0:1],
                         scalar2=None, op0=ALU.mult)
    c = em.inc(op, "ve")
    em.wait("scalar", "ve", c)
    op = SC.activation(out=MT[:, :], in_=MT[:, :], func=AF.Exp)
    c = em.inc(op, "sc")
    em.wait("vector", "sc", c)
    op = V.tensor_tensor(out=MT[:, :], in0=MT[:, :], in1=CAUS[:, :],
                         op=ALU.mult)
    op = V.memset(WT32[:, :], 0.0)
    op = V.memset(WTbf[:, :], 0.0)
    op = V.memset(savedc[:, :], 0.0)
    st_ready = em.inc(op, "ve")
    em.wait("tensor", "ve", st_ready)

    free = {"vtok": 0, "vw": 0, "wkTb": 0, "wktok": 0, "SMt": 0, "readsS": 0,
            "readsT": 0, "lnxT_used": 0}
    ln_hist = [0, 0]
    xo_store = [0, 0]
    xt_used = [0, 0]
    for ch in range(NCH):
        if ch % CPQ == 0:
            if ch > 0:
                em.wait("vector", "pe", em.cnt["pe"])
                for c8 in range(8):
                    op = V.tensor_copy(
                        out=savedc[:, c8:c8 + 1],
                        in_=lnxT[:, c8 * MH + MH - 1: c8 * MH + MH])
                em.inc(op, "ve")
            em.barrier()
            ln_hist = emit_ln_sweep(em, bufs, xin, lnxT, MH,
                                    (ch // CPQ) * MH, MH,
                                    with_tokmajor=lnxtok)
            em.wait("tensor", "ve", em.cnt["ve"])
        tok0 = (ch * C) % MH
        gt0 = ch * C
        em.wait("vector", "pe", free["wkTb"])
        for c8 in range(8):
            if tok0 == 0:
                V.tensor_copy(out=wkTb[:, c8 * C:c8 * C + 1],
                              in_=savedc[:, c8:c8 + 1])
                op = V.tensor_copy(
                    out=wkTb[:, c8 * C + 1:(c8 + 1) * C],
                    in_=lnxT[:, c8 * MH: c8 * MH + C - 1])
            else:
                op = V.tensor_copy(
                    out=wkTb[:, c8 * C:(c8 + 1) * C],
                    in_=lnxT[:, c8 * MH + tok0 - 1: c8 * MH + tok0 + C - 1])
        wk_ready = em.inc(op, "ve")
        em.wait("tensor", "ve", wk_ready)
        for t2 in range(2):
            for dn in range(2):
                idx, ps = poolO.acquire()
                for kc in range(8):
                    op = TE.matmul(
                        ps[:, :],
                        lnxT[:, kc * MH + tok0 + t2 * 128:
                             kc * MH + tok0 + t2 * 128 + 128],
                        wwT[:, kc * D + dn * 512: kc * D + dn * 512 + 512],
                        start=(kc == 0), stop=(kc == 7))
                pe_c = em.inc(op, "pe")
                em.wait("vector", "pe", pe_c)
                em.wait("vector", "pe", free["vtok"])
                op = V.tensor_copy(out=vtok[:, t2 * D + dn * 512:
                                            t2 * D + dn * 512 + 512],
                                   in_=ps[:, :])
                poolO.release(idx, op, "vector")
        for t2 in range(2):
            op = V.tensor_scalar(out=vw[:, t2 * D:(t2 + 1) * D],
                                 in0=vtok[:, t2 * D:(t2 + 1) * D],
                                 scalar1=cols[:, 4 + t2:5 + t2], scalar2=None,
                                 op0=ALU.mult)
        vw_ready = em.inc(op, "ve")
        em.wait("tensor", "ve", wk_ready)
        last = 0
        for c8 in range(8):
            for t2 in range(2):
                idx, pt = bufs["tp_pool"].acquire()
                op = TE.transpose(pt[:, 0:128],
                                  wkTb[:, c8 * C + t2 * 128:
                                       c8 * C + t2 * 128 + 128],
                                  bufs["eye"][:, :])
                pe_c = em.inc(op, "pe")
                em.wait("vector", "pe", pe_c)
                em.wait("vector", "pe", free["wktok"])
                op = V.tensor_copy(
                    out=wktok[:, t2 * D + c8 * 128: t2 * D + c8 * 128 + 128],
                    in_=pt[:, 0:128])
                last = bufs["tp_pool"].release(idx, op, "vector")
        em.wait("tensor", "ve", last)
        sm_tiles = []
        for mt in range(2):
            idx, ps = poolA.acquire()
            for kc in range(8):
                op = TE.matmul(
                    ps[:, 0:C],
                    wkTb[:, kc * C + mt * 128: kc * C + mt * 128 + 128],
                    lnxT[:, kc * MH + tok0: kc * MH + tok0 + C],
                    start=(kc == 0), stop=(kc == 7))
            pe_c = em.inc(op, "pe")
            sm_tiles.append((idx, ps, pe_c))
        lastSM = 0
        for mt in range(2):
            idx, ps, pe_c = sm_tiles[mt]
            em.wait("vector", "pe", pe_c)
            em.wait("vector", "pe", free["SMt"])
            op = V.tensor_tensor(out=SMt[:, mt * C:(mt + 1) * C],
                                 in0=ps[:, 0:C], in1=MT[:, mt * C:(mt + 1) * C],
                                 op=ALU.mult)
            lastSM = poolA.release(idx, op, "vector")
        em.wait("tensor", "ve", lastSM)
        em.wait("vector", "pe", free["readsS"])
        for t2 in range(2):
            for dn in range(2):
                idxI, psI = poolA.acquire()
                for kc in range(8):
                    op = TE.matmul(
                        psI[:, :],
                        lnxT[:, kc * MH + tok0 + t2 * 128:
                             kc * MH + tok0 + t2 * 128 + 128],
                        WTbf[:, kc * D + dn * 512: kc * D + dn * 512 + 512],
                        start=(kc == 0), stop=(kc == 7))
                pe_i = em.inc(op, "pe")
                idxJ, psJ = poolO.acquire()
                for mt in range(2):
                    op = TE.matmul(
                        psJ[:, :],
                        SMt[:, mt * C + t2 * 128: mt * C + t2 * 128 + 128],
                        vtok[:, mt * D + dn * 512: mt * D + dn * 512 + 512],
                        start=(mt == 0), stop=(mt == 1))
                pe_j = em.inc(op, "pe")
                em.wait("vector", "pe", pe_j)
                sl = readsS[:, t2 * D + dn * 512: t2 * D + dn * 512 + 512]
                op = V.tensor_scalar(out=sl, in0=psI[:, :],
                                     scalar1=cols[:, 2 + t2:3 + t2],
                                     scalar2=None, op0=ALU.mult)
                poolA.release(idxI, op, "vector")
                fence(em, "vector")
                op = V.tensor_tensor(out=sl, in0=psJ[:, :], in1=sl, op=ALU.add)
                poolO.release(idxJ, op, "vector")
        fence(em, "vector")
        em.wait("tensor", "ve", em.cnt["ve"])
        last = 0
        for t2 in range(2):
            for c8 in range(8):
                idx, pt = poolA.acquire()
                op = TE.transpose(pt[:, 0:128],
                                  readsS[:, t2 * D + c8 * 128:
                                         t2 * D + c8 * 128 + 128],
                                  bufs["eye32"][:, :])
                pe_c = em.inc(op, "pe")
                em.wait("vector", "pe", pe_c)
                em.wait("vector", "pe", free["readsT"])
                op = V.tensor_copy(
                    out=readsT[:, c8 * C + t2 * 128: c8 * C + t2 * 128 + 128],
                    in_=pt[:, 0:128])
                last = poolA.release(idx, op, "vector")
        em.wait("tensor", "ve", last)
        for t2 in range(2):
            tg = (gt0 // 128) + t2
            b = tg % 2
            em.wait("sync", "ve", xt_used[b])
            em.wait("sync", "sc", ln_hist[b])
            cdma = em.dma("sync", bufs["xt"][b][:, :],
                          xin[tg * 128:(tg + 1) * 128, :], sem=f"xa{b}")
            ve_c = 0
            for dn in range(2):
                idx, ps = poolO.acquire()
                for c8 in range(8):
                    op = TE.matmul(
                        ps[:, :],
                        readsT[:, c8 * C + t2 * 128: c8 * C + t2 * 128 + 128],
                        wrT[:, c8 * D + dn * 512: c8 * D + dn * 512 + 512],
                        start=(c8 == 0), stop=(c8 == 7))
                pe_c = em.inc(op, "pe")
                em.wait("vector", "pe", pe_c)
                if dn == 0:
                    em.wait("vector", f"xa{b}", cdma)
                    em.wait("vector", f"xs{b}", xo_store[b])
                sl = xo[b][:, dn * 512:(dn + 1) * 512]
                op = V.scalar_tensor_tensor(
                    out=sl, in0=ps[:, :], scalar=cols[:, 1:2],
                    in1=bufs["xt"][b][:, dn * 512:(dn + 1) * 512],
                    op0=ALU.mult, op1=ALU.add)
                poolO.release(idx, op, "vector")
                fence(em, "vector")
                lt = (tok0 // 128) + t2
                op = V.tensor_tensor(
                    out=sl, in0=lnxtok[:, lt * D + dn * 512:
                                       lt * D + dn * 512 + 512],
                    in1=sl, op=ALU.add)
                ve_c = em.inc(op, "ve")
            xt_used[b] = ve_c
            em.wait("sync", "ve", ve_c)
            xo_store[b] = em.dma("sync", xout[tg * 128:(tg + 1) * 128, :],
                                 xo[b][:, :], sem=f"xs{b}")
        free["readsS"] = free["readsT"] = em.cnt["pe"]
        free["SMt"] = em.cnt["pe"]
        em.wait("tensor", "ve", vw_ready)
        for jc in range(8):
            for dn in range(2):
                idx, ps = poolA.acquire()
                for t2 in range(2):
                    op = TE.matmul(
                        ps[:, :],
                        wktok[:, t2 * D + jc * 128: t2 * D + jc * 128 + 128],
                        vw[:, t2 * D + dn * 512: t2 * D + dn * 512 + 512],
                        start=(t2 == 0), stop=(t2 == 1))
                pe_c = em.inc(op, "pe")
                em.wait("vector", "pe", pe_c)
                sl = WT32[:, jc * D + dn * 512: jc * D + dn * 512 + 512]
                op = V.scalar_tensor_tensor(
                    out=sl, in0=sl, scalar=cols[:, 6:7], in1=ps[:, :],
                    op0=ALU.mult, op1=ALU.add)
                poolA.release(idx, op, "vector")
        fence(em, "vector")
        op = V.tensor_copy(out=WTbf[:, :], in_=WT32[:, :])
        wt_ready = em.inc(op, "ve")
        em.wait("tensor", "ve", wt_ready)
        free["vtok"] = free["vw"] = em.cnt["pe"]
        free["wkTb"] = free["wktok"] = em.cnt["pe"]


def l1_attn_host(inp):
    a = {}
    a["l1a_wwT"] = np.ascontiguousarray(inp["L1_attn_W_write"].T)
    a["l1a_wrT"] = np.ascontiguousarray(inp["L1_attn_W_read"].T)
    C = 256
    m = np.arange(C)
    diffs = np.zeros((128, 2 * C), np.float32)
    caus = np.zeros((128, 2 * C), np.float32)
    for mt in range(2):
        mm = (mt * 128 + np.arange(128))[:, None]
        ll = np.arange(C)[None, :]
        diffs[:, mt * C:(mt + 1) * C] = np.clip(ll - 1 - mm, 0, None)
        caus[:, mt * C:(mt + 1) * C] = (ll > mm).astype(np.float32)
    a["l1a_diffs"] = diffs
    a["l1a_caus"] = caus
    iota = np.zeros((128, 4), np.float32)
    iota[:, 0] = np.arange(128)
    iota[:, 1] = 128 + np.arange(128)
    iota[:, 2] = C - 1 - np.arange(128)
    iota[:, 3] = C - 1 - (128 + np.arange(128))
    a["l1a_iota"] = iota
    a["l1a_scal"] = np.array([[float(inp["L1_attn_decay"]),
                               float(inp["L1_attn_log_alpha"])]], np.float32)
    return a


ATTN_SETUP["l1a"] = setup_l1_attn
ATTN_HOST["l1a"] = l1_attn_host


def setup_l2_attn(em, bufs, xin, xout, decls):
    nc = em.nc
    name = "l2a"
    NH, HD = 16, 64
    V = em.on("vector")
    SC = em.on("scalar")
    TE = em.on("tensor")
    poolA, poolO = bufs["poolA"], bufs["poolO"]
    tp = bufs["tp_pool"]

    MH = T // 8
    NTQ = MH // 128
    lnxT = em.salloc(f"{name}_lnxT", [128, 8 * MH], BF16)
    wqv = em.salloc(f"{name}_wqs", [128, 8 * D], BF16)
    wkv = em.salloc(f"{name}_wkvs", [128, 8 * 2 * D], BF16)
    wov = em.salloc(f"{name}_wos", [128, 8 * D], BF16)
    wbil = em.salloc(f"{name}_wbil", [64, 16 * HD], BF16)
    costb = [em.salloc(f"{name}_coss{i}", [128, D], BF16) for i in range(2)]
    sintb = [em.salloc(f"{name}_sins{i}", [128, D], BF16) for i in range(2)]
    pf32 = [em.salloc(f"{name}_pf32_{i}", [128, D], F32) for i in range(2)]
    prot = em.salloc(f"{name}_prot", [128, D], F32)
    pt1 = em.salloc(f"{name}_pt1", [128, D], F32)
    pbf = [em.salloc(f"{name}_pbf{i}", [128, D], BF16) for i in range(2)]
    pbfv = em.salloc(f"{name}_pbfv", [128, D], BF16)
    kTh = em.salloc(f"{name}_kTh", [64, T], BF16)
    ktTh = em.salloc(f"{name}_ktTh", [64, T], BF16)
    qTh = em.salloc(f"{name}_qTh", [64, 2 * 512], BF16)
    vh = em.salloc(f"{name}_vh", [128, (T // 128) * HD], BF16)
    Pbuf = em.salloc(f"{name}_P", [128, T], BF16)
    PTb = em.salloc(f"{name}_PT", [128, T], BF16)
    mx = em.salloc(f"{name}_mx", [128, 32], F32)
    oth = em.salloc(f"{name}_oth", [64, 512], BF16)
    orow = em.salloc(f"{name}_orow", [1, 512], F32)
    orep = em.salloc(f"{name}_orep", [128, 512], F32)
    ynT = em.salloc(f"{name}_ynT", [128, 8 * 512], BF16)
    ynS = [em.salloc(f"{name}_ynS{i}", [128, 512], BF16) for i in range(2)]
    xo = [em.salloc(f"{name}_xo{i}", [128, D], F32) for i in range(2)]
    qsp = nc.dram_tensor(f"{name}_QSP", [D, T], BF16, kind="Internal").ap()
    ksp = nc.dram_tensor(f"{name}_KSP", [D, T], BF16, kind="Internal").ap()
    spT = em.salloc(f"{name}_spT", [128, 8 * 128], BF16)
    vsp = nc.dram_tensor(f"{name}_VSP", [T, D], BF16, kind="Internal").ap()
    osp = nc.dram_tensor(f"{name}_OSP", [D, T], BF16, kind="Internal").ap()

    cstS, cstG = f"cst_{name}", f"cstg_{name}"
    h = decls.dram(em, "l2a_wq", [D, D])
    em.dma("gpsimd", wqv.ap().rearrange("p (c d) -> p c d", c=8),
           h.rearrange("(c p) d -> p c d", p=128), sem=cstG)
    h = decls.dram(em, "l2a_wkv", [D, 2 * D])
    em.dma("gpsimd", wkv.ap().rearrange("p (c d) -> p c d", c=8),
           h.rearrange("(c p) d -> p c d", p=128), sem=cstG)
    h = decls.dram(em, "l2a_wo", [D, D])
    em.dma("gpsimd", wov.ap().rearrange("p (c d) -> p c d", c=8),
           h.rearrange("(c p) d -> p c d", p=128), sem=cstG)
    h = decls.dram(em, "l2a_wbilT", [64, 16 * HD])
    cG = em.dma("gpsimd", wbil[0:64, :], h[:, :], sem=cstG)
    gamc = em.salloc(f"{name}_gamc", [128, 8], F32)
    h = decls.dram(em, "l2a_gamcol", [128, 8])
    cS0 = em.dma("sync", gamc[:, :], h[:, :], sem=cstS)
    cosd = decls.dram(em, "l2a_cos", [T, D])
    sind = decls.dram(em, "l2a_sin", [T, D])
    em.wait("tensor", cstG, em.cnt[cstG])
    em.wait("vector", cstG, em.cnt[cstG])
    em.wait("vector", cstS, em.cnt[cstS])
    for kc in range(8):
        V.tensor_scalar(out=wqv[:, kc * D:(kc + 1) * D],
                        in0=wqv[:, kc * D:(kc + 1) * D],
                        scalar1=gamc[:, kc:kc + 1], scalar2=0.125,
                        op0=ALU.mult, op1=ALU.mult)
        V.tensor_scalar(out=wkv[:, kc * 2 * D:(kc + 1) * 2 * D],
                        in0=wkv[:, kc * 2 * D:(kc + 1) * 2 * D],
                        scalar1=gamc[:, kc:kc + 1], scalar2=None, op0=ALU.mult)
    op = em.rec["vector"].ops[-1][3]
    gam_done = em.inc(op, "ve")
    em.wait("tensor", "ve", gam_done)

    ln_hist = [0, 0]
    tab_used = [0, 0]
    pq_used = [0, 0]
    spills = {"q": [0, 0], "k": [0, 0], "v": [0, 0]}
    bil_pe = 0
    smm_pe = 0
    for ti in range(NTILE):
        if ti % NTQ == 0:
            em.barrier()
            ln_hist = emit_ln_sweep(em, bufs, xin, lnxT, MH,
                                    (ti // NTQ) * MH, MH, double_ln=True)
            em.wait("tensor", "ve", em.cnt["ve"])
        lt = ti % NTQ
        tb = ti % 2
        t0 = ti * 128
        em.wait("gpsimd", "ve", tab_used[tb])
        ctab = em.dma("gpsimd", costb[tb][:, :], cosd[t0:t0 + 128, :], sem=f"ct{tb}")
        ctab = em.dma("gpsimd", sintb[tb][:, :], sind[t0:t0 + 128, :], sem=f"ct{tb}")
        for which, wsrc, wc0, ctab_c, ctab_s, dosp, spname in (
                ("q", wqv, 0, costb[tb], sintb[tb], qsp, "q"),
                ("k", wkv, 0, costb[tb], sintb[tb], ksp, "k"),
                ("v", wkv, D, None, None, vsp, "v")):
            pfb = pf32[0] if which != "k" else pf32[1]
            evs = []
            for dn in range(2):
                idx, ps = poolO.acquire()
                for kc in range(8):
                    op = TE.matmul(
                        ps[:, :],
                        lnxT[:, kc * MH + lt * 128: kc * MH + lt * 128 + 128],
                        wsrc[:, kc * (2 * D if which != "q" else D) + wc0 +
                             dn * 512: kc * (2 * D if which != "q" else D) +
                             wc0 + dn * 512 + 512],
                        start=(kc == 0), stop=(kc == 7))
                pe_c = em.inc(op, "pe")
                em.wait("vector", "pe", pe_c)
                if which == "v":
                    if dn == 0:
                        em.wait("vector", "sp_v", spills["v"][tb])
                    op = V.tensor_copy(out=pbfv[:, dn * 512:(dn + 1) * 512],
                                       in_=ps[:, :])
                else:
                    op = V.tensor_copy(out=pfb[:, dn * 512:(dn + 1) * 512],
                                       in_=ps[:, :])
                evs.append(poolO.release(idx, op, "vector"))
            if which == "v":
                em.wait("sync", "ve", evs[-1])
                spills["v"][tb] = em.dma("sync", vsp[t0:t0 + 128, :],
                                         pbfv[:, :], sem="sp_v")
                continue
            em.wait("vector", f"ct{tb}", ctab)
            V.tensor_scalar(out=prot[:, 1:D:2], in0=pfb[:, 0:D:2],
                            scalar1=1.0, scalar2=None, op0=ALU.mult)
            op = V.tensor_scalar(out=prot[:, 0:D:2], in0=pfb[:, 1:D:2],
                                 scalar1=-1.0, scalar2=None, op0=ALU.mult)
            c = em.inc(op, "ve"); em.wait("vector", "ve", c)
            V.tensor_tensor(out=pt1[:, :], in0=pfb[:, :], in1=ctab_c[:, :],
                            op=ALU.mult)
            op = V.tensor_tensor(out=prot[:, :], in0=prot[:, :],
                                 in1=ctab_s[:, :], op=ALU.mult)
            c = em.inc(op, "ve"); em.wait("vector", "ve", c)
            bfo = pbf[0] if which == "q" else pbf[1]
            op = V.tensor_tensor(out=bfo[:, :], in0=pt1[:, :], in1=prot[:, :],
                                 op=ALU.add)
            c = em.inc(op, "ve")
            em.wait("tensor", "ve", c)
            lastev = 0
            for c8 in range(8):
                idx, pt_ = tp.acquire()
                op = TE.transpose(pt_[:, 0:128],
                                  bfo[:, c8 * 128:(c8 + 1) * 128],
                                  bufs["eye"][:, :])
                pe_c = em.inc(op, "pe")
                em.wait("vector", "pe", pe_c)
                if c8 == 0:
                    em.wait("vector", f"sp_{spname}", spills[spname][tb])
                op = V.tensor_copy(out=spT[:, c8 * 128:(c8 + 1) * 128],
                                   in_=pt_[:, 0:128])
                lastev = tp.release(idx, op, "vector")
            em.wait("sync", "ve", lastev)
            spills[spname][tb] = em.dma(
                "sync",
                dosp.rearrange("(c p) t -> p c t", p=128)[:, :, t0:t0 + 128],
                spT.ap().rearrange("p (c t) -> p c t", c=8),
                sem=f"sp_{spname}")
        tab_used[tb] = em.cnt["ve"]
    em.barrier()

    NQM = T // 512
    NKS = T // 512
    NKT = T // 128
    free = {"kTh": 0, "ktTh": 0, "vh": 0, "qTh": 0, "P": 0, "PT": 0,
            "oth": 0}
    osp_st = 0
    c1 = em.dma("sync", kTh[0:64, :], ksp[0:HD, :], sem="ld_k")
    qfree = [0, 0]
    qpend = [0, 0]
    cq0 = em.dma("sync", qTh[0:64, 0:512], qsp[0:HD, 0:512], sem="ld_q")
    qpend[0] = cq0
    for hh in range(NH):
        em.wait("scalar", "pe", free["vh"])
        c2 = em.dma("scalar", vh[:, :],
                    vsp.rearrange("(n p) d -> p n d", p=128)[:, :, hh * HD:(hh + 1) * HD],
                    sem="ld_v")
        em.wait("tensor", "ld_k", c1)
        em.wait("tensor", "ld_v", c2)
        for ks in range(T // 512):
            idx, ps = poolA.acquire()
            op = TE.matmul(ps[0:64, :], wbil[0:64, hh * HD:(hh + 1) * HD],
                           kTh[0:64, ks * 512:(ks + 1) * 512],
                           start=True, stop=True)
            pe_c = em.inc(op, "pe")
            bil_pe = pe_c
            em.wait("vector", "pe", pe_c)
            if ks == 0:
                em.wait("vector", "pe", free["ktTh"])
            op = V.tensor_copy(out=ktTh[0:64, ks * 512:(ks + 1) * 512],
                               in_=ps[0:64, :])
            free["kTh"] = poolA.release(idx, op, "vector")
        if hh + 1 < NH:
            em.wait("sync", "pe", bil_pe)
            c1 = em.dma("sync", kTh[0:64, :],
                        ksp[(hh + 1) * HD:(hh + 2) * HD, :], sem="ld_k")
        em.wait("tensor", "ve", free["kTh"])
        for qm in range(NQM):
            q0 = qm * 512
            qsl = qm % 2
            em.wait("tensor", "ld_q", qpend[qsl])
            nh, nqm = (hh, qm + 1) if qm + 1 < NQM else (hh + 1, 0)
            if nh < NH:
                em.wait("sync", "pe", qfree[1 - qsl])
                qpend[1 - qsl] = em.dma(
                    "sync", qTh[0:64, (1 - qsl) * 512:(2 - qsl) * 512],
                    qsp[nh * HD:(nh + 1) * HD, nqm * 512:nqm * 512 + 512],
                    sem="ld_q")
            negm_cs = [0, 0]

            def stage1(qs):
                so = (qs % 2) * 16
                for ks in range(NKS):
                    idx, ps = poolA.acquire() if ks % 2 == 0 else poolO.acquire()
                    pool_ = poolA if ks % 2 == 0 else poolO
                    op = TE.matmul(ps[:, :],
                                   qTh[0:64, qsl * 512 + qs * 128:
                                       qsl * 512 + (qs + 1) * 128],
                                   ktTh[0:64, ks * 512:(ks + 1) * 512],
                                   start=True, stop=True)
                    pe_c = em.inc(op, "pe")
                    em.wait("vector", "pe", pe_c)
                    op = V.reduce_max(mx[:, so + ks:so + ks + 1], ps[:, :],
                                      axis=AX.X)
                    pool_.release(idx, op, "vector")
                fence(em, "vector")
                op = V.reduce_max(mx[:, so + 12:so + 13], mx[:, so:so + NKS],
                                  axis=AX.X)
                c = em.inc(op, "ve"); em.wait("vector", "ve", c)
                op = V.tensor_scalar(out=mx[:, so + 9:so + 10],
                                     in0=mx[:, so + 12:so + 13],
                                     scalar1=-1.0, scalar2=None, op0=ALU.mult)
                negm_cs[qs % 2] = em.inc(op, "ve")

            def stage2(qs):
                so = (qs % 2) * 16
                em.wait("scalar", "ve", negm_cs[qs % 2])
                for ks in range(NKS):
                    idx, ps = poolA.acquire() if ks % 2 == 0 else poolO.acquire()
                    pool_ = poolA if ks % 2 == 0 else poolO
                    op = TE.matmul(ps[:, :],
                                   qTh[0:64, qsl * 512 + qs * 128:
                                       qsl * 512 + (qs + 1) * 128],
                                   ktTh[0:64, ks * 512:(ks + 1) * 512],
                                   start=True, stop=True)
                    pe_c = em.inc(op, "pe")
                    em.wait("scalar", "pe", pe_c)
                    em.wait("scalar", "ve", free["P"])
                    op = SC.activation(out=Pbuf[:, ks * 512:(ks + 1) * 512],
                                       in_=ps[:, :], func=AF.Exp,
                                       bias=mx[:, so + 9:so + 10],
                                       accum_out=mx[:, so + ks:so + ks + 1])
                    pool_.release(idx, op, "scalar")
                em.wait("vector", "sc", em.cnt["sc"])
                op = V.reduce_sum(mx[:, so + 10:so + 11], mx[:, so:so + NKS],
                                  axis=AX.X)
                c = em.inc(op, "ve"); em.wait("vector", "ve", c)
                op = V.reciprocal(mx[:, so + 11:so + 12], mx[:, so + 10:so + 11])
                inv_c = em.inc(op, "ve")
                em.wait("tensor", "sc", em.cnt["sc"])
                nev = 0
                for kt8 in range(NKT // 4):
                    idx, pt_ = tp.acquire()
                    if kt8 == 0:
                        em.wait("vector", "pe", free["PT"])
                    for k4 in range(4):
                        kt = kt8 * 4 + k4
                        op = TE.transpose(pt_[:, k4 * 128:(k4 + 1) * 128],
                                          Pbuf[:, kt * 128:(kt + 1) * 128],
                                          bufs["eye"][:, :])
                    pe_c = em.inc(op, "pe")
                    em.wait("vector", "pe", pe_c)
                    op = V.tensor_copy(
                        out=PTb[:, kt8 * 512:(kt8 + 1) * 512],
                        in_=pt_[:, 0:512])
                    nev = tp.release(idx, op, "vector")
                free["P"] = em.cnt["ve"]
                em.wait("tensor", "ve", nev)
                idx, ps = poolA.acquire()
                for kt in range(NKT):
                    op = TE.matmul(ps[0:64, 0:128],
                                   vh[:, kt * HD:(kt + 1) * HD],
                                   PTb[:, kt * 128:(kt + 1) * 128],
                                   start=(kt == 0), stop=(kt == NKT - 1))
                pe_c = em.inc(op, "pe")
                free["PT"] = pe_c
                em.wait("tensor", "ve", inv_c)
                idx2, ps2_ = poolO.acquire()
                op = TE.transpose(ps2_[0:1, 0:128], mx[:, so + 11:so + 12],
                                  bufs["eye32"][:, :])
                pe2 = em.inc(op, "pe")
                em.wait("vector", "pe", pe2)
                op = V.tensor_copy(out=orow[0:1, qs * 128:qs * 128 + 128],
                                   in_=ps2_[0:1, 0:128])
                poolO.release(idx2, op, "vector")
                fence(em, "vector")
                em.wait("vector", "pe", pe_c)
                if qs == 0:
                    em.wait("vector", "osp", osp_st)
                op = V.tensor_copy(out=oth[0:64, qs * 128:(qs + 1) * 128],
                                   in_=ps[0:64, 0:128])
                poolA.release(idx, op, "vector")

            stage1(0)
            for qs in range(4):
                if qs < 3:
                    stage1(qs + 1)
                stage2(qs)
            idx3, ps3 = poolO.acquire()
            em.wait("tensor", "ve", em.cnt["ve"])
            op = TE.matmul(ps3[:, :], bufs["ones32"][0:1, :], orow[0:1, :],
                           start=True, stop=True)
            pe3 = em.inc(op, "pe")
            em.wait("vector", "pe", pe3)
            op = V.tensor_copy(out=orep[:, :], in_=ps3[:, :])
            poolO.release(idx3, op, "vector")
            fence(em, "vector")
            op = V.tensor_tensor(out=oth[0:64, :], in0=oth[0:64, :],
                                 in1=orep[0:64, :], op=ALU.mult)
            oc = em.inc(op, "ve")
            em.wait("sync", "ve", oc)
            osp_st = em.dma("sync", osp[hh * HD:(hh + 1) * HD, q0:q0 + 512],
                            oth[0:64, :], sem="osp")
            qfree[qsl] = em.cnt["pe"]
        free["ktTh"] = em.cnt["pe"]
        free["vh"] = em.cnt["pe"]
    em.barrier()

    xo_store = [0, 0]
    xt_used = [0, 0]
    ynS_used = [0, 0]
    for tsl in range(T // 512):
        for t8 in range(8):
            sb = t8 % 2
            em.wait("sync", "ve", ynS_used[sb])
            cdma = em.dma("sync", ynS[sb][:, :],
                          osp[t8 * 128:(t8 + 1) * 128,
                              tsl * 512:(tsl + 1) * 512], sem=f"yn{sb}")
            em.wait("vector", f"yn{sb}", cdma)
            op = V.tensor_copy(out=ynT[:, t8 * 512:(t8 + 1) * 512],
                               in_=ynS[sb][:, :])
            ynS_used[sb] = em.inc(op, "ve")
        em.wait("tensor", "ve", em.cnt["ve"])
        for tt in range(4):
            tg = tsl * 4 + tt
            b = tg % 2
            em.wait("sync", "ve", xt_used[b])
            em.wait("sync", "sc", ln_hist[b])
            c2 = em.dma("sync", bufs["xt"][b][:, :],
                        xin[tg * 128:(tg + 1) * 128, :], sem=f"xa{b}")
            ve_c = 0
            for dn in range(2):
                idx, ps = poolO.acquire()
                for ic in range(8):
                    op = TE.matmul(
                        ps[:, :],
                        ynT[:, ic * 512 + tt * 128: ic * 512 + tt * 128 + 128],
                        wov[:, ic * D + dn * 512: ic * D + dn * 512 + 512],
                        start=(ic == 0), stop=(ic == 7))
                pe_c = em.inc(op, "pe")
                em.wait("vector", "pe", pe_c)
                if dn == 0:
                    em.wait("vector", f"xa{b}", c2)
                    em.wait("vector", f"xs{b}", xo_store[b])
                op = V.tensor_tensor(
                    out=xo[b][:, dn * 512:(dn + 1) * 512], in0=ps[:, :],
                    in1=bufs["xt"][b][:, dn * 512:(dn + 1) * 512], op=ALU.add)
                ve_c = poolO.release(idx, op, "vector")
            xt_used[b] = ve_c
            em.wait("sync", "ve", ve_c)
            xo_store[b] = em.dma("sync", xout[tg * 128:(tg + 1) * 128, :],
                                 xo[b][:, :], sem=f"xs{b}")
        em.wait("sync", "pe", em.cnt["pe"])


def l2_attn_host(inp):
    a = {}
    a["l2a_gamcol"] = np.ascontiguousarray(
        inp["L2_attn_gamma"].reshape(8, 128).T)
    a["l2a_wq"] = inp["L2_attn_Wq"]
    a["l2a_wkv"] = inp["L2_attn_Wkv"]
    a["l2a_wo"] = inp["L2_attn_Wo"]
    wb = np.zeros((64, 16 * 64), np.float32)
    for h in range(16):
        wb[:, h * 64:(h + 1) * 64] = inp["L2_attn_W_bilinear"][h]
    a["l2a_wbilT"] = wb
    ROPE_BASE = 10000.0
    dh = 64
    theta = 1.0 / ROPE_BASE ** (np.arange(0, dh, 2, dtype=np.float64) / dh)
    pos = np.arange(T, dtype=np.float64)
    freqs = pos[:, None] * theta[None, :]
    emb = np.concatenate([freqs, freqs], axis=-1)
    cos1 = np.cos(emb).astype(np.float32)
    sin1 = np.sin(emb).astype(np.float32)
    a["l2a_cos"] = np.tile(cos1, (1, 16))
    a["l2a_sin"] = np.tile(sin1, (1, 16))
    return a


ATTN_SETUP["l2a"] = setup_l2_attn
ATTN_HOST["l2a"] = l2_attn_host


def setup_l0_attn(em, bufs, xin, xout, decls):
    nc = em.nc
    name = "l0a"
    CH = 256
    NCH = T // CH
    NH, HD, DST = 32, 64, 128
    DSSM = 2048
    NBC = 18
    DIP = 4384
    V = em.on("vector")
    SC = em.on("scalar")
    TE = em.on("tensor")
    poolA, poolO = bufs["poolA"], bufs["poolO"]
    tp = bufs["tp_pool"]

    lnxT = em.salloc(f"{name}_lnxT", [128, 8 * CH], BF16)
    winT = em.salloc(f"{name}_winTs", [128, 8 * DIP], BF16)
    woutT = em.salloc(f"{name}_woutTs", [128, 16 * D], BF16)
    raw = [em.salloc(f"{name}_raw{i}", [128, CH], F32)
           for i in range(2)]
    tails = em.salloc(f"{name}_tails", [128, NBC * 4], F32)
    xbcc = em.salloc(f"{name}_xbcc", [128, NBC * CH], BF16)
    xtok = em.salloc(f"{name}_xtok", [128, 2 * DSSM], BF16)
    xdt = em.salloc(f"{name}_xdt", [128, 4 * HD], BF16)
    xdw = em.salloc(f"{name}_xdw", [128, 4 * HD], BF16)
    btok = em.salloc(f"{name}_btok", [128, 2 * DST], BF16)
    ytok = em.salloc(f"{name}_ytok", [128, 2 * DSSM], BF16)
    ygate = em.salloc(f"{name}_yg", [128, 768], F32)
    zsil = em.salloc(f"{name}_zsil", [128, 1024], BF16)
    dtb = em.salloc(f"{name}_dtb", [32, 4 * CH], F32)
    acolw = em.salloc(f"{name}_acolw", [128, 64], F32)
    dtcw = em.salloc(f"{name}_dtcw", [128, 64], F32)
    wdcw = em.salloc(f"{name}_wdcw", [128, 64], F32)
    aflat = em.salloc(f"{name}_aflat", [1, 2 * CH], F32)
    Mbf = em.salloc(f"{name}_Mbf", [128, 4 * CH], BF16)
    cexp = em.salloc(f"{name}_cexp", [128, 2 * CH], BF16)
    maskb = em.salloc(f"{name}_maskbs", [128, 2 * CH], BF16)
    S32 = em.salloc(f"{name}_S32", [128, NH * HD], F32)
    Sbf = em.salloc(f"{name}_Sbf", [128, NH * HD], BF16)
    cdrep = em.salloc(f"{name}_cdrep", [128, 32], F32)
    drep = em.salloc(f"{name}_drep", [128, 32], F32)
    hrow = em.salloc(f"{name}_hrow2", [1, 64], F32)
    ccols = em.salloc(f"{name}_ccolss", [128, NBC * 6], F32)
    hcols = em.salloc(f"{name}_hcolss", [32, 4], F32)
    nwcol = em.salloc(f"{name}_nwcols", [128, 16], F32)
    binz = em.salloc(f"{name}_binzs", [1, DSSM], BF16)
    bout = em.salloc(f"{name}_bouts", [1, D], BF16)
    rmsc = em.salloc(f"{name}_rmsc", [128, 8], F32)
    xo = [em.salloc(f"{name}_xo0", [128, D], F32)] * 2
    asp = nc.dram_tensor(f"{name}_ASP", [32, CH], F32, kind="Internal").ap()
    ynT = xtok

    cstS, cstG = f"cst_{name}", f"cstg_{name}"
    h = decls.dram(em, "l0a_winT", [D, DIP])
    em.dma("gpsimd", winT.ap().rearrange("p (c d) -> p c d", c=8),
           h.rearrange("(c p) d -> p c d", p=128), sem=cstG)
    h = decls.dram(em, "l0a_woutT", [2048, D])
    em.dma("gpsimd", woutT.ap().rearrange("p (c d) -> p c d", c=16),
           h.rearrange("(c p) d -> p c d", p=128), sem=cstG)
    h = decls.dram(em, "l0a_binz", [1, DSSM])
    em.dma("gpsimd", binz[0:1, :], h[0:1, :], sem=cstG)
    h = decls.dram(em, "l0a_bout", [1, D])
    em.dma("gpsimd", bout[0:1, :], h[0:1, :], sem=cstG)
    h = decls.dram(em, "l0a_maskb", [128, 2 * CH])
    em.dma("gpsimd", maskb[:, :], h[:, :], sem=cstG)
    h = decls.dram(em, "l0a_ccols", [128, NBC * 6])
    em.dma("sync", ccols[:, :], h[:, :], sem=cstS)
    h = decls.dram(em, "l0a_hcols", [32, 4])
    em.dma("sync", hcols[0:32, :], h[:, :], sem=cstS)
    h = decls.dram(em, "l0a_nwcol", [128, 16])
    em.dma("sync", nwcol[:, :], h[:, :], sem=cstS)
    em.wait("tensor", cstS, em.cnt[cstS])
    em.wait("tensor", cstG, em.cnt[cstG])
    em.wait("vector", cstS, em.cnt[cstS])
    em.wait("vector", cstG, em.cnt[cstG])
    em.wait("scalar", cstS, em.cnt[cstS])
    em.wait("scalar", cstG, em.cnt[cstG])

    op = SC.activation(out=hcols[0:32, 1:2], in_=hcols[0:32, 1:2], func=AF.Exp)
    c = em.inc(op, "sc")
    em.wait("vector", "sc", c)
    op = V.tensor_scalar(out=hcols[0:32, 1:2], in0=hcols[0:32, 1:2],
                         scalar1=-1.0, scalar2=None, op0=ALU.mult)
    c = em.inc(op, "ve")
    em.wait("tensor", "ve", c)
    idx, ps = poolA.acquire()
    op = TE.transpose(ps[0:1, 0:32], hcols[0:32, 2:3],
                      bufs["eye32"][0:32, 0:32])
    pe_c = em.inc(op, "pe")
    em.wait("vector", "pe", pe_c)
    op = V.tensor_copy(out=hrow[0:1, 32:64], in_=ps[0:1, 0:32])
    poolA.release(idx, op, "vector")
    fence(em, "vector")
    em.wait("tensor", "ve", em.cnt["ve"])
    idx, ps = poolA.acquire()
    op = TE.matmul(ps[:, 0:32], bufs["ones32"][0:1, :], hrow[0:1, 32:64],
                   start=True, stop=True)
    pe_c = em.inc(op, "pe")
    em.wait("vector", "pe", pe_c)
    op = V.tensor_copy(out=drep[:, :], in_=ps[:, 0:32])
    poolA.release(idx, op, "vector")
    for kc in range(16):
        op = V.tensor_scalar(out=woutT[:, kc * D:(kc + 1) * D],
                             in0=woutT[:, kc * D:(kc + 1) * D],
                             scalar1=nwcol[:, kc:kc + 1], scalar2=None,
                             op0=ALU.mult)
    op = V.memset(S32[:, :], 0.0)
    op = V.memset(Sbf[:, :], 0.0)
    op = V.memset(tails[:, :], 0.0)
    st_ready = em.inc(op, "ve")
    em.wait("tensor", "ve", st_ready)
    em.wait("scalar", "ve", st_ready)

    xo_store = [0, 0]
    for ch in range(NCH):
        tok0 = ch * CH
        em.barrier()
        ln_hist = emit_ln_sweep(em, bufs, xin, lnxT, CH, tok0, CH)
        em.wait("tensor", "ve", em.cnt["ve"])
        raw_done = [0, 0]
        for cc in range(NBC):
            rb = cc % 2
            idx, ps = poolA.acquire()
            for kc in range(8):
                op = TE.matmul(
                    ps[:, 0:CH],
                    winT[:, kc * DIP + DSSM + cc * 128:
                         kc * DIP + DSSM + cc * 128 + 128],
                    lnxT[:, kc * CH: kc * CH + CH],
                    start=(kc == 0), stop=(kc == 7))
            pe_c = em.inc(op, "pe")
            em.wait("scalar", "pe", pe_c)
            em.wait("scalar", "ve", raw_done[rb])
            op = SC.activation(out=raw[rb][:, :], in_=ps[:, 0:CH],
                               func=AF.Identity,
                               bias=ccols[:, 5 * NBC + cc: 5 * NBC + cc + 1])
            rdy = poolA.release(idx, op, "scalar")
            em.wait("vector", "sc", rdy)
            sl = xbcc[:, cc * CH:(cc + 1) * CH]
            V.tensor_scalar(out=ygate[:, 0:CH], in0=raw[rb][:, :],
                            scalar1=ccols[:, 4 * cc + 3:4 * cc + 4],
                            scalar2=None, op0=ALU.mult)
            for k in range(3):
                sh = 3 - k
                V.scalar_tensor_tensor(
                    out=ygate[:, sh:CH], in0=raw[rb][:, 0:CH - sh],
                    scalar=ccols[:, 4 * cc + k:4 * cc + k + 1],
                    in1=ygate[:, sh:CH], op0=ALU.mult, op1=ALU.add)
                V.scalar_tensor_tensor(
                    out=ygate[:, 0:sh],
                    in0=tails[:, cc * 4 + 4 - sh: cc * 4 + 4],
                    scalar=ccols[:, 4 * cc + k:4 * cc + k + 1],
                    in1=ygate[:, 0:sh], op0=ALU.mult, op1=ALU.add)
            fence(em, "vector")
            op = V.tensor_copy(out=tails[:, cc * 4 + 1: cc * 4 + 4],
                               in_=raw[rb][:, CH - 3:CH])
            raw_done[rb] = em.inc(op, "ve")
            em.wait("scalar", "ve", raw_done[rb])
            op = SC.activation(out=sl, in_=ygate[:, 0:CH], func=AF.Silu,
                               bias=ccols[:, 4 * NBC + cc: 4 * NBC + cc + 1])
            em.inc(op, "sc")
        idx, psd = poolA.acquire()
        for kc in range(8):
            op = TE.matmul(
                psd[0:32, 0:CH],
                winT[:, kc * DIP + 4352: kc * DIP + 4352 + 32],
                lnxT[:, kc * CH: kc * CH + CH],
                start=(kc == 0), stop=(kc == 7))
        pe_c = em.inc(op, "pe")
        em.wait("vector", "pe", pe_c)
        op = V.tensor_scalar(out=dtb[0:32, 0:CH], in0=psd[0:32, 0:CH],
                             scalar1=hcols[0:32, 0:1], scalar2=None,
                             op0=ALU.add)
        poolA.release(idx, op, "vector")
        fence(em, "vector")
        op = V.tensor_scalar(out=dtb[0:32, CH:2 * CH], in0=dtb[0:32, 0:CH],
                             scalar1=0.0, scalar2=None, op0=ALU.max)
        c = em.inc(op, "ve")
        em.wait("scalar", "ve", c)
        op = SC.activation(out=dtb[0:32, 2 * CH:3 * CH], in_=dtb[0:32, 0:CH],
                           func=AF.Abs)
        c = em.inc(op, "sc")
        em.wait("scalar", "sc", c)
        op = SC.activation(out=dtb[0:32, 2 * CH:3 * CH],
                           in_=dtb[0:32, 2 * CH:3 * CH], func=AF.Exp,
                           scale=-1.0)
        c = em.inc(op, "sc")
        em.wait("scalar", "sc", c)
        op = SC.activation(out=dtb[0:32, 2 * CH:3 * CH],
                           in_=dtb[0:32, 2 * CH:3 * CH], func=AF.Ln,
                           bias=bufs["konst"][0:32, 2:3])
        c = em.inc(op, "sc")
        em.wait("vector", "sc", c)
        op = V.tensor_tensor(out=dtb[0:32, CH:2 * CH],
                             in0=dtb[0:32, CH:2 * CH],
                             in1=dtb[0:32, 2 * CH:3 * CH], op=ALU.add)
        fence(em, "vector")
        op = V.tensor_scalar(out=dtb[0:32, 0:CH], in0=dtb[0:32, CH:2 * CH],
                             scalar1=hcols[0:32, 1:2], scalar2=None,
                             op0=ALU.mult)
        fence(em, "vector")
        cur, nxt = 0, 2 * CH
        for st in range(8):
            sh = 1 << st
            V.tensor_copy(out=dtb[0:32, nxt:nxt + sh],
                          in_=dtb[0:32, cur:cur + sh])
            op = V.tensor_tensor(out=dtb[0:32, nxt + sh:nxt + CH],
                                 in0=dtb[0:32, cur + sh:cur + CH],
                                 in1=dtb[0:32, cur:cur + CH - sh], op=ALU.add)
            cur, nxt = nxt, cur
        fence(em, "vector")
        op = V.tensor_copy(out=dtb[0:32, 2 * CH:3 * CH],
                           in_=dtb[0:32, cur:cur + CH])
        fence(em, "vector")
        op = V.tensor_copy(out=hcols[0:32, 3:4],
                           in_=dtb[0:32, 3 * CH - 1:3 * CH])
        c = em.inc(op, "ve")
        em.wait("scalar", "ve", c)
        op = SC.activation(out=dtb[0:32, 3 * CH:4 * CH],
                           in_=dtb[0:32, 2 * CH:3 * CH], func=AF.Exp,
                           scale=-1.0, bias=hcols[0:32, 3:4])
        c = em.inc(op, "sc")
        em.wait("vector", "sc", c)
        op = V.tensor_tensor(out=dtb[0:32, 3 * CH:4 * CH],
                             in0=dtb[0:32, 3 * CH:4 * CH],
                             in1=dtb[0:32, CH:2 * CH], op=ALU.mult)
        adone = em.inc(op, "ve")
        em.wait("tensor", "ve", adone)
        idx, ps = poolA.acquire()
        op = TE.transpose(ps[0:1, 0:32], hcols[0:32, 3:4],
                          bufs["eye32"][0:32, 0:32])
        pe_c = em.inc(op, "pe")
        em.wait("scalar", "pe", pe_c)
        op = SC.activation(out=hrow[0:1, 0:32], in_=ps[0:1, 0:32], func=AF.Exp)
        cdr = poolA.release(idx, op, "scalar")
        em.wait("tensor", "sc", cdr)
        idx, ps = poolA.acquire()
        op = TE.matmul(ps[:, 0:32], bufs["ones32"][0:1, :], hrow[0:1, 0:32],
                       start=True, stop=True)
        pe_c = em.inc(op, "pe")
        em.wait("vector", "pe", pe_c)
        op = V.tensor_copy(out=cdrep[:, :], in_=ps[:, 0:32])
        poolA.release(idx, op, "vector")
        for src0, dst in ((2 * CH, acolw), (CH, dtcw), (3 * CH, wdcw)):
            for jt in range(2):
                idx, ps = poolA.acquire()
                op = TE.transpose(
                    ps[0:128, 0:32],
                    dtb[0:32, src0 + jt * 128: src0 + jt * 128 + 128],
                    bufs["eye32"][0:32, 0:32])
                pe_c = em.inc(op, "pe")
                em.wait("vector", "pe", pe_c)
                op = V.tensor_copy(out=dst[:, jt * 32:(jt + 1) * 32],
                                   in_=ps[0:128, 0:32])
                poolA.release(idx, op, "vector")
        fence(em, "vector")
        em.wait("sync", "ve", adone)
        ca1 = em.dma("sync", asp[:, :], dtb[0:32, 2 * CH:3 * CH], sem="asw")
        em.wait("tensor", "sc", em.cnt["sc"])
        last = 0
        for cc in range(16):
            for jt in range(2):
                idx, pt_ = tp.acquire()
                op = TE.transpose(pt_[:, 0:128],
                                  xbcc[:, cc * CH + jt * 128:
                                       cc * CH + jt * 128 + 128],
                                  bufs["eye"][:, :])
                pe_c = em.inc(op, "pe")
                em.wait("vector", "pe", pe_c)
                op = V.tensor_copy(
                    out=xtok[:, jt * DSSM + cc * 128: jt * DSSM + cc * 128 + 128],
                    in_=pt_[:, 0:128])
                last = tp.release(idx, op, "vector")
        for jt in range(2):
            idx, pt_ = tp.acquire()
            op = TE.transpose(pt_[:, 0:128],
                              xbcc[:, 16 * CH + jt * 128: 16 * CH + jt * 128 + 128],
                              bufs["eye"][:, :])
            pe_c = em.inc(op, "pe")
            em.wait("vector", "pe", pe_c)
            op = V.tensor_copy(out=btok[:, jt * DST:jt * DST + 128],
                               in_=pt_[:, 0:128])
            last = tp.release(idx, op, "vector")
        em.wait("tensor", "ve", last)
        expa_free = 0
        slot_tok = [None, None]

        def stageA(hh):
            nonlocal ca1, expa_free
            s = hh % 2
            if hh % 2 == 0:
                em.wait("sync", "asw", ca1)
                em.wait("sync", "pe", em.cnt["pe"])
                ca2 = em.dma("sync", aflat[0:1, :],
                             asp.rearrange("h t -> (h t)")[None,
                             (hh // 2) * 2 * CH:(hh // 2 + 1) * 2 * CH],
                             sem="asr")
                em.wait("tensor", "asr", ca2)
            idxR, psR = poolO.acquire()
            op = TE.matmul(psR[:, 0:CH], bufs["ones32"][0:1, :],
                           aflat[0:1, (hh % 2) * CH:(hh % 2 + 1) * CH],
                           start=True, stop=True)
            peR = em.inc(op, "pe")
            em.wait("scalar", "pe", peR)
            em.wait("scalar", "ve", expa_free)
            op = SC.activation(out=ygate[:, 0:CH], in_=psR[:, 0:CH],
                               func=AF.Exp)
            expa_c = em.inc(op, "sc")
            em.wait("vector", "sc", expa_c)
            op = V.tensor_tensor(out=cexp[:, s * CH:(s + 1) * CH],
                                 in0=xbcc[:, 17 * CH:18 * CH],
                                 in1=ygate[:, 0:CH], op=ALU.mult)
            cexp_c = em.inc(op, "ve")
            expa_free = cexp_c
            Ms = 0
            for jt in range(2):
                op = V.tensor_scalar(
                    out=ygate[:, CH:2 * CH], in0=psR[:, 0:CH],
                    scalar1=acolw[:, jt * 32 + hh:jt * 32 + hh + 1],
                    scalar2=None, op0=ALU.subtract)
                c = em.inc(op, "ve")
                em.wait("vector", "ve", c)
                op = V.tensor_tensor(out=ygate[:, CH:2 * CH],
                                     in0=ygate[:, CH:2 * CH],
                                     in1=maskb[:, jt * CH:(jt + 1) * CH],
                                     op=ALU.min)
                c = em.inc(op, "ve")
                em.wait("scalar", "ve", c)
                op = SC.activation(out=ygate[:, 2 * CH:3 * CH],
                                   in_=ygate[:, CH:2 * CH], func=AF.Exp)
                c = em.inc(op, "sc")
                idxG, psG = poolA.acquire()
                op = TE.matmul(psG[:, 0:CH],
                               xbcc[:, 16 * CH + jt * 128: 16 * CH + jt * 128 + 128],
                               xbcc[:, 17 * CH:18 * CH],
                               start=True, stop=True)
                peG = em.inc(op, "pe")
                em.wait("vector", "pe", peG)
                em.wait("vector", "sc", c)
                op = V.tensor_tensor(
                    out=Mbf[:, s * 2 * CH + jt * CH: s * 2 * CH + (jt + 1) * CH],
                    in0=ygate[:, 2 * CH:3 * CH],
                    in1=psG[:, 0:CH], op=ALU.mult)
                Ms = poolA.release(idxG, op, "vector")
            for jt in range(2):
                V.tensor_scalar(out=xdt[:, s * 2 * HD + jt * HD:
                                        s * 2 * HD + (jt + 1) * HD],
                                in0=xtok[:, jt * DSSM + hh * HD:
                                         jt * DSSM + hh * HD + HD],
                                scalar1=dtcw[:, jt * 32 + hh:jt * 32 + hh + 1],
                                scalar2=None, op0=ALU.mult)
                op = V.tensor_scalar(out=xdw[:, s * 2 * HD + jt * HD:
                                             s * 2 * HD + (jt + 1) * HD],
                                     in0=xtok[:, jt * DSSM + hh * HD:
                                              jt * DSSM + hh * HD + HD],
                                     scalar1=wdcw[:, jt * 32 + hh:jt * 32 + hh + 1],
                                     scalar2=None, op0=ALU.mult)
            xd_c = poolO.release(idxR, op, "vector")
            slot_tok[s] = max(xd_c, Ms, cexp_c)

        def stageB(hh):
            s = hh % 2
            em.wait("tensor", "ve", slot_tok[s])
            for ii in range(2):
                idxY, psY = poolO.acquire()
                for jt in range(2):
                    op = TE.matmul(psY[:, 0:HD],
                                   Mbf[:, s * 2 * CH + jt * CH + ii * 128:
                                       s * 2 * CH + jt * CH + ii * 128 + 128],
                                   xdt[:, s * 2 * HD + jt * HD:
                                       s * 2 * HD + (jt + 1) * HD],
                                   start=(jt == 0), stop=False)
                op = TE.matmul(psY[:, 0:HD],
                               cexp[:, s * CH + ii * 128: s * CH + ii * 128 + 128],
                               Sbf[:, hh * HD:(hh + 1) * HD],
                               start=False, stop=True)
                peY = em.inc(op, "pe")
                em.wait("vector", "pe", peY)
                op = V.scalar_tensor_tensor(
                    out=ytok[:, ii * DSSM + hh * HD: ii * DSSM + hh * HD + HD],
                    in0=xtok[:, ii * DSSM + hh * HD: ii * DSSM + hh * HD + HD],
                    scalar=drep[:, hh:hh + 1], in1=psY[:, 0:HD],
                    op0=ALU.mult, op1=ALU.add)
                poolO.release(idxY, op, "vector")
            idxS, psS = poolA.acquire()
            for jt in range(2):
                op = TE.matmul(psS[:, 0:HD],
                               btok[:, jt * DST:jt * DST + 128],
                               xdw[:, s * 2 * HD + jt * HD:
                                   s * 2 * HD + (jt + 1) * HD],
                               start=(jt == 0), stop=(jt == 1))
            peS = em.inc(op, "pe")
            em.wait("vector", "pe", peS)
            op = V.scalar_tensor_tensor(
                out=S32[:, hh * HD:(hh + 1) * HD],
                in0=S32[:, hh * HD:(hh + 1) * HD],
                scalar=cdrep[:, hh:hh + 1], in1=psS[:, 0:HD],
                op0=ALU.mult, op1=ALU.add)
            poolA.release(idxS, op, "vector")

        for hh in range(NH):
            stageA(hh)
            if hh > 0:
                stageB(hh - 1)
        stageB(NH - 1)
        fence(em, "vector")
        op = V.tensor_copy(out=Sbf[:, :], in_=S32[:, :])
        c = em.inc(op, "ve")
        em.wait("tensor", "ve", c)
        for tt in range(2):
            for half in range(2):
                for sub in range(2):
                    nn = half * 2 + sub
                    idx, ps = poolO.acquire()
                    for kc in range(8):
                        op = TE.matmul(
                            ps[:, :],
                            lnxT[:, kc * CH + tt * 128: kc * CH + tt * 128 + 128],
                            winT[:, kc * DIP + nn * 512: kc * DIP + nn * 512 + 512],
                            start=(kc == 0), stop=False)
                    op = TE.matmul(ps[:, :], bufs["ones_row"][0:1, :],
                                   binz[0:1, nn * 512:(nn + 1) * 512],
                                   start=False, stop=True)
                    pe_c = em.inc(op, "pe")
                    em.wait("scalar", "pe", pe_c)
                    em.wait("scalar", "ve", em.cnt["ve"])
                    op = SC.activation(out=zsil[:, sub * 512:(sub + 1) * 512],
                                       in_=ps[:, :], func=AF.Silu)
                    poolO.release(idx, op, "scalar")
                fence(em, "scalar")
                c = em.cnt["sc"]
                em.wait("vector", "sc", c)
                hsl = slice(tt * DSSM + half * 1024,
                            tt * DSSM + half * 1024 + 1024)
                op = V.tensor_tensor(out=ytok[:, hsl], in0=ytok[:, hsl],
                                     in1=zsil[:, :], op=ALU.mult)
                c = em.inc(op, "ve")
                em.wait("scalar", "ve", c)
                op = SC.activation(out=bufs["scr"][:, :], in_=ytok[:, hsl],
                                   func=AF.Square,
                                   accum_out=rmsc[:, half:half + 1])
                em.inc(op, "sc")
            em.wait("vector", "sc", em.cnt["sc"])
            op = V.tensor_tensor(out=rmsc[:, 2:3], in0=rmsc[:, 0:1],
                                 in1=rmsc[:, 1:2], op=ALU.add)
            fence(em, "vector")
            op = V.tensor_scalar(out=rmsc[:, 2:3], in0=rmsc[:, 2:3],
                                 scalar1=1.0 / DSSM, scalar2=None,
                                 op0=ALU.mult)
            c = em.inc(op, "ve")
            em.wait("scalar", "ve", c)
            op = SC.activation(out=rmsc[:, 2:3], in_=rmsc[:, 2:3],
                               func=AF.Sqrt, bias=bufs["konst"][:, 1:2])
            c = em.inc(op, "sc")
            em.wait("vector", "sc", c)
            op = V.reciprocal(rmsc[:, 3:4], rmsc[:, 2:3])
            c = em.inc(op, "ve")
            em.wait("vector", "ve", c)
            op = V.tensor_scalar(out=ytok[:, tt * DSSM:(tt + 1) * DSSM],
                                 in0=ytok[:, tt * DSSM:(tt + 1) * DSSM],
                                 scalar1=rmsc[:, 3:4], scalar2=None,
                                 op0=ALU.mult)
            c = em.inc(op, "ve")
            em.wait("tensor", "ve", c)
            last = 0
            for kc in range(16):
                idx, pt_ = tp.acquire()
                op = TE.transpose(pt_[:, 0:128],
                                  ytok[:, tt * DSSM + kc * 128:
                                       tt * DSSM + kc * 128 + 128],
                                  bufs["eye"][:, :])
                pe_c = em.inc(op, "pe")
                em.wait("vector", "pe", pe_c)
                op = V.tensor_copy(
                    out=ynT[:, kc * CH + tt * 128: kc * CH + tt * 128 + 128],
                    in_=pt_[:, 0:128])
                last = tp.release(idx, op, "vector")
            em.wait("tensor", "ve", last)
            tg = (tok0 // 128) + tt
            b = tg % 2
            em.wait("sync", "ve", em.cnt["ve"])
            em.wait("sync", "sc", ln_hist[b])
            cx = em.dma("sync", bufs["xt"][b][:, :],
                        xin[tg * 128:(tg + 1) * 128, :], sem=f"xa{b}")
            ve_c = 0
            for dn in range(2):
                idx, ps = poolO.acquire()
                for kc in range(16):
                    op = TE.matmul(
                        ps[:, :],
                        ynT[:, kc * CH + tt * 128: kc * CH + tt * 128 + 128],
                        woutT[:, kc * D + dn * 512: kc * D + dn * 512 + 512],
                        start=(kc == 0), stop=False)
                op = TE.matmul(ps[:, :], bufs["ones_row"][0:1, :],
                               bout[0:1, dn * 512:(dn + 1) * 512],
                               start=False, stop=True)
                pe_c = em.inc(op, "pe")
                em.wait("vector", "pe", pe_c)
                if dn == 0:
                    em.wait("vector", f"xa{b}", cx)
                    em.wait("vector", f"xs{b}", xo_store[b])
                op = V.tensor_tensor(
                    out=xo[b][:, dn * 512:(dn + 1) * 512], in0=ps[:, :],
                    in1=bufs["xt"][b][:, dn * 512:(dn + 1) * 512], op=ALU.add)
                ve_c = poolO.release(idx, op, "vector")
            em.wait("sync", "ve", ve_c)
            xo_store[b] = em.dma("sync", xout[tg * 128:(tg + 1) * 128, :],
                                 xo[b][:, :], sem=f"xs{b}")


def l0_attn_host(inp):
    a = {}
    NBC = 18
    a["l0a_winT"] = np.ascontiguousarray(inp["L0_attn_W_in"].T)
    a["l0a_woutT"] = np.ascontiguousarray(inp["L0_attn_W_out"].T)
    a["l0a_binz"] = inp["L0_attn_b_in"][0:2048].reshape(1, 2048)
    a["l0a_bout"] = inp["L0_attn_b_out"].reshape(1, 1024)
    ccols = np.zeros((128, NBC * 6), np.float32)
    cw = inp["L0_attn_conv_w"].reshape(NBC, 128, 4)
    cb = inp["L0_attn_conv_b"].reshape(NBC, 128)
    bx = inp["L0_attn_b_in"][2048:4352].reshape(NBC, 128)
    for cc in range(NBC):
        for k in range(4):
            ccols[:, 4 * cc + k] = cw[cc, :, k]
        ccols[:, 4 * NBC + cc] = cb[cc]
        ccols[:, 5 * NBC + cc] = bx[cc]
    a["l0a_ccols"] = ccols
    hcols = np.zeros((32, 4), np.float32)
    hcols[:, 0] = inp["L0_attn_b_in"][4352:4384] + inp["L0_attn_dt_bias"]
    hcols[:, 1] = inp["L0_attn_A_log"]
    hcols[:, 2] = inp["L0_attn_D"]
    a["l0a_hcols"] = hcols
    a["l0a_nwcol"] = np.ascontiguousarray(
        inp["L0_attn_norm_w"].reshape(16, 128).T)
    CH = 256
    mask = np.zeros((128, 2 * CH), np.float32)
    for jt in range(2):
        jj = (jt * 128 + np.arange(128))[:, None]
        ii = np.arange(CH)[None, :]
        mask[:, jt * CH:(jt + 1) * CH] = np.where(ii >= jj, 0.0, -1e30)
    a["l0a_maskb"] = mask
    return a


ATTN_SETUP["l0a"] = setup_l0_attn
ATTN_HOST["l0a"] = l0_attn_host



ALL_BLOCKS = ["l0a", "l0f", "l1a", "l1f", "l2a", "l2f", "l3a", "l3f"]


def kernel(**inputs):
    blocks = ALL_BLOCKS
    nc, es, decls = build_program(blocks)
    arrays = host_arrays_for(blocks, inputs)
    u = np.asarray(inputs["L0_attn_u"], np.float32)
    in_maps = []
    for core in range(NCORE):
        m = dict(arrays)
        m["XIN"] = np.ascontiguousarray(u[core])
        in_maps.append(m)
    res = run_bass_kernel_spmd(nc, in_maps, core_ids=list(range(NCORE)))
    return np.stack([np.asarray(r["XOUT"], np.float32)
                     for r in res.results], axis=0)
